# version 12
# baseline (speedup 1.0000x reference)
import numpy as np
import ml_dtypes
import concourse.bass as bass
import concourse.mybir as mybir
from concourse.bass_utils import run_bass_kernel_spmd
from concourse.expressions import smax, smin
from contextlib import ExitStack

F32 = mybir.dt.float32
BF16 = mybir.dt.bfloat16
AF = mybir.ActivationFunctionType
ALU = mybir.AluOpType
AXL = mybir.AxisListType
BF = ml_dtypes.bfloat16

NCORES = 8
B, S, D, NH, HD = 2, 4096, 2048, 16, 128
TC = 1024
DC = D // 128
HC = 64
SM_SCALE = 1.0 / float(np.sqrt(HD))
NEG = -1e9

RGB = [[0, 1, 2, 3], [4, 5, 6, 7]]

IDN_BF = np.eye(128, dtype=BF)
IDN_F32 = np.eye(128, dtype=np.float32)
IOTA64 = np.broadcast_to(np.arange(64, dtype=np.float32), (128, 64)).astype(BF).copy()

_ENG = {"pe": "tensor", "dve": "vector", "act": "scalar", "pool": "gpsimd",
        "sp": "sync"}


class P:
    def __init__(self, name="prog"):
        self.nc = bass.Bass("TRN2", target_bir_lowering=False)
        self.ctx = ExitStack()
        self.scopes = []
        self.S = {}
        self.c = {}
        self.block = None

    def start(self):
        e = self.ctx.enter_context
        for s in ("pe", "dve", "act", "pool", "ds", "dg"):
            self.S[s] = e(self.nc.semaphore(s))
            self.c[s] = 0
        self.block = e(self.nc.Block())

    def sem(self, name):
        if name not in self.S:
            self.S[name] = self.ctx.enter_context(self.nc.semaphore(name))
            self.c[name] = 0
        return self.S[name]

    def finish(self):
        while self.scopes:
            self.pop()
        self.ctx.close()
        return self.nc

    def push(self):
        self.scopes.append(ExitStack())

    def pop(self):
        self.scopes.pop().close()

    def _enter(self, cm):
        tgt = self.scopes[-1] if self.scopes else self.ctx
        return tgt.enter_context(cm)

    def sb(self, name, shape, dtype):
        return self._enter(self.nc.sbuf_tensor(name, shape, dtype))

    def ps(self, name, shape, dtype=F32):
        return self._enter(self.nc.psum_tensor(name, shape, dtype))

    def dram(self, name, shape, dtype=F32, kind="Internal"):
        return self.nc.dram_tensor(name, shape, dtype, kind=kind)

    def const(self, arr, name):
        return self.nc.inline_tensor(np.ascontiguousarray(arr), name)

    def sec(self, eng, fn):
        getattr(self.block, _ENG[eng])(fn)

    def inc(self, key, inst, n=1):
        inst.then_inc(self.S[key], n)
        self.c[key] += n
        return self.c[key]

    def dma(self, key, inst):
        return self.inc(key, inst, 16)

    def w(self, eng_obj, key, count=None):
        if count is None:
            count = self.c[key]
        if count > 0:
            eng_obj.wait_ge(self.S[key], count)

    def sw(self, eng_obj, key):
        self.w(eng_obj, key)

    def snap(self, *keys):
        if not keys:
            return dict(self.c)
        return {k: self.c[k] for k in keys}

    def dyn(self, static=False):
        if static:
            return {"o128l": 0, "o128r": 128, "oTl": 0, "oTl960": 960,
                    "oTr": TC, "oTo": 0}
        pid = self.nc.partition_id(engines=[mybir.EngineType.Pool])
        q4 = pid % 4
        lft = smax(q4 - 1, 0)
        rgt = smin(q4 + 1, 3)
        return {
            "o128l": lft * 128, "o128r": rgt * 128,
            "oTl": lft * TC, "oTl960": lft * TC + 960, "oTr": rgt * TC,
            "oTo": q4 * TC,
        }


def emit_cc(p, g, ain, aout, semname, kind="AllGather", alu=ALU.bypass,
            rg=None, simcc=False):
    p.sem(semname)
    if simcc:
        rr = ain.shape[0]
        for r in range(aout.shape[0] // rr):
            p.dma(semname, g.dma_start(out=aout[r * rr:(r + 1) * rr, :], in_=ain[:, :]))
    else:
        if rg is None:
            rg = [list(range(NCORES))]
        g.collective_compute(
            kind, alu, replica_groups=rg,
            ins=[ain.ap().opt()], outs=[aout.ap().opt()]).then_inc(p.S[semname], 1)
        p.c[semname] += 1


def phase_fence(p):
    sn = p.snap()
    dma_sems = [k for k in p.S
                if k not in ("pe", "dve", "act", "pool") and not k.startswith("cc")]

    def body(sy, sn=sn, dma_sems=dma_sems):
        for k in ("pe", "dve", "act", "pool"):
            p.w(sy, k, sn[k])
        for k in dma_sems:
            p.w(sy, k, sn[k])
    p.sec("sp", body)

    def body(g, sn=sn, dma_sems=dma_sems):
        for k in ("pe", "dve", "act"):
            p.w(g, k, sn[k])
        for k in dma_sems:
            p.w(g, k, sn[k])
    p.sec("pool", body)

    def body(v, sn=sn, dma_sems=dma_sems):
        for k in dma_sems:
            p.w(v, k, sn[k])
    p.sec("dve", body)

    def body(a, sn=sn, dma_sems=dma_sems):
        for k in dma_sems:
            p.w(a, k, sn[k])
    p.sec("act", body)



def alloc_ln_scr(p, pre):
    return {
        "s1": p.sb(pre + "lr_s1", [128, 1], F32),
        "s2": p.sb(pre + "lr_s2", [128, 1], F32),
        "negm": p.sb(pre + "lr_negm", [128, 1], F32),
        "m2": p.sb(pre + "lr_m2", [128, 1], F32),
        "var": p.sb(pre + "lr_var", [128, 1], F32),
        "rstd": p.sb(pre + "lr_rstd", [128, 1], F32),
        "sqd": p.sb(pre + "lr_sqd", [128, D], BF16),
        "xt": p.sb(pre + "lr_xt", [128, D], F32),
        "xo": p.sb(pre + "lr_xo", [128, D], F32),
    }


def emit_ln_residual(p, scr, src_sb, x, xo, tt, src_ready):
    lpar = "dxa%d" % (tt % 2)

    def body(sy, tt=tt, lpar=lpar):
        p.w(sy, "dve")
        p.w(sy, "dst")
        p.w(sy, lpar)
        p.dma(lpar, sy.dma_start(out=scr["xt"][:, :], in_=x[tt * 128:(tt + 1) * 128, :]))
    p.sec("sp", body)
    dl = p.snap()

    def body(v, dl=dl, src_ready=src_ready):
        p.w(v, "dve", src_ready["dve"])
        p.inc("dve", v.reduce_sum(scr["s1"][:, :], src_sb[:, :], axis=AXL.X))
    p.sec("dve", body)

    def body(a, src_ready=src_ready):
        p.w(a, "dve", src_ready["dve"])
        p.w(a, "act", src_ready["act"])
        p.inc("act", a.activation(scr["sqd"][:, :], src_sb[:, :], AF.Square,
                                  accum_out=scr["s2"][:, :]))
    p.sec("act", body)
    stat = p.snap()

    def body(v, stat=stat, dl=dl, lpar=lpar):
        p.w(v, "act", stat["act"])
        p.w(v, lpar, dl[lpar])
        p.w(v, "dst", dl["dst"])
        p.w(v, "dve")
        p.inc("dve", v.tensor_scalar(scr["negm"][:, :], scr["s1"][:, :], -1.0 / D, None, op0=ALU.mult))
        p.sw(v, "dve")
        p.inc("dve", v.tensor_mul(scr["m2"][:, :], scr["negm"][:, :], scr["negm"][:, :]))
        p.inc("dve", v.tensor_scalar(scr["var"][:, :], scr["s2"][:, :], 1.0 / D, 1e-5,
                                     op0=ALU.mult, op1=ALU.add))
        p.sw(v, "dve")
        p.inc("dve", v.tensor_tensor(scr["var"][:, :], scr["var"][:, :], scr["m2"][:, :], ALU.subtract))
    p.sec("dve", body)

    def body(a):
        p.w(a, "dve")
        p.inc("act", a.activation(scr["var"][:, :], scr["var"][:, :], AF.Sqrt))
    p.sec("act", body)

    def body(v):
        p.w(v, "act")
        p.inc("dve", v.reciprocal(scr["rstd"][:, :], scr["var"][:, :]))
        p.sw(v, "dve")
        p.inc("dve", v.tensor_scalar(scr["xo"][:, :], src_sb[:, :], scr["negm"][:, :],
                                     scr["rstd"][:, :], op0=ALU.add, op1=ALU.mult))
        p.sw(v, "dve")
        p.inc("dve", v.tensor_tensor(scr["xo"][:, :], scr["xo"][:, :], scr["xt"][:, :], ALU.add))
    p.sec("dve", body)
    done = p.snap()

    def body(sy, tt=tt, done=done):
        p.w(sy, "dve", done["dve"])
        p.dma("dst", sy.dma_start(out=xo[tt * 128:(tt + 1) * 128, :], in_=scr["xo"][:, :]))
    p.sec("sp", body)



def emit_preln(p, scr2, x, sb_xn, sb_xnT, sb_idn, ps_t, ntile, per_t):
    for tt in range(ntile):
        lpar = "dxa%d" % (tt % 2)

        def body(sy, tt=tt, lpar=lpar):
            p.w(sy, "dve")
            p.w(sy, "dst")
            p.w(sy, lpar)
            p.dma(lpar, sy.dma_start(out=scr2["xt"][:, :], in_=x[tt * 128:(tt + 1) * 128, :]))
        p.sec("sp", body)
        dl = p.snap()

        def body(v, dl=dl, lpar=lpar):
            p.w(v, lpar, dl[lpar])
            p.inc("dve", v.reduce_sum(scr2["s1"][:, :], scr2["xt"][:, :], axis=AXL.X))
        p.sec("dve", body)

        def body(a, dl=dl, lpar=lpar):
            p.w(a, lpar, dl[lpar])
            p.w(a, "act")
            p.inc("act", a.activation(scr2["sqd"][:, :], scr2["xt"][:, :], AF.Square,
                                      accum_out=scr2["s2"][:, :]))
        p.sec("act", body)
        stat = p.snap()

        def body(v, stat=stat):
            p.w(v, "act", stat["act"])
            p.w(v, "dve")
            p.inc("dve", v.tensor_scalar(scr2["negm"][:, :], scr2["s1"][:, :], -1.0 / D, None, op0=ALU.mult))
            p.sw(v, "dve")
            p.inc("dve", v.tensor_mul(scr2["m2"][:, :], scr2["negm"][:, :], scr2["negm"][:, :]))
            p.inc("dve", v.tensor_scalar(scr2["var"][:, :], scr2["s2"][:, :], 1.0 / D, 1e-5,
                                         op0=ALU.mult, op1=ALU.add))
            p.sw(v, "dve")
            p.inc("dve", v.tensor_tensor(scr2["var"][:, :], scr2["var"][:, :], scr2["m2"][:, :], ALU.subtract))
        p.sec("dve", body)

        def body(a):
            p.w(a, "dve")
            p.inc("act", a.activation(scr2["var"][:, :], scr2["var"][:, :], AF.Sqrt))
        p.sec("act", body)

        def body(v, tt=tt):
            p.w(v, "act")
            p.w(v, "pe")
            p.inc("dve", v.reciprocal(scr2["rstd"][:, :], scr2["var"][:, :]))
            p.sw(v, "dve")
            p.inc("dve", v.tensor_scalar(sb_xn[:, tt * D:(tt + 1) * D], scr2["xt"][:, :],
                                         scr2["negm"][:, :], scr2["rstd"][:, :],
                                         op0=ALU.add, op1=ALU.mult))
        p.sec("dve", body)

    for tt in range(ntile):
        for dc4 in range(0, DC, 4):
            def body(t, tt=tt, dc4=dc4):
                p.w(t, "dve")
                for jj in range(4):
                    dc = dc4 + jj
                    mmi = t.transpose(ps_t[:, jj * 128:(jj + 1) * 128],
                                      sb_xn[:, tt * D + dc * 128: tt * D + (dc + 1) * 128],
                                      sb_idn[:, :])
                p.inc("pe", mmi)
            p.sec("pe", body)
            pl = p.snap()

            def body(v, tt=tt, dc4=dc4, pl=pl):
                p.w(v, "pe", pl["pe"])
                dst = sb_xnT[:, :].rearrange("p (c t) -> p c t", t=per_t)[
                    :, dc4:dc4 + 4, tt * 128:(tt + 1) * 128]
                p.inc("dve", v.tensor_copy(dst, ps_t[:, :].rearrange(
                    "p (c t) -> p c t", t=128)[:, 0:4, :]))
            p.sec("dve", body)
    return p.snap()



def emit_prep(p, io):
    nc = p.nc
    sb_idn = io["sb_idn"]
    sb_onerow = io["sb_onerow"]
    sb_onef = io["sb_onef"]
    p.push()
    sb_in = [p.sb("pp_in0", [128, 8192], F32), p.sb("pp_in1", [128, 8192], F32)]
    sb_q = p.sb("pp_q", [128, 8192], BF16)
    sb_s2 = p.sb("pp_s2", [128, 8192], BF16)
    sb_abs = p.sb("pp_abs", [128, 1], F32)
    sb_acc = p.sb("pp_acc", [128, 2], F32)
    sb_tot = p.sb("pp_tot", [1, 2], F32)
    sb_th = p.sb("pp_th", [128, 2], F32)
    sb_thn = p.sb("pp_thn", [128, 2], F32)
    sb_stage = p.sb("pp_stage", [128, 16384], BF16)
    ps_t = p.ps("pp_pst", [128, 512], BF16)
    ps_f = p.ps("pp_psf", [128, 2], F32)

    def body(g):
        p.inc("pool", g.memset(sb_acc[:, :], 0.0))
    p.sec("pool", body)

    ldi = [0]
    in_act = {}
    for mi, tname in enumerate(["w1s", "w2s"]):
        t = io[tname]
        rows, width = (1024, 2048) if mi == 0 else (256, 8192)
        for i in range(rows // 128):
            idx = ldi[0]
            buf = idx % 2
            lpar = "dxa%d" % buf
            ldi[0] += 1

            def body(sy, t=t, i=i, width=width, lpar=lpar, g_=in_act.get(idx - 2, 0)):
                p.w(sy, "act", g_)
                p.w(sy, lpar)
                p.dma(lpar, sy.dma_start(out=sb_in[buf][:, :width],
                                         in_=t[i * 128:(i + 1) * 128, :]))
            p.sec("sp", body)
            dl = p.snap()

            def body(a, width=width, dl=dl, lpar=lpar, buf=buf):
                p.w(a, lpar, dl[lpar])
                p.w(a, "dve")
                p.inc("act", a.activation(sb_q[:, :width], sb_in[buf][:, :width],
                                          AF.Abs, accum_out=sb_abs[:, :]))
            p.sec("act", body)
            in_act[idx] = p.c["act"]

            def body(v, mi=mi):
                p.w(v, "act")
                p.w(v, "pool")
                p.inc("dve", v.tensor_tensor(sb_acc[:, mi:mi + 1], sb_acc[:, mi:mi + 1],
                                             sb_abs[:, :], ALU.add))
            p.sec("dve", body)

    def body(t):
        p.w(t, "dve")
        p.w(t, "pool")
        p.inc("pe", t.matmul(ps_f[0:1, :], sb_onef[:, :], sb_acc[:, :],
                             start=True, stop=True))
    p.sec("pe", body)

    def body(v):
        p.w(v, "pe")
        p.inc("dve", v.tensor_copy(sb_tot[:, :], ps_f[0:1, :]))
    p.sec("dve", body)

    p.sem("ccth")

    def body(g):
        p.w(g, "dve")
        p.dma("dg", g.dma_start(out=io["arth_in"].ap(), in_=sb_tot[:, :]))
        p.w(g, "dg")
        emit_cc(p, g, io["arth_in"], io["arth_out"], "ccth",
                kind="AllReduce", alu=ALU.add, simcc=io.get("simcc", False))
        p.w(g, "ccth")
        p.dma("dg", g.dma_start(out=sb_tot[:, :], in_=io["arth_out"].ap()))
    p.sec("pool", body)
    dl_ar = p.snap()

    jobs = [
        ("up0s", "agu0", 1024, 2048, None),
        ("dn0s", "agd0", 256, 8192, None),
        ("w1s", "agw1", 1024, 2048, 0),
        ("w2s", "agw2", 256, 8192, 1),
    ]
    ag_emitters = []
    prev_ev_box = [None]

    def emit_job(jobi, thw):
        tname, agname, rows, width, thi = jobs[jobi]
        prev_ev = prev_ev_box[0]
        t = io[tname]
        agin = io[agname + "_in"]
        agout = io[agname + "_out"]
        ntile = rows // 128
        nj = width // 128
        evsem = "dga%d" % jobi
        p.sem(evsem)
        for i in range(ntile):
            idx = ldi[0]
            buf = idx % 2
            lpar = "dxa%d" % buf
            ldi[0] += 1

            def body(sy, t=t, i=i, width=width, lpar=lpar, buf=buf,
                     g_=in_act.get(idx - 2, 0)):
                p.w(sy, "act", g_)
                p.w(sy, lpar)
                p.dma(lpar, sy.dma_start(out=sb_in[buf][:, :width],
                                         in_=t[i * 128:(i + 1) * 128, :]))
            p.sec("sp", body)
            dl = p.snap()

            if thi is None:
                def body(a, width=width, dl=dl, lpar=lpar, buf=buf, tg=p.c["pe"]):
                    p.w(a, lpar, dl[lpar])
                    p.w(a, "pe", tg)
                    p.w(a, "dve", thw)
                    p.inc("act", a.activation(sb_q[:, :width], sb_in[buf][:, :width], AF.Sign))
                p.sec("act", body)
                qready, qsem = p.c["act"], "act"
            else:
                def body(a, width=width, dl=dl, thi=thi, lpar=lpar, buf=buf, tg=p.c["pe"]):
                    p.w(a, lpar, dl[lpar])
                    p.w(a, "pe", tg)
                    p.w(a, "dve", thw)
                    p.inc("act", a.activation(sb_q[:, :width], sb_in[buf][:, :width], AF.Sign,
                                              bias=sb_thn[:, thi:thi + 1]))
                    p.inc("act", a.activation(sb_s2[:, :width], sb_in[buf][:, :width], AF.Sign,
                                              bias=sb_th[:, thi:thi + 1]))
                p.sec("act", body)
                al = p.snap()

                def body(v, width=width, al=al):
                    p.w(v, "act", al["act"])
                    p.inc("dve", v.tensor_tensor(sb_q[:, :width], sb_q[:, :width],
                                                 sb_s2[:, :width], ALU.add))
                    p.sw(v, "dve")
                    p.inc("dve", v.tensor_scalar(sb_q[:, :width], sb_q[:, :width],
                                                 0.5, None, op0=ALU.mult))
                p.sec("dve", body)
                qready, qsem = p.c["dve"], "dve"
            in_act[idx] = p.c["act"]

            for j4 in range(0, nj, 4):
                def body(t_, j4=j4, qready=qready, qsem=qsem):
                    p.w(t_, "ds", io["boot_ds"])
                    p.w(t_, qsem, qready)
                    p.w(t_, "dve")
                    for jj in range(4):
                        j = j4 + jj
                        mmi = t_.transpose(ps_t[:, jj * 128:(jj + 1) * 128],
                                           sb_q[:, j * 128:(j + 1) * 128],
                                           sb_idn[:, :])
                    p.inc("pe", mmi)
                p.sec("pe", body)
                pl = p.snap()

                def body(v, j4=j4, i=i, rows=rows, pl=pl, prev_ev=prev_ev):
                    p.w(v, "pe", pl["pe"])
                    if prev_ev is not None:
                        p.w(v, prev_ev[0], prev_ev[1])
                    dst = sb_stage[:, :].rearrange("p (j r) -> p j r", r=rows)[
                        :, j4:j4 + 4, i * 128:(i + 1) * 128]
                    p.inc("dve", v.tensor_copy(dst, ps_t[:, :].rearrange(
                        "p (j r) -> p j r", r=128)[:, 0:4, :]))
                p.sec("dve", body)

        st = p.snap()
        ccname = "cc" + agname
        p.sem(ccname)

        def body(sy, agin=agin, ntile=ntile, rows=rows, st=st, evsem=evsem):
            p.w(sy, "dve", st["dve"])
            for i in range(ntile):
                p.dma(evsem, sy.dma_start(
                    out=agin[i * 128:(i + 1) * 128, :].rearrange("p (j r) -> p j r", r=128),
                    in_=sb_stage[:, :].rearrange("p (j r) -> p j r", r=rows)[:, :, i * 128:(i + 1) * 128]))
        p.sec("sp", body)
        stg = p.snap()

        def emit_ag(agin=agin, agout=agout, ccname=ccname, stg=stg, evsem=evsem):
            def body(g):
                p.w(g, evsem, stg[evsem])
                emit_cc(p, g, agin, agout, ccname, simcc=io.get("simcc", False))
            p.sec("pool", body)
        ag_emitters.append(emit_ag)
        prev_ev_box[0] = (evsem, p.c[evsem])

    emit_job(0, 0)
    emit_job(1, 0)

    def body(a, dl_ar=dl_ar):
        p.w(a, "dg", dl_ar["dg"])
        p.inc("act", a.mul(sb_tot[:, :], sb_tot[:, :], 0.7 / (8192.0 * 2048.0)))
    p.sec("act", body)
    mull = p.snap()

    def body(t, mull=mull):
        p.w(t, "act", mull["act"])
        p.inc("pe", t.matmul(ps_f[:, :], sb_onerow[:, :], sb_tot[0:1, :],
                             start=True, stop=True))
    p.sec("pe", body)

    def body(v):
        p.w(v, "pe")
        p.inc("dve", v.tensor_copy(sb_th[:, :], ps_f[:, :]))
        p.sw(v, "dve")
        p.inc("dve", v.tensor_scalar(sb_thn[:, :], sb_th[:, :], -1.0, None, op0=ALU.mult))
    p.sec("dve", body)
    th_ready = p.snap()

    emit_job(2, th_ready["dve"])
    emit_job(3, th_ready["dve"])
    p.pop()
    return ag_emitters



def emit_l0a(p, io):
    nc = p.nc
    sb_idn = io["sb_idn"]
    sb_onerow = io["sb_onerow"]
    sb_oneb = io["sb_oneb"]
    sb_qn = io["sb_qn"]
    x = io["x"]
    wqkv = io["wqkv0"]

    phase_fence(p)
    p.push()
    sb_xn = p.sb("a0_xn", [128, 8 * D], BF16)
    sb_xnT = p.sb("a0_xnT", [128, DC * TC], BF16)
    sb_kn = p.sb("a0_kn", [128, NH * TC], BF16)
    sb_vf = sb_xn
    sb_w = [p.sb("a0_w0", [128, D], BF16), p.sb("a0_w1", [128, D], BF16)]
    sb_qwsb = p.sb("a0_qw", [128, DC], F32)
    sb_kwsb = p.sb("a0_kw", [128, DC], F32)
    sb_cos = p.sb("a0_cos", [128, TC], BF16)
    sb_sin = p.sb("a0_sin", [128, TC], BF16)
    sb_rp = p.sb("a0_rp", [128, TC], BF16)
    sb_qsq = [p.sb("a0_qsq0", [128, TC], BF16), p.sb("a0_qsq1", [128, TC], BF16)]
    sb_sqrow = p.sb("a0_sqrow", [1, TC], F32)
    sb_vstage = p.sb("a0_vst", [128, D], BF16)
    scr2 = {k: p.sb("a0scr_" + k, [128, 1] if k not in ("sqd", "xt") else [128, D],
                    F32 if k != "sqd" else BF16)
            for k in ("s1", "s2", "negm", "m2", "var", "rstd", "sqd", "xt")}
    ps_a = [p.ps("a0_psa0", [128, 1024], F32), p.ps("a0_psa1", [128, 1024], F32)]
    ps_t = p.ps("a0_pst", [128, 512], BF16)
    ps_rb = p.ps("a0_psrb", [128, TC], F32)

    p.sem("dc"); p.sem("dw0"); p.sem("dw1")

    def body(sy):
        p.dma("ds", sy.dma_start(out=sb_qwsb[:, :], in_=io["qw"].ap()))
        p.dma("ds", sy.dma_start(out=sb_kwsb[:, :], in_=io["kw"].ap()))
    p.sec("sp", body)

    def body(g):
        p.dma("dc", g.dma_start(out=sb_cos[:, :], in_=io["cosr"].ap()))
        p.dma("dc", g.dma_start(out=sb_sin[:, :], in_=io["sinr"].ap()))
    p.sec("pool", body)
    consts = p.snap()

    xnT_done = emit_preln(p, scr2, x, sb_xn, sb_xnT, sb_idn, ps_t, 8, TC)

    p.sem("cckn"); p.sem("ccvt"); p.sem("dkn"); p.sem("dvt")
    evac_dve = {}
    mm_qkv = {}
    eidx = [0]

    def emit_qkv_group(jcs):
        for jc in jcs:
            idx = eidx[0]
            eidx[0] += 1
            par = idx % 2

            def body(g, jc=jc, par=par, idx=idx):
                p.w(g, "pe", mm_qkv.get(idx - 2, 0))
                p.dma("dw%d" % par, g.dma_start(
                    out=sb_w[par][:, :],
                    in_=wqkv[jc * 128:(jc + 1) * 128, :]))
            p.sec("pool", body)
            wl = p.snap()

            def body(t, par=par, wl=wl, idx=idx):
                p.w(t, "dw%d" % par, wl["dw%d" % par])
                p.w(t, "dve", max(xnT_done["dve"], evac_dve.get(idx - 2, 0)))
                for half in range(2):
                    for dc in range(DC):
                        lhs = sb_w[par][:, dc * 128:(dc + 1) * 128]
                        mmi = t.matmul(ps_a[par][:, half * 512:(half + 1) * 512], lhs,
                                       sb_xnT[:, dc * TC + half * 512: dc * TC + (half + 1) * 512],
                                       start=(dc == 0), stop=(dc == DC - 1))
                mm_qkv[idx] = p.inc("pe", mmi)
            p.sec("pe", body)
            ml = p.snap()

            def body(v, jc=jc, par=par, ml=ml, idx=idx):
                p.w(v, "pe", ml["pe"])
                if jc < DC:
                    dst = sb_qn[:, jc * TC:(jc + 1) * TC]
                elif jc < 2 * DC:
                    dst = sb_kn[:, (jc - DC) * TC:(jc - DC + 1) * TC]
                else:
                    dst = sb_vf[:, (jc - 2 * DC) * TC:(jc - 2 * DC + 1) * TC]
                evac_dve[idx] = p.inc("dve", v.tensor_copy(dst, ps_a[par][:, :]))
            p.sec("dve", body)

    def emit_rmsnorm(src, wsb, is_q):
        for h in range(NH):
            par = h % 2

            def body(a, h=h, par=par, src=src):
                p.w(a, "dve", p.c["dve"])
                if h >= 2:
                    p.w(a, "pe")
                p.inc("act", a.activation(sb_qsq[par][:, :], src[:, h * TC:(h + 1) * TC],
                                          AF.Square))
            p.sec("act", body)
            al = p.snap()

            def body(t, h=h, par=par, al=al, dve_now=p.c["dve"]):
                p.w(t, "act", al["act"])
                if h == 0:
                    p.w(t, "dve", dve_now)
                t.matmul(ps_rb[0:1, 0:512], sb_oneb[:, :], sb_qsq[par][:, 0:512],
                         start=(h == 0), stop=(h == NH - 1))
                p.inc("pe", t.matmul(ps_rb[0:1, 512:1024], sb_oneb[:, :],
                                     sb_qsq[par][:, 512:1024],
                                     start=(h == 0), stop=(h == NH - 1)))
            p.sec("pe", body)
        sq = p.snap()

        def body(v, sq=sq):
            p.w(v, "pe", sq["pe"])
            p.inc("dve", v.tensor_scalar(sb_sqrow[:, :], ps_rb[0:1, :], 1.0 / D, 1e-6,
                                         op0=ALU.mult, op1=ALU.add))
        p.sec("dve", body)

        def body(a):
            p.w(a, "dve")
            p.inc("act", a.activation(sb_sqrow[:, :], sb_sqrow[:, :], AF.Sqrt))
        p.sec("act", body)

        def body(v, is_q=is_q):
            p.w(v, "act")
            p.inc("dve", v.reciprocal(sb_sqrow[:, :], sb_sqrow[:, :]))
            if is_q:
                p.sw(v, "dve")
                p.inc("dve", v.tensor_scalar(sb_sqrow[:, :], sb_sqrow[:, :], SM_SCALE,
                                             None, op0=ALU.mult))
        p.sec("dve", body)
        rr = p.snap()

        def body(t, rr=rr):
            p.w(t, "dve", rr["dve"])
            t.matmul(ps_rb[:, 0:512], sb_onerow[:, :], sb_sqrow[0:1, 0:512],
                     start=True, stop=True)
            p.inc("pe", t.matmul(ps_rb[:, 512:1024], sb_onerow[:, :],
                                 sb_sqrow[0:1, 512:1024], start=True, stop=True))
        p.sec("pe", body)
        bb = p.snap()

        for h in range(NH):
            def body(v, h=h, src=src, wsb=wsb, bb=bb):
                p.w(v, "pe", bb["pe"])
                p.w(v, "ds", consts["ds"])
                p.inc("dve", v.scalar_tensor_tensor(
                    src[:, h * TC:(h + 1) * TC], src[:, h * TC:(h + 1) * TC],
                    wsb[:, h:h + 1], ps_rb[:, :],
                    op0=ALU.mult, op1=ALU.mult))
            p.sec("dve", body)

    def emit_rope(src):
        for h in range(NH):
            x1 = src[0:64, h * TC:(h + 1) * TC]
            x2 = src[64:128, h * TC:(h + 1) * TC]

            def body(g, x1=x1, x2=x2):
                p.w(g, "dc", consts["dc"])
                p.w(g, "dve")
                p.inc("pool", g.tensor_tensor(sb_rp[0:64, :], x2, sb_sin[64:128, :], ALU.mult))
                p.inc("pool", g.tensor_tensor(sb_rp[64:128, :], x1, sb_sin[0:64, :], ALU.mult))
            p.sec("pool", body)
            gl = p.snap()

            def body(v, x1=x1, x2=x2, gl=gl):
                p.w(v, "pool", gl["pool"])
                p.inc("dve", v.tensor_tensor(x1, x1, sb_cos[0:64, :], ALU.mult))
                p.inc("dve", v.tensor_tensor(x2, x2, sb_cos[64:128, :], ALU.mult))
                p.sw(v, "dve")
                p.inc("dve", v.tensor_tensor(x1, x1, sb_rp[0:64, :], ALU.subtract))
                p.inc("dve", v.tensor_tensor(x2, x2, sb_rp[64:128, :], ALU.add))
            p.sec("dve", body)

    emit_qkv_group(range(DC, 2 * DC))
    emit_rmsnorm(sb_kn, sb_kwsb, False)
    emit_rope(sb_kn)
    krope_done = p.snap()

    def body(sy, kd=krope_done):
        p.w(sy, "dve", kd["dve"])
        p.dma("dkn", sy.dma_start(out=io["agkn_in"].ap(), in_=sb_kn[:, :]))
    p.sec("sp", body)
    knst = p.snap()

    def body(g, knst=knst):
        p.w(g, "dkn", knst["dkn"])
        emit_cc(p, g, io["agkn_in"], io["agkn_out"], "cckn", rg=RGB,
                simcc=io.get("simcc", False))
    p.sec("pool", body)

    emit_qkv_group(range(2 * DC, 3 * DC))

    for tt in range(8):
        for dc4 in range(0, DC, 4):
            def body(t, tt=tt, dc4=dc4, dnow=p.c["dve"]):
                p.w(t, "dve", max(dnow, p.c["dve"]))
                for jj in range(4):
                    dc = dc4 + jj
                    mmi = t.transpose(ps_t[:, jj * 128:(jj + 1) * 128],
                                      sb_vf[:, dc * TC + tt * 128: dc * TC + (tt + 1) * 128],
                                      sb_idn[:, :])
                p.inc("pe", mmi)
            p.sec("pe", body)
            pl = p.snap()

            def body(v, dc4=dc4, pl=pl, dvt_now=p.c["dvt"]):
                p.w(v, "pe", pl["pe"])
                p.w(v, "dvt", dvt_now)
                p.inc("dve", v.tensor_copy(sb_vstage[:, dc4 * 128:(dc4 + 4) * 128], ps_t[:, :]))
            p.sec("dve", body)
        vl = p.snap()

        def body(sy, tt=tt, vl=vl):
            p.w(sy, "dve", vl["dve"])
            p.dma("dvt", sy.dma_start(out=io["agvt_in"][tt * 128:(tt + 1) * 128, :],
                                      in_=sb_vstage[:, :]))
        p.sec("sp", body)
    vst = p.snap()

    def body(g, vst=vst):
        p.w(g, "dvt", vst["dvt"])
        emit_cc(p, g, io["agvt_in"], io["agvt_out"], "ccvt", rg=RGB,
                simcc=io.get("simcc", False))
    p.sec("pool", body)

    emit_qkv_group(range(0, DC))
    emit_rmsnorm(sb_qn, sb_qwsb, True)
    emit_rope(sb_qn)
    p.pop()



def emit_l0b(p, io):
    nc = p.nc
    sb_idn = io["sb_idn"]
    sb_qn = io["sb_qn"]
    x = io["x"]
    xo = io["x1"]
    agkn = io["agkn_out"]
    agvt = io["agvt_out"]
    simcc = io.get("simcc", False)

    phase_fence(p)
    p.push()
    scr = alloc_ln_scr(p, "b0")
    sb_ctxT = p.sb("b0_ctxT", [128, NH * TC], BF16)
    sb_kh = [p.sb("b0_kh0", [128, 2048], BF16), p.sb("b0_kh1", [128, 2048], BF16)]
    sb_vh = [p.sb("b0_vh0", [128, 2048], BF16), p.sb("b0_vh1", [128, 2048], BF16)]
    sb_msk = p.sb("b0_msk", [128, 8 * 1152], BF16)
    sb_probs = [p.sb("b0_probs0", [128, 1152], BF16), p.sb("b0_probs1", [128, 1152], BF16)]
    sb_probsT = p.sb("b0_probsT", [128, 1152], BF16)
    sb_mx = p.sb("b0_mx", [128, 1], F32)
    sb_negmx = p.sb("b0_negmx", [128, 1], F32)
    sb_l = p.sb("b0_l", [128, 1], F32)
    sb_rl = p.sb("b0_rl", [128, 1], F32)
    sb_wo = p.sb("b0_wo", [128, DC * D], BF16)
    sb_at = p.sb("b0_at", [128, D], F32)
    ps_s = [p.ps("b0_pss0", [128, 1152], F32), p.ps("b0_pss1", [128, 1152], F32)]
    ps_pt = p.ps("b0_pspt", [128, 640], BF16)
    ps_c = p.ps("b0_psc", [128, 128], F32)

    p.sem("dmk"); p.sem("dwo"); p.sem("dk0"); p.sem("dk1")
    p.sem("dv0"); p.sem("dv1"); p.sem("dsgk"); p.sem("dsgv")

    def body(g):
        p.dma("dmk", g.dma_start(out=sb_msk[:, :], in_=io["amask"].ap()))
        p.dma("dwo", g.dma_start(out=sb_wo[:, :].rearrange("p (c j) -> p c j", j=D),
                                 in_=io["wo"].ap().rearrange("(c p) j -> p c j", p=128)))
    p.sec("pool", body)
    consts = p.snap()

    def body(g):
        p.w(g, "cckn", p.c["cckn"])
        dy = io["dy"]
        p.dma("dsgk", g.dma_start(out=io["stg_k0"][:, :],
                                  in_=agkn[bass.ds(dy["o128l"], 128), :]))
        p.w(g, "ccvt", p.c["ccvt"])
        p.dma("dsgv", g.dma_start(out=io["stg_v0"][:, :],
                                  in_=agvt[bass.ds(dy["oTl"], TC), :]))
    p.sec("pool", body)
    stg = p.snap()

    exp2_act = {"prev": p.c["act"]}
    norm_dve0 = {}
    ptcopy_act = {"a": 0, "v": 0}
    last_ctx_dve = 0
    pv_pe = {}

    for h in range(NH):
        par = h % 2

        def body(g, h=h, par=par):
            p.w(g, "pe", pv_pe.get(h - 2, 0))
            p.w(g, "dsgk", stg["dsgk"])
            p.dma("dk%d" % par, g.dma_start(
                out=sb_kh[par][:, 0:1024],
                in_=io["stg_k0"][:, h * TC:(h + 1) * TC]))
            p.dma("dk%d" % par, g.dma_start(
                out=sb_kh[par][:, 1024:2048],
                in_=io["agkn_in"][:, h * TC:(h + 1) * TC]))
        p.sec("pool", body)
        kl = p.c["dk%d" % par]

        def body(g, h=h, par=par):
            p.w(g, "dsgv", stg["dsgv"])
            p.dma("dv%d" % par, g.dma_start(
                out=sb_vh[par][:, 0:1024].rearrange("p (kt j) -> p kt j", j=128),
                in_=io["stg_v0"][:, h * 128:(h + 1) * 128]
                    .rearrange("(kt p) j -> p kt j", p=128)))
            p.dma("dv%d" % par, g.dma_start(
                out=sb_vh[par][:, 1024:2048].rearrange("p (kt j) -> p kt j", j=128),
                in_=io["agvt_in"][:, h * 128:(h + 1) * 128]
                    .rearrange("(kt p) j -> p kt j", p=128)))
        p.sec("pool", body)
        vl = p.c["dv%d" % par]

        for step in range(9):
            if step < 8:
                qt = step
                spar = qt % 2

                def body(t, h=h, qt=qt, par=par, kl=kl, spar=spar,
                         ea=exp2_act.get((h, qt - 2), exp2_act.get("prev", 0))):
                    p.w(t, "dk%d" % par, kl)
                    p.w(t, "act", ea)
                    lhs = sb_qn[:, h * TC + qt * 128: h * TC + (qt + 1) * 128]
                    for w0, w1 in [(0, 512), (512, 1024), (1024, 1152)]:
                        mmi = t.matmul(ps_s[spar][:, w0:w1], lhs,
                                       sb_kh[par][:, qt * 128 + w0: qt * 128 + w1],
                                       start=True, stop=True)
                    p.inc("pe", mmi)
                p.sec("pe", body)
                sc = p.snap()

                def body(v, qt=qt, spar=spar, sc=sc):
                    p.w(v, "pe", sc["pe"])
                    p.w(v, "dmk", consts["dmk"])
                    p.w(v, "dve")
                    p.inc("dve", v.tensor_tensor(
                        ps_s[spar][:, :], ps_s[spar][:, :],
                        sb_msk[:, qt * 1152:(qt + 1) * 1152], ALU.add))
                    p.sw(v, "dve")
                    p.inc("dve", v.reduce_max(sb_mx[:, :], ps_s[spar][:, :], axis=AXL.X))
                    p.sw(v, "dve")
                    p.inc("dve", v.tensor_scalar(sb_negmx[:, :], sb_mx[:, :], -1.0,
                                                 None, op0=ALU.mult))
                p.sec("dve", body)
                mx_ = p.snap()

                def body(a, spar=spar, mx_=mx_):
                    p.w(a, "dve", mx_["dve"])
                    p.w(a, "act")
                    p.inc("act", a.activation(sb_probs[spar][:, :], ps_s[spar][:, :],
                                              AF.Exp, bias=sb_negmx[:, :],
                                              accum_out=sb_l[:, :]))
                p.sec("act", body)
                exp2_act[(h, qt)] = p.c["act"]
                e1 = p.snap()

                def body(v, spar=spar, e1=e1):
                    p.w(v, "act", e1["act"])
                    p.inc("dve", v.reciprocal(sb_rl[:, :], sb_l[:, :]))
                    p.sw(v, "dve")
                    p.inc("dve", v.tensor_scalar(sb_probs[spar][:, :], sb_probs[spar][:, :],
                                                 sb_rl[:, :], None, op0=ALU.mult))
                p.sec("dve", body)
                norm_dve0[(h, qt)] = p.c["dve"]

            if step >= 1:
                q = step - 1
                spar2 = q % 2
                for half, (t0, t1) in enumerate([(0, 5), (5, 9)]):
                    def body(t, q=q, spar2=spar2, t0=t0, t1=t1,
                             nd=norm_dve0[(h, q)], pc=ptcopy_act):
                        p.w(t, "dve", max(nd, pc["v"]))
                        for j in range(t0, t1):
                            mmi = t.transpose(ps_pt[:, (j - t0) * 128:(j - t0 + 1) * 128],
                                              sb_probs[spar2][:, j * 128:(j + 1) * 128],
                                              sb_idn[:, :])
                        p.inc("pe", mmi)
                    p.sec("pe", body)
                    tp = p.snap()

                    def body(v, t0=t0, t1=t1, tp=tp):
                        p.w(v, "pe", tp["pe"])
                        p.inc("dve", v.tensor_copy(
                            sb_probsT[:, t0 * 128:t1 * 128],
                            ps_pt[:, 0:(t1 - t0) * 128]))
                    p.sec("dve", body)
                    ptcopy_act["v"] = p.c["dve"]
                pt_done = p.snap()

                def body(t, q=q, par=par, vl=vl, pt_done=pt_done, lc=last_ctx_dve):
                    p.w(t, "dve", max(lc, pt_done["dve"]))
                    p.w(t, "dv%d" % par, vl)
                    for j in range(9):
                        mmi = t.matmul(ps_c[:, :],
                                       sb_vh[par][:, (q + j) * 128:(q + j + 1) * 128],
                                       sb_probsT[:, j * 128:(j + 1) * 128],
                                       start=(j == 0), stop=(j == 8))
                    p.inc("pe", mmi)
                p.sec("pe", body)
                pv = p.snap()

                def body(v, h=h, q=q, pv=pv):
                    p.w(v, "pe", pv["pe"])
                    p.inc("dve", v.tensor_copy(
                        sb_ctxT[:, h * TC + q * 128: h * TC + (q + 1) * 128], ps_c[:, :]))
                p.sec("dve", body)
                last_ctx_dve = p.c["dve"]
        pv_pe[h] = p.c["pe"]
        exp2_act["prev"] = exp2_act[(h, 7)]
    attn_done = p.snap()

    for tt in range(8):
        for half in range(2):
            def body(t, tt=tt, half=half, ad=attn_done):
                p.w(t, "dve", ad["dve"])
                p.w(t, "act", ad["act"])
                p.w(t, "dwo", consts["dwo"])
                p.w(t, "dve")
                for js in range(2):
                    for dc in range(DC):
                        lhs = sb_ctxT[:, dc * TC + tt * 128: dc * TC + (tt + 1) * 128]
                        mmi = t.matmul(
                            ps_s[0][:, js * 512:(js + 1) * 512], lhs,
                            sb_wo[:, dc * D + half * 1024 + js * 512: dc * D + half * 1024 + (js + 1) * 512],
                            start=(dc == 0), stop=(dc == DC - 1))
                p.inc("pe", mmi)
            p.sec("pe", body)
            wl = p.snap()

            def body(v, half=half, wl=wl):
                p.w(v, "pe", wl["pe"])
                p.w(v, "act")
                p.inc("dve", v.tensor_copy(sb_at[:, half * 1024:(half + 1) * 1024],
                                           ps_s[0][:, 0:1024]))
            p.sec("dve", body)
        src_ready = p.snap()
        emit_ln_residual(p, scr, sb_at, x, xo, tt, src_ready)
    p.pop()



def emit_ffn(p, io, layer):
    nc = p.nc
    sb_idn = io["sb_idn"]
    sb_onerow = io["sb_onerow"]
    x = io["x1"] if layer == 0 else io["x3"]
    xo = io["x2"] if layer == 0 else io["out"]
    wupT = io["agu0_out"] if layer == 0 else io["agw1_out"]
    wdnT = io["agd0_out"] if layer == 0 else io["agw2_out"]
    ccu = "ccagu0" if layer == 0 else "ccagw1"
    ccd = "ccagd0" if layer == 0 else "ccagw2"

    phase_fence(p)
    p.push()
    pre = "f%d_" % layer
    scr = alloc_ln_scr(p, pre)
    sb_xn = p.sb(pre + "xn", [128, 4 * D], BF16)
    sb_xnT = p.sb(pre + "xnT", [128, DC * 512], BF16)
    sb_hid = p.sb(pre + "hid", [128, HC * 512], BF16)
    sb_wu = [p.sb(pre + "wu%d" % i, [128, D], BF16) for i in range(4)]
    sb_wd = [p.sb(pre + "wd0", [128, 8192], BF16), p.sb(pre + "wd1", [128, 8192], BF16)]
    sb_dn = p.sb(pre + "dn", [128, DC * 512], BF16)
    sb_at = p.sb(pre + "at", [128, D], BF16)
    sb_sc1 = p.sb(pre + "sc1", [128, HC], F32)
    sb_sc2 = p.sb(pre + "sc2", [128, DC], F32)
    sb_osc = p.sb(pre + "osc", [1, 1], F32)
    scr2 = {k: p.sb(pre + "scr_" + k, [128, 1] if k not in ("sqd", "xt") else [128, D],
                    F32 if k != "sqd" else BF16)
            for k in ("s1", "s2", "negm", "m2", "var", "rstd", "sqd", "xt")}
    ps_u = [p.ps(pre + "psu0", [128, 512], F32), p.ps(pre + "psu1", [128, 512], F32)]
    ps_d = [p.ps(pre + "psd0", [128, 512], F32), p.ps(pre + "psd1", [128, 512], F32)]
    ps_t = p.ps(pre + "pst", [128, 512], BF16)
    ps_b = p.ps(pre + "psb", [128, 2], F32)
    for k in ("du0", "du1", "du2", "du3", "dd0", "dd1"):
        p.sem(k)

    def body(sy):
        if layer == 0:
            p.dma("ds", sy.dma_start(out=sb_sc1[:, :], in_=io["upsc"].ap()))
            p.dma("ds", sy.dma_start(out=sb_sc2[:, :], in_=io["dsc"].ap()))
            p.dma("ds", sy.dma_start(out=sb_osc[:, :], in_=io["osc"].ap()))
        else:
            p.dma("ds", sy.dma_start(out=sb_sc1[:, :], in_=io["b1"].ap()))
            p.dma("ds", sy.dma_start(out=sb_sc2[:, :], in_=io["b2"].ap()))
    p.sec("sp", body)
    consts = p.snap()

    if layer == 0:
        def body(t):
            p.w(t, "ds", consts["ds"])
            p.w(t, "pool", 1)
            p.inc("pe", t.matmul(ps_b[:, 0:1], sb_onerow[:, :], sb_osc[:, :],
                                 start=True, stop=True))
        p.sec("pe", body)

        def body(v):
            p.w(v, "pe")
            p.inc("dve", v.scalar_tensor_tensor(sb_sc2[:, :], sb_sc2[:, :], 1.0,
                                                ps_b[:, 0:1].to_broadcast([128, DC]),
                                                op0=ALU.mult, op1=ALU.mult))
        p.sec("dve", body)
    sc_ready = p.snap()

    mm_up = {}
    mm_dn = {}
    for h2 in range(2):
        xnT_done = emit_preln(p, scr2, x[h2 * 512: h2 * 512 + 512, :],
                              sb_xn, sb_xnT, sb_idn, ps_t, 4, 512)

        evac_act = {}
        for hc in range(HC):
            g_ = h2 * HC + hc
            par = hc % 2
            buf = g_ % 4
            rnk, hcl = hc // 8, hc % 8

            def body(g, buf=buf, rnk=rnk, hcl=hcl, g_=g_):
                p.w(g, "pe", mm_up.get(g_ - 4, 0))
                p.w(g, ccu, p.c[ccu])
                p.dma("du%d" % buf, g.dma_start(
                    out=sb_wu[buf][:, :],
                    in_=wupT[rnk * 1024 + hcl * 128: rnk * 1024 + (hcl + 1) * 128, :]))
            p.sec("pool", body)
            wl = p.snap()

            def body(t, buf=buf, par=par, wl=wl, xd=xnT_done,
                     ea=evac_act.get(hc - 2, 0), g_=g_):
                p.w(t, "du%d" % buf, wl["du%d" % buf])
                p.w(t, "dve", xd["dve"])
                p.w(t, "act", max(ea, sc_ready["act"]))
                for dc in range(DC):
                    mmi = t.matmul(ps_u[par][:, :], sb_wu[buf][:, dc * 128:(dc + 1) * 128],
                                   sb_xnT[:, dc * 512:(dc + 1) * 512],
                                   start=(dc == 0), stop=(dc == DC - 1))
                mm_up[g_] = p.inc("pe", mmi)
            p.sec("pe", body)
            ml = p.snap()

            def body(a, hc=hc, par=par, ml=ml):
                p.w(a, "pe", ml["pe"])
                p.w(a, "dve", sc_ready["dve"])
                p.w(a, "ds", consts["ds"])
                if layer == 0:
                    p.inc("act", a.activation(sb_hid[:, hc * 512:(hc + 1) * 512],
                                              ps_u[par][:, :], AF.Relu,
                                              scale=sb_sc1[:, hc:hc + 1]))
                else:
                    p.inc("act", a.activation(sb_hid[:, hc * 512:(hc + 1) * 512],
                                              ps_u[par][:, :], AF.Gelu,
                                              bias=sb_sc1[:, hc:hc + 1]))
            p.sec("act", body)
            evac_act[hc] = p.c["act"]
        hid_done = p.snap()

        evac_dve = {}
        for jc in range(DC):
            g_ = h2 * DC + jc
            par = jc % 2
            rnk, jcl = jc // 2, jc % 2

            def body(g, par=par, rnk=rnk, jcl=jcl, g_=g_):
                p.w(g, "pe", mm_dn.get(g_ - 2, 0))
                p.w(g, ccd, p.c[ccd])
                p.dma("dd%d" % par, g.dma_start(
                    out=sb_wd[par][:, :],
                    in_=wdnT[rnk * 256 + jcl * 128: rnk * 256 + (jcl + 1) * 128, :]))
            p.sec("pool", body)
            wl = p.snap()

            def body(t, par=par, wl=wl, hd=hid_done, ed=evac_dve.get(jc - 2, 0), g_=g_):
                p.w(t, "dd%d" % par, wl["dd%d" % par])
                p.w(t, "act", hd["act"])
                p.w(t, "dve", ed)
                for hc in range(HC):
                    mmi = t.matmul(ps_d[par][:, :], sb_wd[par][:, hc * 128:(hc + 1) * 128],
                                   sb_hid[:, hc * 512:(hc + 1) * 512],
                                   start=(hc == 0), stop=(hc == HC - 1))
                mm_dn[g_] = p.inc("pe", mmi)
            p.sec("pe", body)
            ml = p.snap()

            def body(v, jc=jc, par=par, ml=ml):
                p.w(v, "pe", ml["pe"])
                p.w(v, "ds", consts["ds"])
                op = ALU.mult if layer == 0 else ALU.add
                evac_dve[jc] = p.inc("dve", v.tensor_scalar(
                    sb_dn[:, jc * 512:(jc + 1) * 512], ps_d[par][:, :],
                    sb_sc2[:, jc:jc + 1], None, op0=op))
            p.sec("dve", body)
        dn_done = p.snap()

        for tl in range(4):
            tt = h2 * 4 + tl
            for jc4 in range(0, DC, 4):
                def body(t, tl=tl, jc4=jc4, dd=dn_done):
                    p.w(t, "dve", max(dd["dve"], p.c["dve"]))
                    for jj in range(4):
                        jc = jc4 + jj
                        mmi = t.transpose(ps_t[:, jj * 128:(jj + 1) * 128],
                                          sb_dn[:, jc * 512 + tl * 128: jc * 512 + (tl + 1) * 128],
                                          sb_idn[:, :])
                    p.inc("pe", mmi)
                p.sec("pe", body)
                pl = p.snap()

                def body(v, jc4=jc4, pl=pl):
                    p.w(v, "pe", pl["pe"])
                    p.w(v, "act")
                    p.inc("dve", v.tensor_copy(sb_at[:, jc4 * 128:(jc4 + 4) * 128], ps_t[:, :]))
                p.sec("dve", body)
            src_ready = p.snap()
            emit_ln_residual(p, scr, sb_at, x, xo, tt, src_ready)
    p.pop()



def emit_l1a(p, io):
    nc = p.nc
    sb_idn = io["sb_idn"]
    sb_oneb = io["sb_oneb"]
    sb_qf = io["sb_qf"]
    sb_qt = io["sb_qt"]
    x = io["x2"]
    wqkv = io["wqkv1"]

    phase_fence(p)
    p.push()
    sb_xn = p.sb("a1_xn", [128, 8 * D], BF16)
    sb_xnT = p.sb("a1_xnT", [128, DC * TC], BF16)
    sb_kn = p.sb("a1_kn", [128, NH * TC], BF16)
    sb_vf = sb_xn
    sb_w = [p.sb("a1_w0", [128, D], BF16), p.sb("a1_w1", [128, D], BF16)]
    sb_bias = p.sb("a1_bias", [128, 48], F32)
    sb_qsq = [p.sb("a1_qsq0", [128, TC], BF16), p.sb("a1_qsq1", [128, TC], BF16)]
    sb_nrow = p.sb("a1_nrow", [1, TC], F32)
    sb_vstage = p.sb("a1_vst", [128, D], BF16)
    scr2 = {k: p.sb("a1scr_" + k, [128, 1] if k not in ("sqd", "xt") else [128, D],
                    F32 if k != "sqd" else BF16)
            for k in ("s1", "s2", "negm", "m2", "var", "rstd", "sqd", "xt")}
    ps_a = [p.ps("a1_psa0", [128, 1024], F32), p.ps("a1_psa1", [128, 1024], F32)]
    ps_t = p.ps("a1_pst", [128, 512], BF16)
    ps_r = p.ps("a1_psr", [1, TC], F32)
    p.sem("dn")

    def body(sy):
        p.dma("ds", sy.dma_start(out=sb_bias[:, :], in_=io["bias1"].ap()))
    p.sec("sp", body)
    consts = p.snap()

    xnT_done = emit_preln(p, scr2, x, sb_xn, sb_xnT, sb_idn, ps_t, 8, TC)

    p.sem("dn"); p.sem("ccnrm"); p.sem("cckf1"); p.sem("ccvt1")
    p.sem("dkf"); p.sem("dvt1")
    evac_dve = {}
    mm_qkv = {}
    eidx = [0]

    def emit_qkv_group(jcs):
        for jc in jcs:
            idx = eidx[0]
            eidx[0] += 1
            par = idx % 2

            def body(g, jc=jc, par=par, idx=idx):
                p.w(g, "pe", mm_qkv.get(idx - 2, 0))
                p.dma("dw%d" % par, g.dma_start(
                    out=sb_w[par][:, :],
                    in_=wqkv[jc * 128:(jc + 1) * 128, :]))
            p.sec("pool", body)
            wl = p.snap()

            def body(t, par=par, wl=wl, idx=idx):
                p.w(t, "dw%d" % par, wl["dw%d" % par])
                p.w(t, "dve", max(xnT_done["dve"], evac_dve.get(idx - 2, 0)))
                for half in range(2):
                    for dc in range(DC):
                        lhs = sb_w[par][:, dc * 128:(dc + 1) * 128]
                        mmi = t.matmul(ps_a[par][:, half * 512:(half + 1) * 512], lhs,
                                       sb_xnT[:, dc * TC + half * 512: dc * TC + (half + 1) * 512],
                                       start=(dc == 0), stop=(dc == DC - 1))
                mm_qkv[idx] = p.inc("pe", mmi)
            p.sec("pe", body)
            ml = p.snap()

            def body(v, jc=jc, par=par, ml=ml, idx=idx):
                p.w(v, "pe", ml["pe"])
                p.w(v, "ds", consts["ds"])
                if jc < DC:
                    evac_dve[idx] = p.inc("dve", v.tensor_scalar(
                        sb_qf[:, jc * TC:(jc + 1) * TC], ps_a[par][:, :],
                        sb_bias[:, jc:jc + 1], SM_SCALE, op0=ALU.add, op1=ALU.mult))
                elif jc < 2 * DC:
                    evac_dve[idx] = p.inc("dve", v.tensor_scalar(
                        sb_kn[:, (jc - DC) * TC:(jc - DC + 1) * TC], ps_a[par][:, :],
                        sb_bias[:, jc:jc + 1], None, op0=ALU.add))
                else:
                    evac_dve[idx] = p.inc("dve", v.tensor_scalar(
                        sb_vf[:, (jc - 2 * DC) * TC:(jc - 2 * DC + 1) * TC], ps_a[par][:, :],
                        sb_bias[:, jc:jc + 1], None, op0=ALU.add))
            p.sec("dve", body)

    def emit_transposes(src, is_v):
        for tt in range(8):
            for dc4 in range(0, DC, 4):
                def body(t, tt=tt, dc4=dc4, src=src, dnow=p.c["dve"]):
                    p.w(t, "dve", max(dnow, p.c["dve"]))
                    for jj in range(4):
                        dc = dc4 + jj
                        mmi = t.transpose(ps_t[:, jj * 128:(jj + 1) * 128],
                                          src[:, dc * TC + tt * 128: dc * TC + (tt + 1) * 128],
                                          sb_idn[:, :])
                    p.inc("pe", mmi)
                p.sec("pe", body)
                pl = p.snap()

                if is_v:
                    def body(v, dc4=dc4, pl=pl, dvt_now=p.c["dvt1"]):
                        p.w(v, "pe", pl["pe"])
                        p.w(v, "dvt1", dvt_now)
                        p.inc("dve", v.tensor_copy(sb_vstage[:, dc4 * 128:(dc4 + 4) * 128], ps_t[:, :]))
                    p.sec("dve", body)
                else:
                    def body(v, tt=tt, dc4=dc4, pl=pl):
                        p.w(v, "pe", pl["pe"])
                        p.inc("dve", v.tensor_copy(
                            sb_qt[:, tt * D + dc4 * 128: tt * D + (dc4 + 4) * 128], ps_t[:, :]))
                    p.sec("dve", body)
            if is_v:
                vl = p.snap()

                def body(sy, tt=tt, vl=vl):
                    p.w(sy, "dve", vl["dve"])
                    p.dma("dvt1", sy.dma_start(out=io["agvt1_in"][tt * 128:(tt + 1) * 128, :],
                                               in_=sb_vstage[:, :]))
                p.sec("sp", body)

    emit_qkv_group(range(DC, 2 * DC))
    k_done = p.snap()

    def body(sy, kd=k_done):
        p.w(sy, "dve", kd["dve"])
        p.dma("dkf", sy.dma_start(out=io["agkf1_in"].ap(), in_=sb_kn[:, :]))
    p.sec("sp", body)
    knst = p.snap()

    def body(g, knst=knst):
        p.w(g, "dkf", knst["dkf"])
        emit_cc(p, g, io["agkf1_in"], io["agkf1_out"], "cckf1", rg=RGB,
                simcc=io.get("simcc", False))
    p.sec("pool", body)

    emit_qkv_group(range(2 * DC, 3 * DC))
    emit_transposes(sb_vf, True)
    vst = p.snap()

    def body(g, vst=vst):
        p.w(g, "dvt1", vst["dvt1"])
        emit_cc(p, g, io["agvt1_in"], io["agvt1_out"], "ccvt1", rg=RGB,
                simcc=io.get("simcc", False))
    p.sec("pool", body)

    emit_qkv_group(range(0, DC))
    q_done = p.snap()

    for h in range(NH):
        par = h % 2

        def body(a, h=h, par=par, qd=q_done):
            p.w(a, "dve", qd["dve"])
            if h >= 2:
                p.w(a, "pe")
            p.inc("act", a.activation(sb_qsq[par][:, :], sb_qf[:, h * TC:(h + 1) * TC],
                                      AF.Square))
        p.sec("act", body)
        al = p.snap()

        def body(t, h=h, par=par, al=al, dve_now=p.c["dve"]):
            p.w(t, "act", al["act"])
            if h == 0:
                p.w(t, "dve", dve_now)
            if h > 0:
                p.w(t, "dve")
            t.matmul(ps_r[0:1, 0:512], sb_oneb[:, :], sb_qsq[par][:, 0:512],
                     start=True, stop=True)
            p.inc("pe", t.matmul(ps_r[0:1, 512:1024], sb_oneb[:, :], sb_qsq[par][:, 512:1024],
                                 start=True, stop=True))
        p.sec("pe", body)
        pe_l = p.snap()

        def body(v, h=h, pe_l=pe_l):
            p.w(v, "pe", pe_l["pe"])
            p.w(v, "dn", p.c["dn"])
            p.inc("dve", v.tensor_copy(sb_nrow[:, :], ps_r[0:1, :]))
        p.sec("dve", body)
        ne = p.snap()

        def body(sy, h=h, ne=ne):
            p.w(sy, "dve", ne["dve"])
            p.dma("dn", sy.dma_start(out=io["agnrm_in"].ap()[:, h * TC:(h + 1) * TC],
                                     in_=sb_nrow[:, :]))
        p.sec("sp", body)
    nrml = p.snap()

    def body(g, nrml=nrml):
        p.w(g, "dn", nrml["dn"])
        emit_cc(p, g, io["agnrm_in"], io["agnrm_out"], "ccnrm", rg=RGB,
                simcc=io.get("simcc", False))
    p.sec("pool", body)

    emit_transposes(sb_qf, False)
    p.pop()



def emit_l1mid(p, io):
    nc = p.nc
    simcc = io.get("simcc", False)
    phase_fence(p)
    p.push()
    sb_n = p.sb("m_n", [16, S], F32)
    sb_z = p.sb("m_z", [16, S], F32)
    sb_mx = p.sb("m_mx", [16, 8], F32)
    sb_msk = p.sb("m_msk", [16, S], F32)
    sb_zero = p.sb("m_zero", [16, S], F32)
    sb_slot = p.sb("m_slot", [16, S], F32)
    sb_slown = io["sb_slotm"]
    p.sem("dnm"); p.sem("dsl")

    def body(g):
        p.w(g, "ccnrm", p.c["ccnrm"])
        p.dma("dnm", g.dma_start(
            out=sb_n[:, :].rearrange("h (r t) -> h r t", t=TC),
            in_=io["agnrm_out"].ap().rearrange("r (h t) -> h r t", h=16)))
        p.inc("pool", g.memset(sb_zero[:, :], 0.0))
    p.sec("pool", body)
    cl = p.snap()

    def body(v, cl=cl):
        p.w(v, "dnm", cl["dnm"])
        p.w(v, "pool", cl["pool"])
        src = sb_n
        for r in range(8):
            p.inc("dve", v.max(out=sb_mx[:, :], in_=src[:, :]))
            p.sw(v, "dve")
            if r == 7:
                p.inc("dve", v.memset(sb_mx[:, 6:8], -1e30))
                p.sw(v, "dve")
            p.inc("dve", v.match_replace(out=sb_z[:, :], in_to_replace=sb_mx[:, :],
                                         in_values=src[:, :], imm_value=-1e30))
            p.sw(v, "dve")
            src = sb_z
        p.inc("dve", v.tensor_scalar(sb_msk[:, :], sb_z[:, :], -1e29, None, op0=ALU.is_lt))
        p.sw(v, "dve")
        p.inc("dve", v.memset(sb_msk[:, 0:1], 1.0))
        p.inc("dve", v.memset(sb_msk[:, S - 1:S], 1.0))
        p.sw(v, "dve")
        p.inc("dve", v.tensor_tensor_scan(sb_slot[:, :], sb_zero[:, :], sb_msk[:, :],
                                          0.0, op0=ALU.add, op1=ALU.add))
        p.sw(v, "dve")
        p.inc("dve", v.tensor_mul(sb_slot[:, :], sb_slot[:, :], sb_msk[:, :]))
        p.sw(v, "dve")
        p.inc("dve", v.tensor_scalar(sb_slot[:, :], sb_slot[:, :], -1.0, None, op0=ALU.add))
    p.sec("dve", body)
    done = p.snap()

    def body(g, done=done):
        p.w(g, "dve", done["dve"])
        p.dma("dsl", g.dma_start(out=io["slotscr"].ap(), in_=sb_slot[:, :]))
        p.w(g, "dsl")
        p.dma("dsl", g.dma_start(out=sb_slown[:, :],
                                 in_=io["slotscr"].ap()[:, bass.ds(io["dy"]["oTo"], TC)]))
    p.sec("pool", body)
    p.pop()



def emit_l1b(p, io):
    nc = p.nc
    sb_idn = io["sb_idn"]
    sb_idnf = io["sb_idnf"]
    sb_iota = io["sb_iota"]
    sb_onerow = io["sb_onerow"]
    sb_qf = io["sb_qf"]
    sb_qt = io["sb_qt"]
    sb_slotm = io["sb_slotm"]
    x = io["x2"]
    xo = io["x3"]
    agkf = io["agkf1_out"]
    agvt = io["agvt1_out"]
    RG = [[0, 1, 2, 3], [4, 5, 6, 7]]
    simcc = io.get("simcc", False)

    phase_fence(p)
    p.push()
    scr = alloc_ln_scr(p, "b1")
    sb_ctx = p.sb("b1_ctx", [128, 8 * D], BF16)
    sb_kh = [p.sb("b1_kh0", [128, 1152], BF16), p.sb("b1_kh1", [128, 1152], BF16)]
    sb_vh = [p.sb("b1_vh0", [128, 1152], BF16), p.sb("b1_vh1", [128, 1152], BF16)]
    sb_kg = [p.sb("b1_kg0", [128, S], BF16), p.sb("b1_kg1", [128, S], BF16)]
    sb_vg = [p.sb("b1_vg0", [128, S], BF16), p.sb("b1_vg1", [128, S], BF16)]
    sb_msk = p.sb("b1_msk", [128, 8 * 256], BF16)
    sb_probs = [p.sb("b1_probs0", [128, 256], BF16), p.sb("b1_probs1", [128, 256], BF16)]
    sb_probsT = p.sb("b1_probsT", [128, 256], BF16)
    sb_mx = p.sb("b1_mx", [128, 1], F32)
    sb_negmx = p.sb("b1_negmx", [128, 1], F32)
    sb_l = p.sb("b1_l", [128, 1], F32)
    sb_rl = p.sb("b1_rl", [128, 1], F32)
    sb_slotm_tm = p.sb("b1_sltm", [128, 8 * 16], F32)
    sb_sct = p.sb("b1_sct", [128, NH * 8 * 64], BF16)
    sb_qg = p.sb("b1_qg", [64, 2048], F32)
    sb_qga = p.sb("b1_qga", [64, 2048], F32)
    sb_qgb = p.sb("b1_qgb", [64, 128], BF16)
    sb_qgT = p.sb("b1_qgT", [128, 64], BF16)
    sb_mh = p.sb("b1_mh", [64, 2], F32)
    sb_mrow = p.sb("b1_mrow", [64, 1], F32)
    sb_m64 = p.sb("b1_m64", [1, 64], F32)
    sb_mp = p.sb("b1_mp", [1, 1], F32)
    sb_negmb = p.sb("b1_negmb", [128, 1], F32)
    sb_lacc = p.sb("b1_lacc", [64, 8], F32)
    sb_lsum = p.sb("b1_lsum", [64, 1], F32)
    sb_grl = p.sb("b1_grl", [64, 1], F32)
    sb_dump = p.sb("b1_dump", [64, 2048], BF16)
    sb_pt = [p.sb("b1_pt0", [128, 256], BF16), p.sb("b1_pt1", [128, 256], BF16)]
    sb_ga = p.sb("b1_ga", [64, 129], BF16)
    sb_scc = p.sb("b1_scc", [64, 128], BF16)
    sb_om = p.sb("b1_om", [128, 1], F32)
    p.sem("dmk1"); p.sem("dq0"); p.sem("ccar"); p.sem("dsg")
    p.sem("dk0"); p.sem("dk1"); p.sem("dsl")

    def body(g):
        p.dma("dmk1", g.dma_start(out=sb_msk[:, :], in_=io["lmask"].ap()))
        p.inc("pool", g.memset(sb_ga[:, :], 1.0))
    p.sec("pool", body)
    consts = p.snap()

    scA = ExitStack()
    ps_trf = scA.enter_context(nc.psum_tensor("b1_pstrf", [128, 16], F32))
    ps_qg = scA.enter_context(nc.psum_tensor("b1_psqg", [64, 128], F32))
    ps_s = [scA.enter_context(nc.psum_tensor("b1_pss0", [128, 256], F32)),
            scA.enter_context(nc.psum_tensor("b1_pss1", [128, 256], F32))]
    ps_pt = scA.enter_context(nc.psum_tensor("b1_pspt", [128, 256], BF16))
    ps_c = scA.enter_context(nc.psum_tensor("b1_psc", [128, 128], F32))

    for tt in range(8):
        def body(t, tt=tt):
            p.w(t, "dsl", p.c["dsl"])
            if tt > 0:
                p.w(t, "dve")
            p.inc("pe", t.transpose(ps_trf[:, :], sb_slotm[:, tt * 128:(tt + 1) * 128],
                                    sb_idnf[0:16, 0:16]))
        p.sec("pe", body)
        pl = p.snap()

        def body(v, tt=tt, pl=pl):
            p.w(v, "pe", pl["pe"])
            p.inc("dve", v.tensor_copy(sb_slotm_tm[:, tt * 16:(tt + 1) * 16], ps_trf[:, :]))
        p.sec("dve", body)
    sl_done = p.snap()

    for h in range(NH):
        def body(v, h=h):
            p.w(v, "dve", sl_done["dve"])
            for tt in range(8):
                p.inc("dve", v.tensor_scalar(
                    sb_sct[:, (h * 8 + tt) * 64:(h * 8 + tt + 1) * 64],
                    sb_iota[:, :], sb_slotm_tm[:, tt * 16 + h: tt * 16 + h + 1],
                    None, op0=ALU.is_equal))
        p.sec("dve", body)
    sct_done = p.snap()

    for h in range(NH):
        def body(t, h=h, sd=sct_done):
            p.w(t, "dve", sd["dve"])
            if h > 0:
                p.w(t, "dve")
            for tt in range(8):
                mmi = t.matmul(ps_qg[:, :],
                               sb_sct[:, (h * 8 + tt) * 64:(h * 8 + tt + 1) * 64],
                               sb_qt[:, tt * D + h * 128: tt * D + (h + 1) * 128],
                               start=(tt == 0), stop=(tt == 7))
            p.inc("pe", mmi)
        p.sec("pe", body)
        ql = p.snap()

        def body(v, h=h, ql=ql):
            p.w(v, "pe", ql["pe"])
            p.inc("dve", v.tensor_copy(sb_qg[:, h * 128:(h + 1) * 128], ps_qg[:, :]))
        p.sec("dve", body)
    qg_done = p.snap()

    def body(g, qd=qg_done):
        p.w(g, "dve", qd["dve"])
        p.dma("dq0", g.dma_start(out=io["arqg_in"].ap(), in_=sb_qg[:, :]))
        p.w(g, "dq0", p.c["dq0"])
        emit_cc(p, g, io["arqg_in"], io["arqg_out"], "ccar",
                kind="AllReduce", alu=ALU.add, rg=RG, simcc=simcc)
    p.sec("pool", body)

    def body(g):
        p.w(g, "cckf1", p.c["cckf1"])
        p.w(g, "ccvt1", p.c["ccvt1"])
        dy = io["dy"]
        p.dma("dsg", g.dma_start(out=io["stg_k1l"][:, :],
                                 in_=agkf[bass.ds(dy["o128l"], 128), :]))
        p.dma("dsg", g.dma_start(out=io["stg_k1r"][:, :],
                                 in_=agkf[bass.ds(dy["o128r"], 128), :]))
        p.dma("dsg", g.dma_start(out=io["stg_v1l"][:, :],
                                 in_=agvt[bass.ds(dy["oTl960"], 64), :]))
        p.dma("dsg", g.dma_start(out=io["stg_v1r"][:, :],
                                 in_=agvt[bass.ds(dy["oTr"], 64), :]))
    p.sec("pool", body)
    stg1 = p.snap()

    exp_act1 = {"prev": p.c["act"]}
    ptc_dve1 = {0: 0}
    norm_dve1 = {}
    last_ctx_dve = 0
    pv_pe1 = {}
    for h in range(NH):
        par = h % 2

        def body(g, h=h, par=par):
            p.w(g, "pe", pv_pe1.get(h - 2, 0))
            p.w(g, "dsg", stg1["dsg"])
            p.dma("dk%d" % par, g.dma_start(
                out=sb_kh[par][:, 0:64],
                in_=io["stg_k1l"][:, h * TC + 960: h * TC + 1024]))
            p.dma("dk%d" % par, g.dma_start(
                out=sb_kh[par][:, 64:1088],
                in_=io["agkf1_in"][:, h * TC:(h + 1) * TC]))
            p.dma("dk%d" % par, g.dma_start(
                out=sb_kh[par][:, 1088:1152],
                in_=io["stg_k1r"][:, h * TC: h * TC + 64]))
            p.dma("dk%d" % par, g.dma_start(
                out=sb_vh[par][0:64, 0:128],
                in_=io["stg_v1l"][:, h * 128:(h + 1) * 128]))
            p.dma("dk%d" % par, g.dma_start(
                out=sb_vh[par][64:128, 0:128],
                in_=io["agvt1_in"][0:64, h * 128:(h + 1) * 128]))
            p.dma("dk%d" % par, g.dma_start(
                out=sb_vh[par][:, 128:1024].rearrange("p (kt j) -> p kt j", j=128),
                in_=io["agvt1_in"][64:960, h * 128:(h + 1) * 128]
                    .rearrange("(kt p) j -> p kt j", p=128)))
            p.dma("dk%d" % par, g.dma_start(
                out=sb_vh[par][0:64, 1024:1152],
                in_=io["agvt1_in"][960:1024, h * 128:(h + 1) * 128]))
            p.dma("dk%d" % par, g.dma_start(
                out=sb_vh[par][64:128, 1024:1152],
                in_=io["stg_v1r"][:, h * 128:(h + 1) * 128]))
        p.sec("pool", body)
        kl = p.c["dk%d" % par]

        for step in range(9):
            if step < 8:
                qt = step
                spar = qt % 2

                def body(t, h=h, qt=qt, par=par, kl=kl, spar=spar,
                         ea=exp_act1.get((h, qt - 2), exp_act1.get("prev", 0))):
                    p.w(t, "dk%d" % par, kl)
                    p.w(t, "act", ea)
                    p.w(t, "dmk1", consts["dmk1"])
                    t.matmul(ps_s[spar][:, 0:256], sb_idn[:, :],
                             sb_msk[:, qt * 256:(qt + 1) * 256],
                             start=True, stop=True)
                    p.inc("pe", t.matmul(ps_s[spar][:, 0:256],
                                         sb_qf[:, h * TC + qt * 128: h * TC + (qt + 1) * 128],
                                         sb_kh[par][:, qt * 128: qt * 128 + 256],
                                         start=False, stop=True))
                p.sec("pe", body)
                sc = p.snap()

                def body(v, qt=qt, spar=spar, sc=sc):
                    p.w(v, "pe", sc["pe"])
                    p.w(v, "dve")
                    p.inc("dve", v.reduce_max(sb_mx[:, :], ps_s[spar][:, 0:256], axis=AXL.X))
                    p.sw(v, "dve")
                    p.inc("dve", v.tensor_scalar(sb_negmx[:, :], sb_mx[:, :], -1.0,
                                                 None, op0=ALU.mult))
                p.sec("dve", body)
                mm_ = p.snap()

                def body(a, spar=spar, mm_=mm_):
                    p.w(a, "dve", mm_["dve"])
                    p.inc("act", a.activation(sb_probs[spar][:, :], ps_s[spar][:, 0:256],
                                              AF.Exp, bias=sb_negmx[:, :],
                                              accum_out=sb_l[:, :]))
                p.sec("act", body)
                ee = p.snap()
                exp_act1[(h, qt)] = ee["act"]

                def body(v, spar=spar, ee=ee):
                    p.w(v, "act", ee["act"])
                    p.inc("dve", v.reciprocal(sb_rl[:, :], sb_l[:, :]))
                    p.sw(v, "dve")
                    p.inc("dve", v.tensor_scalar(sb_probs[spar][:, :], sb_probs[spar][:, :],
                                                 sb_rl[:, :], None, op0=ALU.mult))
                p.sec("dve", body)
                norm_dve1[(h, qt)] = p.c["dve"]

            if step >= 1:
                q = step - 1
                spar2 = q % 2

                def body(t, q=q, spar2=spar2, nd=norm_dve1[(h, q)], pt=ptc_dve1[0]):
                    p.w(t, "dve", max(nd, pt))
                    for j in range(2):
                        mmi = t.transpose(ps_pt[:, j * 128:(j + 1) * 128],
                                          sb_probs[spar2][:, j * 128:(j + 1) * 128],
                                          sb_idn[:, :])
                    p.inc("pe", mmi)
                p.sec("pe", body)
                tp = p.snap()

                def body(v, tp=tp):
                    p.w(v, "pe", tp["pe"])
                    p.inc("dve", v.tensor_copy(sb_probsT[:, :], ps_pt[:, :]))
                p.sec("dve", body)
                ptc_dve1[0] = p.c["dve"]
                pte = p.snap()

                def body(t, q=q, par=par, kl=kl, pte=pte, lc=last_ctx_dve):
                    p.w(t, "dve", max(pte["dve"], lc))
                    p.w(t, "dk%d" % par, kl)
                    for j in range(2):
                        mmi = t.matmul(ps_c[:, :],
                                       sb_probsT[:, j * 128:(j + 1) * 128],
                                       sb_vh[par][:, (q + j) * 128:(q + j + 1) * 128],
                                       start=(j == 0), stop=(j == 1))
                    p.inc("pe", mmi)
                p.sec("pe", body)
                pv = p.snap()

                def body(v, h=h, q=q, pv=pv):
                    p.w(v, "pe", pv["pe"])
                    p.inc("dve", v.tensor_copy(
                        sb_ctx[:, q * D + h * 128: q * D + (h + 1) * 128], ps_c[:, :]))
                p.sec("dve", body)
                last_ctx_dve = p.c["dve"]
        pv_pe1[h] = p.c["pe"]
        exp_act1["prev"] = exp_act1[(h, 7)]
    local_done = p.snap()

    def body(g):
        p.w(g, "ccar")
        p.dma("dq0", g.dma_start(out=sb_qga[:, :], in_=io["arqg_out"].ap()))
    p.sec("pool", body)
    ar_l = p.snap()

    scA.close()
    ps_g = p.ps("b1_psg", [64, 2048], F32)
    ps_st = [p.ps("b1_psst0", [128, 256], F32), p.ps("b1_psst1", [128, 256], F32)]
    ps_gcbl = p.ps("b1_psgc", [128, 132], F32)
    ps_tr2 = p.ps("b1_pstr2", [128, 128], BF16)

    gpv_pe = {}
    p.sem("dkg0"); p.sem("dkg1")
    for h in range(NH):
        par = h % 2

        def body(g, h=h, par=par):
            p.w(g, "pe", gpv_pe.get(h - 2, 0))
            p.dma("dkg%d" % par, g.dma_start(
                out=sb_kg[par][:, :].rearrange("p (r t) -> p r t", t=TC),
                in_=agkf[:, h * TC:(h + 1) * TC].rearrange("(r p) t -> p r t", p=128)))
            p.dma("dkg%d" % par, g.dma_start(
                out=sb_vg[par][:, :].rearrange("p (kt j2) -> p kt j2", j2=128),
                in_=agvt[:, h * 128:(h + 1) * 128]
                    .rearrange("(kt p) j2 -> p kt j2", p=128)))
        p.sec("pool", body)
        kl = p.c["dkg%d" % par]

        def body(v, h=h, al=ar_l):
            p.w(v, "dq0", al["dq0"])
            p.w(v, "pe")
            p.inc("dve", v.tensor_copy(sb_qgb[:, :], sb_qga[:, h * 128:(h + 1) * 128]))
        p.sec("dve", body)
        cq = p.snap()

        def body(t, cq=cq):
            p.w(t, "dve", cq["dve"])
            p.inc("pe", t.transpose(ps_tr2[0:128, 0:64], sb_qgb[:, :], sb_idn[0:64, 0:64]))
        p.sec("pe", body)
        tq = p.snap()

        def body(v, tq=tq):
            p.w(v, "pe", tq["pe"])
            p.inc("dve", v.tensor_copy(sb_qgT[:, :], ps_tr2[0:128, 0:64]))
        p.sec("dve", body)
        qgT_done = p.snap()

        def emit_scores_half(half, dw, aw, kl=kl, par=par):
            def body(t, half=half, dw=dw, aw=aw, kl=kl, par=par):
                p.w(t, "dkg%d" % par, kl)
                p.w(t, "dve", dw)
                p.w(t, "act", aw)
                for i4 in range(4):
                    mmi = t.matmul(ps_g[:, i4 * 512:(i4 + 1) * 512], sb_qgT[:, :],
                                   sb_kg[par][:, half * 2048 + i4 * 512: half * 2048 + (i4 + 1) * 512],
                                   start=True, stop=True)
                p.inc("pe", mmi)
            p.sec("pe", body)
            return p.snap()

        s0 = emit_scores_half(0, qgT_done["dve"], p.c["act"])

        def body(v, s0=s0):
            p.w(v, "pe", s0["pe"])
            p.inc("dve", v.reduce_max(sb_mh[:, 0:1], ps_g[:, :], axis=AXL.X))
        p.sec("dve", body)
        m0 = p.snap()

        s1 = emit_scores_half(1, m0["dve"], p.c["act"])

        def body(v, s1=s1):
            p.w(v, "pe", s1["pe"])
            p.inc("dve", v.reduce_max(sb_mh[:, 1:2], ps_g[:, :], axis=AXL.X))
            p.sw(v, "dve")
            p.inc("dve", v.tensor_max(sb_mrow[:, :], sb_mh[:, 0:1], sb_mh[:, 1:2]))
        p.sec("dve", body)
        m1 = p.snap()

        def body(t, m1=m1):
            p.w(t, "dve", m1["dve"])
            p.inc("pe", t.transpose(ps_gcbl[0:1, 0:64], sb_mrow[:, :], sb_idnf[0:64, 0:64]))
        p.sec("pe", body)
        tm = p.snap()

        def body(v, tm=tm):
            p.w(v, "pe", tm["pe"])
            p.inc("dve", v.tensor_copy(sb_m64[:, :], ps_gcbl[0:1, 0:64]))
            p.sw(v, "dve")
            p.inc("dve", v.reduce_max(sb_mp[:, :], sb_m64[:, :], axis=AXL.X))
            p.sw(v, "dve")
            p.inc("dve", v.tensor_scalar(sb_mp[:, :], sb_mp[:, :], -1.0, None, op0=ALU.mult))
        p.sec("dve", body)
        mp = p.snap()

        def body(t, mp=mp):
            p.w(t, "dve", mp["dve"])
            p.inc("pe", t.matmul(ps_gcbl[0:128, 64:65], sb_onerow[:, :], sb_mp[:, :],
                                 start=True, stop=True))
        p.sec("pe", body)
        bb = p.snap()

        def body(v, bb=bb):
            p.w(v, "pe", bb["pe"])
            p.inc("dve", v.tensor_copy(sb_negmb[:, :], ps_gcbl[0:128, 64:65]))
        p.sec("dve", body)
        nb = p.snap()

        def body(a, nb=nb):
            p.w(a, "dve", nb["dve"])
            for i4 in range(4):
                p.inc("act", a.activation(sb_dump[:, i4 * 512:(i4 + 1) * 512],
                                          ps_g[:, i4 * 512:(i4 + 1) * 512],
                                          AF.Exp, bias=sb_negmb[0:64, :],
                                          accum_out=sb_lacc[:, 4 + i4:5 + i4]))
        p.sec("act", body)
        e1 = p.snap()

        s0b = emit_scores_half(0, p.c["dve"], e1["act"])

        def body(a, s0b=s0b):
            p.w(a, "pe", s0b["pe"])
            for i4 in range(4):
                p.inc("act", a.activation(sb_dump[:, i4 * 512:(i4 + 1) * 512],
                                          ps_g[:, i4 * 512:(i4 + 1) * 512],
                                          AF.Exp, bias=sb_negmb[0:64, :],
                                          accum_out=sb_lacc[:, i4:1 + i4]))
        p.sec("act", body)
        e0 = p.snap()

        def body(v, e0=e0):
            p.w(v, "act", e0["act"])
            p.inc("dve", v.reduce_sum(sb_lsum[:, :], sb_lacc[:, :], axis=AXL.X))
        p.sec("dve", body)
        lv = p.snap()

        def body(v, lv=lv):
            p.inc("dve", v.reciprocal(sb_grl[:, :], sb_lsum[:, :]))
        p.sec("dve", body)
        rl_done = p.snap()

        exp_acts = {}
        for kgp in range(4):
            kt0 = kgp * 8

            def body(t, kt0=kt0, par=par, rd=rl_done, ea=exp_acts.get(kgp - 1, 0)):
                p.w(t, "dve", rd["dve"])
                p.w(t, "act", ea)
                for par2 in range(2):
                    for i in range(4):
                        kt = kt0 + par2 * 4 + i
                        mmi = t.matmul(ps_st[par2][:, i * 64:(i + 1) * 64],
                                       sb_kg[par][:, kt * 128:(kt + 1) * 128],
                                       sb_qgT[:, :], start=True, stop=True)
                p.inc("pe", mmi)
            p.sec("pe", body)
            stl = p.snap()

            def body(a, stl=stl):
                p.w(a, "pe", stl["pe"])
                p.inc("act", a.activation(sb_pt[0][:, :], ps_st[0][:, :], AF.Exp,
                                          bias=sb_negmb[:, :]))
                p.inc("act", a.activation(sb_pt[1][:, :], ps_st[1][:, :], AF.Exp,
                                          bias=sb_negmb[:, :]))
            p.sec("act", body)
            exp_acts[kgp] = p.c["act"]
            el = p.snap()

            def body(t, kt0=kt0, kgp=kgp, par=par, el=el):
                p.w(t, "act", el["act"])
                for par2 in range(2):
                    for i in range(4):
                        kt = kt0 + par2 * 4 + i
                        mmi = t.matmul(ps_gcbl[0:64, 0:128],
                                       sb_pt[par2][:, i * 64:(i + 1) * 64],
                                       sb_vg[par][:, kt * 128:(kt + 1) * 128],
                                       start=(kgp == 0 and par2 == 0 and i == 0),
                                       stop=(kgp == 3 and par2 == 1 and i == 3))
                p.inc("pe", mmi)
            p.sec("pe", body)
        pv_done = p.snap()
        gpv_pe[h] = pv_done["pe"]

        def body(v, pv_done=pv_done):
            p.w(v, "pe", pv_done["pe"])
            p.w(v, "act")
            p.w(v, "pool", consts["pool"])
            p.inc("dve", v.tensor_scalar(sb_ga[0:64, 0:128], ps_gcbl[0:64, 0:128],
                                         sb_grl[:, :], None, op0=ALU.mult))
        p.sec("dve", body)
        ga_done = p.snap()

        for tt in range(8):
            def body(t, h=h, tt=tt, ga=ga_done):
                p.w(t, "dve", max(ga["dve"], p.c["dve"]))
                p.inc("pe", t.transpose(ps_tr2[0:64, 0:128],
                                        sb_sct[:, (h * 8 + tt) * 64:(h * 8 + tt + 1) * 64],
                                        sb_idn[:, :]))
            p.sec("pe", body)
            t1 = p.snap()

            def body(v, t1=t1):
                p.w(v, "pe", t1["pe"])
                p.inc("dve", v.tensor_copy(sb_scc[:, :], ps_tr2[0:64, 0:128]))
            p.sec("dve", body)
            t2 = p.snap()

            def body(t, t2=t2):
                p.w(t, "dve", t2["dve"])
                p.inc("pe", t.matmul(ps_gcbl[0:128, 0:129], sb_scc[:, :], sb_ga[:, :],
                                     start=True, stop=True))
            p.sec("pe", body)
            t3 = p.snap()

            def body(v, h=h, tt=tt, t3=t3):
                p.w(v, "pe", t3["pe"])
                p.inc("dve", v.tensor_scalar(sb_om[:, :], ps_gcbl[0:128, 128:129],
                                             -1.0, 1.0, op0=ALU.mult, op1=ALU.add))
                p.sw(v, "dve")
                p.inc("dve", v.scalar_tensor_tensor(
                    sb_ctx[:, tt * D + h * 128: tt * D + (h + 1) * 128],
                    sb_ctx[:, tt * D + h * 128: tt * D + (h + 1) * 128],
                    sb_om[:, :], ps_gcbl[0:128, 0:128],
                    op0=ALU.mult, op1=ALU.add))
            p.sec("dve", body)

    for tt in range(8):
        src_ready = p.snap()
        emit_ln_residual(p, scr, sb_ctx[:, tt * D:(tt + 1) * D], x, xo, tt, src_ready)
    p.pop()



def build_fused(sim_no_cc=False, upto=8):
    p = P("mega")
    nc = p.nc
    io = {"simcc": sim_no_cc}

    def ei(n, s, d=F32):
        return nc.dram_tensor(n, s, d, kind="ExternalInput")

    io["x"] = ei("x", [TC, D])
    io["wqkv0"] = ei("wqkv0", [3 * D, D])
    io["qw"] = ei("qw", [128, DC])
    io["kw"] = ei("kw", [128, DC])
    io["cosr"] = ei("cosr", [128, TC])
    io["sinr"] = ei("sinr", [128, TC])
    io["amask"] = ei("amask", [128, 8 * 1152], BF16)
    io["wo"] = ei("wo", [D, D])
    io["up0s"] = ei("up0s", [1024, 2048])
    io["dn0s"] = ei("dn0s", [256, 8192])
    io["w1s"] = ei("w1s", [1024, 2048])
    io["w2s"] = ei("w2s", [256, 8192])
    io["upsc"] = ei("upsc", [128, 64])
    io["dsc"] = ei("dsc", [128, 16])
    io["osc"] = ei("osc", [1, 1])
    io["wqkv1"] = ei("wqkv1", [3 * D, D])
    io["bias1"] = ei("bias1", [128, 48])
    io["lmask"] = ei("lmask", [128, 8 * 256], BF16)
    io["b1"] = ei("b1", [128, 64])
    io["b2"] = ei("b2", [128, 16])
    io["out"] = nc.dram_tensor("out", [TC, D], F32, kind="ExternalOutput")

    io["x1"] = p.dram("x1s", [TC, D])
    io["x2"] = p.dram("x2s", [TC, D])
    io["x3"] = p.dram("x3s", [TC, D])
    io["arth_in"] = p.dram("arth_in", [1, 2])
    io["arth_out"] = p.dram("arth_out", [1, 2])
    for nm, shp in [("agu0", ([1024, 2048], [8 * 1024, 2048])),
                    ("agd0", ([256, 8192], [8 * 256, 8192])),
                    ("agw1", ([1024, 2048], [8 * 1024, 2048])),
                    ("agw2", ([256, 8192], [8 * 256, 8192]))]:
        io[nm + "_in"] = p.dram(nm + "_in", shp[0], BF16)
        io[nm + "_out"] = p.dram(nm + "_out", shp[1], BF16)
    io["agkn_in"] = p.dram("agkn_in", [128, NH * TC], BF16)
    io["agkn_out"] = p.dram("agkn_out", [4 * 128, NH * TC], BF16)
    io["agvt_in"] = p.dram("agvt_in", [TC, D], BF16)
    io["agvt_out"] = p.dram("agvt_out", [4 * TC, D], BF16)
    io["agkf1_in"] = p.dram("agkf1_in", [128, NH * TC], BF16)
    io["agkf1_out"] = p.dram("agkf1_out", [4 * 128, NH * TC], BF16)
    io["agvt1_in"] = p.dram("agvt1_in", [TC, D], BF16)
    io["agvt1_out"] = p.dram("agvt1_out", [4 * TC, D], BF16)
    io["agnrm_in"] = p.dram("agnrm_in", [1, NH * TC])
    io["agnrm_out"] = p.dram("agnrm_out", [4, NH * TC])
    io["arqg_in"] = p.dram("arqg_in", [64, 2048])
    io["arqg_out"] = p.dram("arqg_out", [64, 2048])
    io["slotscr"] = p.dram("slotscr", [16, S])
    io["stg_k0"] = p.dram("stg_k0", [128, NH * TC], BF16)
    io["stg_v0"] = p.dram("stg_v0", [TC, D], BF16)
    io["stg_k1l"] = p.dram("stg_k1l", [128, NH * TC], BF16)
    io["stg_k1r"] = p.dram("stg_k1r", [128, NH * TC], BF16)
    io["stg_v1l"] = p.dram("stg_v1l", [64, D], BF16)
    io["stg_v1r"] = p.dram("stg_v1r", [64, D], BF16)

    idn = p.const(IDN_BF, "c_idn")
    idnf = p.const(IDN_F32, "c_idnf")
    iota = p.const(IOTA64, "c_iota")

    p.start()

    io["sb_idn"] = p.sb("g_idn", [128, 128], BF16)
    io["sb_idnf"] = p.sb("g_idnf", [128, 128], F32)
    io["sb_iota"] = p.sb("g_iota", [128, 64], BF16)
    io["sb_onerow"] = p.sb("g_onerow", [1, 128], F32)
    io["sb_oneb"] = p.sb("g_oneb", [128, 1], BF16)
    io["sb_onef"] = p.sb("g_onef", [128, 1], F32)

    p.sem("dxa0"); p.sem("dxa1"); p.sem("dst")

    def body(sy):
        p.dma("ds", sy.dma_start(out=io["sb_idn"][:, :], in_=idn.ap()))
        p.dma("ds", sy.dma_start(out=io["sb_idnf"][:, :], in_=idnf.ap()))
        p.dma("ds", sy.dma_start(out=io["sb_iota"][:, :], in_=iota.ap()))
    p.sec("sp", body)
    io["boot_ds"] = p.c["ds"]

    def body(g):
        io["dy"] = p.dyn(static=sim_no_cc)
        g.memset(io["sb_onerow"][:, :], 1.0)
        g.memset(io["sb_oneb"][:, :], 1.0)
        p.inc("pool", g.memset(io["sb_onef"][:, :], 1.0))
    p.sec("pool", body)

    ag_emitters = []
    if upto >= 1:
        ag_emitters = emit_prep(p, io)

    if upto >= 2:
        p.push()
        io["sb_qn"] = p.sb("g_qn", [128, NH * TC], BF16)
        emit_l0a(p, io)
        for em in ag_emitters[:2]:
            em()
        if upto >= 3:
            emit_l0b(p, io)
        p.pop()

    if upto >= 4:
        emit_ffn(p, io, 0)

    if upto >= 5:
        p.push()
        io["sb_qf"] = p.sb("g_qf", [128, NH * TC], BF16)
        io["sb_qt"] = p.sb("g_qt", [128, 8 * D], BF16)
        io["sb_slotm"] = p.sb("g_slotm", [16, TC], F32)
        emit_l1a(p, io)
        for em in ag_emitters[2:]:
            em()
        if upto >= 6:
            emit_l1mid(p, io)
        if upto >= 7:
            emit_l1b(p, io)
        p.pop()

    if upto >= 8:
        emit_ffn(p, io, 1)

    def body(sy):
        for k in list(p.S.keys()):
            if k in ("pe", "dve", "act", "pool"):
                continue
            p.w(sy, k)
    p.sec("sp", body)
    return p



def _rope_tables():
    inv_freq = 1.0 / (10000.0 ** (np.arange(0, 128, 2, dtype=np.float32) / 128.0))
    pos = np.arange(S, dtype=np.float32)
    fr = pos[:, None] * inv_freq[None, :]
    return np.cos(fr).astype(np.float32), np.sin(fr).astype(np.float32)


def _l0_mask(c):
    s0 = (c % 4) * TC
    m = np.zeros((8, 128, 1152), np.float32)
    for qt in range(8):
        qpos = s0 + qt * 128 + np.arange(128)
        kpos = s0 - 1024 + qt * 128 + np.arange(1152)
        valid = (kpos[None, :] <= qpos[:, None]) & (qpos[:, None] - kpos[None, :] <= 1023) \
                & (kpos[None, :] >= 0)
        m[qt] = np.where(valid, 0.0, NEG)
    return np.ascontiguousarray(m.transpose(1, 0, 2).reshape(128, 8 * 1152).astype(BF))


def _l1_mask(c):
    s0 = (c % 4) * TC
    lo = s0 - 64
    m = np.zeros((8, 128, 256), np.float32)
    for qt in range(8):
        kpos = lo + qt * 128 + np.arange(256)
        bad = (kpos < 0) | (kpos >= S)
        m[qt, :, :] = np.where(bad[None, :], NEG, 0.0)
    return np.ascontiguousarray(m.transpose(1, 0, 2).reshape(128, 8 * 256).astype(BF))


def _qkv_arr(w):
    return np.ascontiguousarray(
        w.reshape(16, 128, 48, 128).transpose(2, 1, 0, 3).reshape(6144, 2048))


def kernel(**inputs):
    x_full = np.asarray(inputs["L0_attn_hidden_states"], dtype=np.float32)
    wqkv0 = np.ascontiguousarray(inputs["L0_attn_Wqkv"], dtype=np.float32)
    qw = np.asarray(inputs["L0_attn_q_norm_w"], dtype=np.float32)
    kw = np.asarray(inputs["L0_attn_k_norm_w"], dtype=np.float32)
    wo = np.ascontiguousarray(inputs["L0_attn_Wo"], dtype=np.float32)
    up0 = np.asarray(inputs["L0_ffn_up_weight"], dtype=np.float32)
    dn0 = np.asarray(inputs["L0_ffn_down_weight"], dtype=np.float32)
    upsc = np.asarray(inputs["L0_ffn_up_scale"], dtype=np.float32)
    dsc = np.asarray(inputs["L0_ffn_down_scale"], dtype=np.float32)
    osc = np.asarray(inputs["L0_ffn_output_scale"], dtype=np.float32)
    wq1 = np.asarray(inputs["L1_attn_Wq"], dtype=np.float32)
    wk1 = np.asarray(inputs["L1_attn_Wk"], dtype=np.float32)
    wv1 = np.asarray(inputs["L1_attn_Wv"], dtype=np.float32)
    bq1 = np.asarray(inputs["L1_attn_bq"], dtype=np.float32)
    bk1 = np.asarray(inputs["L1_attn_bk"], dtype=np.float32)
    bv1 = np.asarray(inputs["L1_attn_bv"], dtype=np.float32)
    w1 = np.asarray(inputs["L1_ffn_w1"], dtype=np.float32)
    b1 = np.asarray(inputs["L1_ffn_b1"], dtype=np.float32)
    w2 = np.asarray(inputs["L1_ffn_w2"], dtype=np.float32)
    b2 = np.asarray(inputs["L1_ffn_b2"], dtype=np.float32)

    cosf, sinf = _rope_tables()
    wqkv0a = _qkv_arr(wqkv0)
    wqkv1a = _qkv_arr(np.concatenate([wq1, wk1, wv1], axis=1))
    bias1 = np.ascontiguousarray(np.concatenate([bq1, bk1, bv1]).reshape(48, 128).T)

    p = build_fused()
    nc = p.finish()
    in_maps = []
    for c in range(NCORES):
        b, s0 = c // 4, (c % 4) * TC
        in_maps.append({
            "x": np.ascontiguousarray(x_full[b, s0:s0 + TC, :]),
            "wqkv0": wqkv0a,
            "qw": np.ascontiguousarray(qw.reshape(16, 128).T),
            "kw": np.ascontiguousarray(kw.reshape(16, 128).T),
            "cosr": np.ascontiguousarray(np.concatenate([cosf[s0:s0 + TC, :].T] * 2, 0)),
            "sinr": np.ascontiguousarray(np.concatenate([sinf[s0:s0 + TC, :].T] * 2, 0)),
            "amask": _l0_mask(c),
            "wo": wo,
            "up0s": np.ascontiguousarray(up0[c * 1024:(c + 1) * 1024, :]),
            "dn0s": np.ascontiguousarray(dn0[c * 256:(c + 1) * 256, :]),
            "w1s": np.ascontiguousarray(w1[c * 1024:(c + 1) * 1024, :]),
            "w2s": np.ascontiguousarray(w2[c * 256:(c + 1) * 256, :]),
            "upsc": np.ascontiguousarray(upsc.reshape(64, 128).T),
            "dsc": np.ascontiguousarray(dsc.reshape(16, 128).T),
            "osc": np.ascontiguousarray(osc.reshape(1, 1)),
            "wqkv1": wqkv1a,
            "bias1": bias1,
            "lmask": _l1_mask(c),
            "b1": np.ascontiguousarray(b1.reshape(64, 128).T),
            "b2": np.ascontiguousarray(b2.reshape(16, 128).T),
        })
    res = run_bass_kernel_spmd(nc, in_maps, core_ids=list(range(NCORES))).results
    out = np.concatenate([np.asarray(r["out"]) for r in res], axis=0)
    return out.reshape(B, S, D)


# version 13
# speedup vs baseline: 1.0029x; 1.0029x over previous
import numpy as np
import ml_dtypes
import concourse.bass as bass
import concourse.mybir as mybir
from concourse.bass_utils import run_bass_kernel_spmd
from concourse.expressions import smax, smin
from contextlib import ExitStack

F32 = mybir.dt.float32
BF16 = mybir.dt.bfloat16
AF = mybir.ActivationFunctionType
ALU = mybir.AluOpType
AXL = mybir.AxisListType
BF = ml_dtypes.bfloat16

NCORES = 8
B, S, D, NH, HD = 2, 4096, 2048, 16, 128
TC = 1024
DC = D // 128
HC = 64
SM_SCALE = 1.0 / float(np.sqrt(HD))
NEG = -1e9

RGB = [[0, 1, 2, 3], [4, 5, 6, 7]]

IDN_BF = np.eye(128, dtype=BF)
IDN_F32 = np.eye(128, dtype=np.float32)
IOTA64 = np.broadcast_to(np.arange(64, dtype=np.float32), (128, 64)).astype(BF).copy()

_ENG = {"pe": "tensor", "dve": "vector", "act": "scalar", "pool": "gpsimd",
        "sp": "sync"}


class P:
    def __init__(self, name="prog"):
        self.nc = bass.Bass("TRN2", target_bir_lowering=False)
        self.ctx = ExitStack()
        self.scopes = []
        self.S = {}
        self.c = {}
        self.block = None

    def start(self):
        e = self.ctx.enter_context
        for s in ("pe", "dve", "act", "pool", "ds", "dg"):
            self.S[s] = e(self.nc.semaphore(s))
            self.c[s] = 0
        self.block = e(self.nc.Block())

    def sem(self, name):
        if name not in self.S:
            self.S[name] = self.ctx.enter_context(self.nc.semaphore(name))
            self.c[name] = 0
        return self.S[name]

    def finish(self):
        while self.scopes:
            self.pop()
        self.ctx.close()
        return self.nc

    def push(self):
        self.scopes.append(ExitStack())

    def pop(self):
        self.scopes.pop().close()

    def _enter(self, cm):
        tgt = self.scopes[-1] if self.scopes else self.ctx
        return tgt.enter_context(cm)

    def sb(self, name, shape, dtype):
        return self._enter(self.nc.sbuf_tensor(name, shape, dtype))

    def ps(self, name, shape, dtype=F32):
        return self._enter(self.nc.psum_tensor(name, shape, dtype))

    def dram(self, name, shape, dtype=F32, kind="Internal"):
        return self.nc.dram_tensor(name, shape, dtype, kind=kind)

    def const(self, arr, name):
        return self.nc.inline_tensor(np.ascontiguousarray(arr), name)

    def sec(self, eng, fn):
        getattr(self.block, _ENG[eng])(fn)

    def inc(self, key, inst, n=1):
        inst.then_inc(self.S[key], n)
        self.c[key] += n
        return self.c[key]

    def dma(self, key, inst):
        return self.inc(key, inst, 16)

    def w(self, eng_obj, key, count=None):
        if count is None:
            count = self.c[key]
        if count > 0:
            eng_obj.wait_ge(self.S[key], count)

    def sw(self, eng_obj, key):
        self.w(eng_obj, key)

    def snap(self, *keys):
        if not keys:
            return dict(self.c)
        return {k: self.c[k] for k in keys}

    def dyn(self, static=False):
        if static:
            return {"o128l": 0, "o128r": 128, "oTl": 0, "oTl960": 960,
                    "oTr": TC, "oTo": 0}
        pid = self.nc.partition_id(engines=[mybir.EngineType.Pool])
        q4 = pid % 4
        lft = smax(q4 - 1, 0)
        rgt = smin(q4 + 1, 3)
        return {
            "o128l": lft * 128, "o128r": rgt * 128,
            "oTl": lft * TC, "oTl960": lft * TC + 960, "oTr": rgt * TC,
            "oTo": q4 * TC,
        }


def emit_cc(p, g, ain, aout, semname, kind="AllGather", alu=ALU.bypass,
            rg=None, simcc=False):
    p.sem(semname)
    if simcc:
        rr = ain.shape[0]
        for r in range(aout.shape[0] // rr):
            p.dma(semname, g.dma_start(out=aout[r * rr:(r + 1) * rr, :], in_=ain[:, :]))
    else:
        if rg is None:
            rg = [list(range(NCORES))]
        g.collective_compute(
            kind, alu, replica_groups=rg,
            ins=[ain.ap().opt()], outs=[aout.ap().opt()]).then_inc(p.S[semname], 1)
        p.c[semname] += 1


def phase_fence(p):
    sn = p.snap()
    dma_sems = [k for k in p.S
                if k not in ("pe", "dve", "act", "pool") and not k.startswith("cc")]

    def body(sy, sn=sn, dma_sems=dma_sems):
        for k in ("pe", "dve", "act", "pool"):
            p.w(sy, k, sn[k])
        for k in dma_sems:
            p.w(sy, k, sn[k])
    p.sec("sp", body)

    def body(g, sn=sn, dma_sems=dma_sems):
        for k in ("pe", "dve", "act"):
            p.w(g, k, sn[k])
        for k in dma_sems:
            p.w(g, k, sn[k])
    p.sec("pool", body)

    def body(v, sn=sn, dma_sems=dma_sems):
        for k in dma_sems:
            p.w(v, k, sn[k])
    p.sec("dve", body)

    def body(a, sn=sn, dma_sems=dma_sems):
        for k in dma_sems:
            p.w(a, k, sn[k])
    p.sec("act", body)



def alloc_ln_scr(p, pre):
    return {
        "s1": p.sb(pre + "lr_s1", [128, 1], F32),
        "s2": p.sb(pre + "lr_s2", [128, 1], F32),
        "negm": p.sb(pre + "lr_negm", [128, 1], F32),
        "m2": p.sb(pre + "lr_m2", [128, 1], F32),
        "var": p.sb(pre + "lr_var", [128, 1], F32),
        "rstd": p.sb(pre + "lr_rstd", [128, 1], F32),
        "sqd": p.sb(pre + "lr_sqd", [128, D], BF16),
        "xt": p.sb(pre + "lr_xt", [128, D], F32),
        "xo": p.sb(pre + "lr_xo", [128, D], F32),
    }


def emit_ln_residual(p, scr, src_sb, x, xo, tt, src_ready):
    lpar = "dxa%d" % (tt % 2)

    def body(sy, tt=tt, lpar=lpar):
        p.w(sy, "dve")
        p.w(sy, "dst")
        p.w(sy, lpar)
        p.dma(lpar, sy.dma_start(out=scr["xt"][:, :], in_=x[tt * 128:(tt + 1) * 128, :]))
    p.sec("sp", body)
    dl = p.snap()

    def body(v, dl=dl, src_ready=src_ready):
        p.w(v, "dve", src_ready["dve"])
        p.inc("dve", v.reduce_sum(scr["s1"][:, :], src_sb[:, :], axis=AXL.X))
    p.sec("dve", body)

    def body(a, src_ready=src_ready):
        p.w(a, "dve", src_ready["dve"])
        p.w(a, "act", src_ready["act"])
        p.inc("act", a.activation(scr["sqd"][:, :], src_sb[:, :], AF.Square,
                                  accum_out=scr["s2"][:, :]))
    p.sec("act", body)
    stat = p.snap()

    def body(v, stat=stat, dl=dl, lpar=lpar):
        p.w(v, "act", stat["act"])
        p.w(v, lpar, dl[lpar])
        p.w(v, "dst", dl["dst"])
        p.w(v, "dve")
        p.inc("dve", v.tensor_scalar(scr["negm"][:, :], scr["s1"][:, :], -1.0 / D, None, op0=ALU.mult))
        p.sw(v, "dve")
        p.inc("dve", v.tensor_mul(scr["m2"][:, :], scr["negm"][:, :], scr["negm"][:, :]))
        p.inc("dve", v.tensor_scalar(scr["var"][:, :], scr["s2"][:, :], 1.0 / D, 1e-5,
                                     op0=ALU.mult, op1=ALU.add))
        p.sw(v, "dve")
        p.inc("dve", v.tensor_tensor(scr["var"][:, :], scr["var"][:, :], scr["m2"][:, :], ALU.subtract))
    p.sec("dve", body)

    def body(a):
        p.w(a, "dve")
        p.inc("act", a.activation(scr["var"][:, :], scr["var"][:, :], AF.Sqrt))
    p.sec("act", body)

    def body(v):
        p.w(v, "act")
        p.inc("dve", v.reciprocal(scr["rstd"][:, :], scr["var"][:, :]))
        p.sw(v, "dve")
        p.inc("dve", v.tensor_scalar(scr["xo"][:, :], src_sb[:, :], scr["negm"][:, :],
                                     scr["rstd"][:, :], op0=ALU.add, op1=ALU.mult))
        p.sw(v, "dve")
        p.inc("dve", v.tensor_tensor(scr["xo"][:, :], scr["xo"][:, :], scr["xt"][:, :], ALU.add))
    p.sec("dve", body)
    done = p.snap()

    def body(sy, tt=tt, done=done):
        p.w(sy, "dve", done["dve"])
        p.dma("dst", sy.dma_start(out=xo[tt * 128:(tt + 1) * 128, :], in_=scr["xo"][:, :]))
    p.sec("sp", body)



def emit_preln(p, scr2, x, sb_xn, sb_xnT, sb_idn, ps_t, ntile, per_t):
    for tt in range(ntile):
        lpar = "dxa%d" % (tt % 2)

        def body(sy, tt=tt, lpar=lpar):
            p.w(sy, "dve")
            p.w(sy, "dst")
            p.w(sy, lpar)
            p.dma(lpar, sy.dma_start(out=scr2["xt"][:, :], in_=x[tt * 128:(tt + 1) * 128, :]))
        p.sec("sp", body)
        dl = p.snap()

        def body(v, dl=dl, lpar=lpar):
            p.w(v, lpar, dl[lpar])
            p.inc("dve", v.reduce_sum(scr2["s1"][:, :], scr2["xt"][:, :], axis=AXL.X))
        p.sec("dve", body)

        def body(a, dl=dl, lpar=lpar):
            p.w(a, lpar, dl[lpar])
            p.w(a, "act")
            p.inc("act", a.activation(scr2["sqd"][:, :], scr2["xt"][:, :], AF.Square,
                                      accum_out=scr2["s2"][:, :]))
        p.sec("act", body)
        stat = p.snap()

        def body(v, stat=stat):
            p.w(v, "act", stat["act"])
            p.w(v, "dve")
            p.inc("dve", v.tensor_scalar(scr2["negm"][:, :], scr2["s1"][:, :], -1.0 / D, None, op0=ALU.mult))
            p.sw(v, "dve")
            p.inc("dve", v.tensor_mul(scr2["m2"][:, :], scr2["negm"][:, :], scr2["negm"][:, :]))
            p.inc("dve", v.tensor_scalar(scr2["var"][:, :], scr2["s2"][:, :], 1.0 / D, 1e-5,
                                         op0=ALU.mult, op1=ALU.add))
            p.sw(v, "dve")
            p.inc("dve", v.tensor_tensor(scr2["var"][:, :], scr2["var"][:, :], scr2["m2"][:, :], ALU.subtract))
        p.sec("dve", body)

        def body(a):
            p.w(a, "dve")
            p.inc("act", a.activation(scr2["var"][:, :], scr2["var"][:, :], AF.Sqrt))
        p.sec("act", body)

        def body(v, tt=tt):
            p.w(v, "act")
            p.w(v, "pe")
            p.inc("dve", v.reciprocal(scr2["rstd"][:, :], scr2["var"][:, :]))
            p.sw(v, "dve")
            p.inc("dve", v.tensor_scalar(sb_xn[:, tt * D:(tt + 1) * D], scr2["xt"][:, :],
                                         scr2["negm"][:, :], scr2["rstd"][:, :],
                                         op0=ALU.add, op1=ALU.mult))
        p.sec("dve", body)

    for tt in range(ntile):
        for dc4 in range(0, DC, 4):
            def body(t, tt=tt, dc4=dc4):
                p.w(t, "dve")
                for jj in range(4):
                    dc = dc4 + jj
                    mmi = t.transpose(ps_t[:, jj * 128:(jj + 1) * 128],
                                      sb_xn[:, tt * D + dc * 128: tt * D + (dc + 1) * 128],
                                      sb_idn[:, :])
                p.inc("pe", mmi)
            p.sec("pe", body)
            pl = p.snap()

            def body(v, tt=tt, dc4=dc4, pl=pl):
                p.w(v, "pe", pl["pe"])
                dst = sb_xnT[:, :].rearrange("p (c t) -> p c t", t=per_t)[
                    :, dc4:dc4 + 4, tt * 128:(tt + 1) * 128]
                p.inc("dve", v.tensor_copy(dst, ps_t[:, :].rearrange(
                    "p (c t) -> p c t", t=128)[:, 0:4, :]))
            p.sec("dve", body)
    return p.snap()



def emit_prep(p, io):
    nc = p.nc
    sb_idn = io["sb_idn"]
    sb_onerow = io["sb_onerow"]
    sb_onef = io["sb_onef"]
    p.push()
    sb_in = [p.sb("pp_in0", [128, 8192], F32), p.sb("pp_in1", [128, 8192], F32)]
    sb_q = p.sb("pp_q", [128, 8192], BF16)
    sb_s2 = p.sb("pp_s2", [128, 8192], BF16)
    sb_abs = p.sb("pp_abs", [128, 1], F32)
    sb_acc = p.sb("pp_acc", [128, 2], F32)
    sb_tot = p.sb("pp_tot", [1, 2], F32)
    sb_th = p.sb("pp_th", [128, 2], F32)
    sb_thn = p.sb("pp_thn", [128, 2], F32)
    sb_stage = p.sb("pp_stage", [128, 16384], BF16)
    ps_t = p.ps("pp_pst", [128, 512], BF16)
    ps_f = p.ps("pp_psf", [128, 2], F32)

    def body(g):
        p.inc("pool", g.memset(sb_acc[:, :], 0.0))
    p.sec("pool", body)

    ldi = [0]
    in_act = {}
    for mi, tname in enumerate(["w1s", "w2s"]):
        t = io[tname]
        rows, width = (1024, 2048) if mi == 0 else (256, 8192)
        for i in range(rows // 128):
            idx = ldi[0]
            buf = idx % 2
            lpar = "dxa%d" % buf
            ldi[0] += 1

            def body(sy, t=t, i=i, width=width, lpar=lpar, g_=in_act.get(idx - 2, 0)):
                p.w(sy, "act", g_)
                p.w(sy, lpar)
                p.dma(lpar, sy.dma_start(out=sb_in[buf][:, :width],
                                         in_=t[i * 128:(i + 1) * 128, :]))
            p.sec("sp", body)
            dl = p.snap()

            def body(a, width=width, dl=dl, lpar=lpar, buf=buf):
                p.w(a, lpar, dl[lpar])
                p.w(a, "dve")
                p.inc("act", a.activation(sb_q[:, :width], sb_in[buf][:, :width],
                                          AF.Abs, accum_out=sb_abs[:, :]))
            p.sec("act", body)
            in_act[idx] = p.c["act"]

            def body(v, mi=mi):
                p.w(v, "act")
                p.w(v, "pool")
                p.inc("dve", v.tensor_tensor(sb_acc[:, mi:mi + 1], sb_acc[:, mi:mi + 1],
                                             sb_abs[:, :], ALU.add))
            p.sec("dve", body)

    def body(t):
        p.w(t, "dve")
        p.w(t, "pool")
        p.inc("pe", t.matmul(ps_f[0:1, :], sb_onef[:, :], sb_acc[:, :],
                             start=True, stop=True))
    p.sec("pe", body)

    def body(v):
        p.w(v, "pe")
        p.inc("dve", v.tensor_copy(sb_tot[:, :], ps_f[0:1, :]))
    p.sec("dve", body)

    p.sem("ccth")

    def body(g):
        p.w(g, "dve")
        p.dma("dg", g.dma_start(out=io["arth_in"].ap(), in_=sb_tot[:, :]))
        p.w(g, "dg")
        emit_cc(p, g, io["arth_in"], io["arth_out"], "ccth",
                kind="AllReduce", alu=ALU.add, simcc=io.get("simcc", False))
        p.w(g, "ccth")
        p.dma("dg", g.dma_start(out=sb_tot[:, :], in_=io["arth_out"].ap()))
    p.sec("pool", body)
    dl_ar = p.snap()

    jobs = [
        ("up0s", "agu0", 1024, 2048, None),
        ("dn0s", "agd0", 256, 8192, None),
        ("w1s", "agw1", 1024, 2048, 0),
        ("w2s", "agw2", 256, 8192, 1),
    ]
    ag_emitters = []
    prev_ev_box = [None]

    def emit_job(jobi, thw):
        tname, agname, rows, width, thi = jobs[jobi]
        prev_ev = prev_ev_box[0]
        t = io[tname]
        agin = io[agname + "_in"]
        agout = io[agname + "_out"]
        ntile = rows // 128
        nj = width // 128
        evsem = "dga%d" % jobi
        p.sem(evsem)
        for i in range(ntile):
            idx = ldi[0]
            buf = idx % 2
            lpar = "dxa%d" % buf
            ldi[0] += 1

            def body(sy, t=t, i=i, width=width, lpar=lpar, buf=buf,
                     g_=in_act.get(idx - 2, 0)):
                p.w(sy, "act", g_)
                p.w(sy, lpar)
                p.dma(lpar, sy.dma_start(out=sb_in[buf][:, :width],
                                         in_=t[i * 128:(i + 1) * 128, :]))
            p.sec("sp", body)
            dl = p.snap()

            if thi is None:
                def body(a, width=width, dl=dl, lpar=lpar, buf=buf, tg=p.c["pe"]):
                    p.w(a, lpar, dl[lpar])
                    p.w(a, "pe", tg)
                    p.w(a, "dve", thw)
                    p.inc("act", a.activation(sb_q[:, :width], sb_in[buf][:, :width], AF.Sign))
                p.sec("act", body)
                qready, qsem = p.c["act"], "act"
            else:
                def body(a, width=width, dl=dl, thi=thi, lpar=lpar, buf=buf, tg=p.c["pe"]):
                    p.w(a, lpar, dl[lpar])
                    p.w(a, "pe", tg)
                    p.w(a, "dve", thw)
                    p.inc("act", a.activation(sb_q[:, :width], sb_in[buf][:, :width], AF.Sign,
                                              bias=sb_thn[:, thi:thi + 1]))
                    p.inc("act", a.activation(sb_s2[:, :width], sb_in[buf][:, :width], AF.Sign,
                                              bias=sb_th[:, thi:thi + 1]))
                p.sec("act", body)
                al = p.snap()

                def body(v, width=width, al=al):
                    p.w(v, "act", al["act"])
                    p.inc("dve", v.tensor_tensor(sb_q[:, :width], sb_q[:, :width],
                                                 sb_s2[:, :width], ALU.add))
                    p.sw(v, "dve")
                    p.inc("dve", v.tensor_scalar(sb_q[:, :width], sb_q[:, :width],
                                                 0.5, None, op0=ALU.mult))
                p.sec("dve", body)
                qready, qsem = p.c["dve"], "dve"
            in_act[idx] = p.c["act"]

            for j4 in range(0, nj, 4):
                def body(t_, j4=j4, qready=qready, qsem=qsem):
                    p.w(t_, "ds", io["boot_ds"])
                    p.w(t_, qsem, qready)
                    p.w(t_, "dve")
                    for jj in range(4):
                        j = j4 + jj
                        mmi = t_.transpose(ps_t[:, jj * 128:(jj + 1) * 128],
                                           sb_q[:, j * 128:(j + 1) * 128],
                                           sb_idn[:, :])
                    p.inc("pe", mmi)
                p.sec("pe", body)
                pl = p.snap()

                def body(v, j4=j4, i=i, rows=rows, pl=pl, prev_ev=prev_ev):
                    p.w(v, "pe", pl["pe"])
                    if prev_ev is not None:
                        p.w(v, prev_ev[0], prev_ev[1])
                    dst = sb_stage[:, :].rearrange("p (j r) -> p j r", r=rows)[
                        :, j4:j4 + 4, i * 128:(i + 1) * 128]
                    p.inc("dve", v.tensor_copy(dst, ps_t[:, :].rearrange(
                        "p (j r) -> p j r", r=128)[:, 0:4, :]))
                p.sec("dve", body)

        st = p.snap()
        ccname = "cc" + agname
        p.sem(ccname)

        def body(sy, agin=agin, ntile=ntile, rows=rows, st=st, evsem=evsem):
            p.w(sy, "dve", st["dve"])
            for i in range(ntile):
                p.dma(evsem, sy.dma_start(
                    out=agin[i * 128:(i + 1) * 128, :].rearrange("p (j r) -> p j r", r=128),
                    in_=sb_stage[:, :].rearrange("p (j r) -> p j r", r=rows)[:, :, i * 128:(i + 1) * 128]))
        p.sec("sp", body)
        stg = p.snap()

        def emit_ag(agin=agin, agout=agout, ccname=ccname, stg=stg, evsem=evsem):
            def body(g):
                p.w(g, evsem, stg[evsem])
                emit_cc(p, g, agin, agout, ccname, simcc=io.get("simcc", False))
            p.sec("pool", body)
        ag_emitters.append(emit_ag)
        prev_ev_box[0] = (evsem, p.c[evsem])

    emit_job(0, 0)
    emit_job(1, 0)

    def body(a, dl_ar=dl_ar):
        p.w(a, "dg", dl_ar["dg"])
        p.inc("act", a.mul(sb_tot[:, :], sb_tot[:, :], 0.7 / (8192.0 * 2048.0)))
    p.sec("act", body)
    mull = p.snap()

    def body(t, mull=mull):
        p.w(t, "act", mull["act"])
        p.inc("pe", t.matmul(ps_f[:, :], sb_onerow[:, :], sb_tot[0:1, :],
                             start=True, stop=True))
    p.sec("pe", body)

    def body(v):
        p.w(v, "pe")
        p.inc("dve", v.tensor_copy(sb_th[:, :], ps_f[:, :]))
        p.sw(v, "dve")
        p.inc("dve", v.tensor_scalar(sb_thn[:, :], sb_th[:, :], -1.0, None, op0=ALU.mult))
    p.sec("dve", body)
    th_ready = p.snap()

    emit_job(2, th_ready["dve"])
    emit_job(3, th_ready["dve"])
    p.pop()
    return ag_emitters



def emit_l0a(p, io):
    nc = p.nc
    sb_idn = io["sb_idn"]
    sb_onerow = io["sb_onerow"]
    sb_oneb = io["sb_oneb"]
    sb_qn = io["sb_qn"]
    x = io["x"]
    wqkv = io["wqkv0"]

    phase_fence(p)
    p.push()
    sb_xn = p.sb("a0_xn", [128, 8 * D], BF16)
    sb_xnT = p.sb("a0_xnT", [128, DC * TC], BF16)
    sb_kn = p.sb("a0_kn", [128, NH * TC], BF16)
    sb_vf = sb_xn
    sb_w = [p.sb("a0_w0", [128, D], BF16), p.sb("a0_w1", [128, D], BF16)]
    sb_qwsb = p.sb("a0_qw", [128, DC], F32)
    sb_kwsb = p.sb("a0_kw", [128, DC], F32)
    sb_cos = p.sb("a0_cos", [128, TC], BF16)
    sb_sin = p.sb("a0_sin", [128, TC], BF16)
    sb_rp = p.sb("a0_rp", [128, TC], BF16)
    sb_qsq = [p.sb("a0_qsq0", [128, TC], BF16), p.sb("a0_qsq1", [128, TC], BF16)]
    sb_sqrow = p.sb("a0_sqrow", [1, TC], F32)
    sb_vstage = p.sb("a0_vst", [128, D], BF16)
    scr2 = {k: p.sb("a0scr_" + k, [128, 1] if k not in ("sqd", "xt") else [128, D],
                    F32 if k != "sqd" else BF16)
            for k in ("s1", "s2", "negm", "m2", "var", "rstd", "sqd", "xt")}
    ps_a = [p.ps("a0_psa0", [128, 1024], F32), p.ps("a0_psa1", [128, 1024], F32)]
    ps_t = p.ps("a0_pst", [128, 512], BF16)
    ps_rb = p.ps("a0_psrb", [128, TC], F32)

    p.sem("dc"); p.sem("dw0"); p.sem("dw1")

    def body(sy):
        p.dma("ds", sy.dma_start(out=sb_qwsb[:, :], in_=io["qw"].ap()))
        p.dma("ds", sy.dma_start(out=sb_kwsb[:, :], in_=io["kw"].ap()))
    p.sec("sp", body)

    def body(g):
        p.dma("dc", g.dma_start(out=sb_cos[:, :], in_=io["cosr"].ap()))
        p.dma("dc", g.dma_start(out=sb_sin[:, :], in_=io["sinr"].ap()))
    p.sec("pool", body)
    consts = p.snap()

    xnT_done = emit_preln(p, scr2, x, sb_xn, sb_xnT, sb_idn, ps_t, 8, TC)

    p.sem("cckn"); p.sem("ccvt"); p.sem("dkn"); p.sem("dvt")
    evac_dve = {}
    mm_qkv = {}
    eidx = [0]

    def emit_qkv_group(jcs):
        for jc in jcs:
            idx = eidx[0]
            eidx[0] += 1
            par = idx % 2

            def body(g, jc=jc, par=par, idx=idx):
                p.w(g, "pe", mm_qkv.get(idx - 2, 0))
                p.dma("dw%d" % par, g.dma_start(
                    out=sb_w[par][:, :],
                    in_=wqkv[jc * 128:(jc + 1) * 128, :]))
            p.sec("pool", body)
            wl = p.snap()

            def body(t, par=par, wl=wl, idx=idx):
                p.w(t, "dw%d" % par, wl["dw%d" % par])
                p.w(t, "dve", max(xnT_done["dve"], evac_dve.get(idx - 2, 0)))
                for half in range(2):
                    for dc in range(DC):
                        lhs = sb_w[par][:, dc * 128:(dc + 1) * 128]
                        mmi = t.matmul(ps_a[par][:, half * 512:(half + 1) * 512], lhs,
                                       sb_xnT[:, dc * TC + half * 512: dc * TC + (half + 1) * 512],
                                       start=(dc == 0), stop=(dc == DC - 1))
                mm_qkv[idx] = p.inc("pe", mmi)
            p.sec("pe", body)
            ml = p.snap()

            def body(v, jc=jc, par=par, ml=ml, idx=idx):
                p.w(v, "pe", ml["pe"])
                if jc < DC:
                    dst = sb_qn[:, jc * TC:(jc + 1) * TC]
                elif jc < 2 * DC:
                    dst = sb_kn[:, (jc - DC) * TC:(jc - DC + 1) * TC]
                else:
                    dst = sb_vf[:, (jc - 2 * DC) * TC:(jc - 2 * DC + 1) * TC]
                evac_dve[idx] = p.inc("dve", v.tensor_copy(dst, ps_a[par][:, :]))
            p.sec("dve", body)

    def emit_rmsnorm(src, wsb, is_q):
        for h in range(NH):
            par = h % 2

            def body(a, h=h, par=par, src=src):
                p.w(a, "dve", p.c["dve"])
                if h >= 2:
                    p.w(a, "pe")
                p.inc("act", a.activation(sb_qsq[par][:, :], src[:, h * TC:(h + 1) * TC],
                                          AF.Square))
            p.sec("act", body)
            al = p.snap()

            def body(t, h=h, par=par, al=al, dve_now=p.c["dve"]):
                p.w(t, "act", al["act"])
                if h == 0:
                    p.w(t, "dve", dve_now)
                t.matmul(ps_rb[0:1, 0:512], sb_oneb[:, :], sb_qsq[par][:, 0:512],
                         start=(h == 0), stop=(h == NH - 1))
                p.inc("pe", t.matmul(ps_rb[0:1, 512:1024], sb_oneb[:, :],
                                     sb_qsq[par][:, 512:1024],
                                     start=(h == 0), stop=(h == NH - 1)))
            p.sec("pe", body)
        sq = p.snap()

        def body(v, sq=sq):
            p.w(v, "pe", sq["pe"])
            p.inc("dve", v.tensor_scalar(sb_sqrow[:, :], ps_rb[0:1, :], 1.0 / D, 1e-6,
                                         op0=ALU.mult, op1=ALU.add))
        p.sec("dve", body)

        def body(a):
            p.w(a, "dve")
            p.inc("act", a.activation(sb_sqrow[:, :], sb_sqrow[:, :], AF.Sqrt))
        p.sec("act", body)

        def body(v, is_q=is_q):
            p.w(v, "act")
            p.inc("dve", v.reciprocal(sb_sqrow[:, :], sb_sqrow[:, :]))
            if is_q:
                p.sw(v, "dve")
                p.inc("dve", v.tensor_scalar(sb_sqrow[:, :], sb_sqrow[:, :], SM_SCALE,
                                             None, op0=ALU.mult))
        p.sec("dve", body)
        rr = p.snap()

        def body(t, rr=rr):
            p.w(t, "dve", rr["dve"])
            t.matmul(ps_rb[:, 0:512], sb_onerow[:, :], sb_sqrow[0:1, 0:512],
                     start=True, stop=True)
            p.inc("pe", t.matmul(ps_rb[:, 512:1024], sb_onerow[:, :],
                                 sb_sqrow[0:1, 512:1024], start=True, stop=True))
        p.sec("pe", body)
        bb = p.snap()

        for h in range(NH):
            def body(v, h=h, src=src, wsb=wsb, bb=bb):
                p.w(v, "pe", bb["pe"])
                p.w(v, "ds", consts["ds"])
                p.inc("dve", v.scalar_tensor_tensor(
                    src[:, h * TC:(h + 1) * TC], src[:, h * TC:(h + 1) * TC],
                    wsb[:, h:h + 1], ps_rb[:, :],
                    op0=ALU.mult, op1=ALU.mult))
            p.sec("dve", body)

    def emit_rope(src):
        for h in range(NH):
            x1 = src[0:64, h * TC:(h + 1) * TC]
            x2 = src[64:128, h * TC:(h + 1) * TC]

            def body(g, x1=x1, x2=x2):
                p.w(g, "dc", consts["dc"])
                p.w(g, "dve")
                p.inc("pool", g.tensor_tensor(sb_rp[0:64, :], x2, sb_sin[64:128, :], ALU.mult))
                p.inc("pool", g.tensor_tensor(sb_rp[64:128, :], x1, sb_sin[0:64, :], ALU.mult))
            p.sec("pool", body)
            gl = p.snap()

            def body(v, x1=x1, x2=x2, gl=gl):
                p.w(v, "pool", gl["pool"])
                p.inc("dve", v.tensor_tensor(x1, x1, sb_cos[0:64, :], ALU.mult))
                p.inc("dve", v.tensor_tensor(x2, x2, sb_cos[64:128, :], ALU.mult))
                p.sw(v, "dve")
                p.inc("dve", v.tensor_tensor(x1, x1, sb_rp[0:64, :], ALU.subtract))
                p.inc("dve", v.tensor_tensor(x2, x2, sb_rp[64:128, :], ALU.add))
            p.sec("dve", body)

    emit_qkv_group(range(DC, 2 * DC))
    emit_rmsnorm(sb_kn, sb_kwsb, False)
    emit_rope(sb_kn)
    krope_done = p.snap()

    def body(sy, kd=krope_done):
        p.w(sy, "dve", kd["dve"])
        p.dma("dkn", sy.dma_start(out=io["agkn_in"].ap(), in_=sb_kn[:, :]))
    p.sec("sp", body)
    knst = p.snap()

    def body(g, knst=knst):
        p.w(g, "dkn", knst["dkn"])
        emit_cc(p, g, io["agkn_in"], io["agkn_out"], "cckn", rg=RGB,
                simcc=io.get("simcc", False))
    p.sec("pool", body)

    emit_qkv_group(range(2 * DC, 3 * DC))

    for tt in range(8):
        for dc4 in range(0, DC, 4):
            def body(t, tt=tt, dc4=dc4, dnow=p.c["dve"]):
                p.w(t, "dve", max(dnow, p.c["dve"]))
                for jj in range(4):
                    dc = dc4 + jj
                    mmi = t.transpose(ps_t[:, jj * 128:(jj + 1) * 128],
                                      sb_vf[:, dc * TC + tt * 128: dc * TC + (tt + 1) * 128],
                                      sb_idn[:, :])
                p.inc("pe", mmi)
            p.sec("pe", body)
            pl = p.snap()

            def body(v, dc4=dc4, pl=pl, dvt_now=p.c["dvt"]):
                p.w(v, "pe", pl["pe"])
                p.w(v, "dvt", dvt_now)
                p.inc("dve", v.tensor_copy(sb_vstage[:, dc4 * 128:(dc4 + 4) * 128], ps_t[:, :]))
            p.sec("dve", body)
        vl = p.snap()

        def body(sy, tt=tt, vl=vl):
            p.w(sy, "dve", vl["dve"])
            p.dma("dvt", sy.dma_start(out=io["agvt_in"][tt * 128:(tt + 1) * 128, :],
                                      in_=sb_vstage[:, :]))
        p.sec("sp", body)
    vst = p.snap()

    def body(g, vst=vst):
        p.w(g, "dvt", vst["dvt"])
        emit_cc(p, g, io["agvt_in"], io["agvt_out"], "ccvt", rg=RGB,
                simcc=io.get("simcc", False))
    p.sec("pool", body)

    emit_qkv_group(range(0, DC))
    emit_rmsnorm(sb_qn, sb_qwsb, True)
    emit_rope(sb_qn)
    p.pop()



def emit_l0b(p, io):
    nc = p.nc
    sb_idn = io["sb_idn"]
    sb_qn = io["sb_qn"]
    x = io["x"]
    xo = io["x1"]
    agkn = io["agkn_out"]
    agvt = io["agvt_out"]
    simcc = io.get("simcc", False)

    phase_fence(p)
    p.push()
    scr = alloc_ln_scr(p, "b0")
    sb_ctxT = p.sb("b0_ctxT", [128, NH * TC], BF16)
    sb_kh = [p.sb("b0_kh0", [128, 2048], BF16), p.sb("b0_kh1", [128, 2048], BF16)]
    sb_vh = [p.sb("b0_vh0", [128, 2048], BF16), p.sb("b0_vh1", [128, 2048], BF16)]
    sb_msk = p.sb("b0_msk", [128, 8 * 1152], BF16)
    sb_probs = [p.sb("b0_probs0", [128, 1152], BF16), p.sb("b0_probs1", [128, 1152], BF16)]
    sb_probsT = p.sb("b0_probsT", [128, 1152], BF16)
    sb_mx = p.sb("b0_mx", [128, 1], F32)
    sb_negmx = p.sb("b0_negmx", [128, 1], F32)
    sb_l = p.sb("b0_l", [128, 1], F32)
    sb_rl = p.sb("b0_rl", [128, 1], F32)
    sb_wo = p.sb("b0_wo", [128, DC * D], BF16)
    sb_at = p.sb("b0_at", [128, D], F32)
    ps_s = [p.ps("b0_pss0", [128, 1152], F32), p.ps("b0_pss1", [128, 1152], F32)]
    ps_pt = p.ps("b0_pspt", [128, 640], BF16)
    ps_c = p.ps("b0_psc", [128, 128], F32)

    p.sem("dmk"); p.sem("dwo"); p.sem("dk0"); p.sem("dk1"); p.sem("dsg")

    def body(g):
        p.dma("dmk", g.dma_start(out=sb_msk[:, :], in_=io["amask"].ap()))
        p.dma("dwo", g.dma_start(out=sb_wo[:, :].rearrange("p (c j) -> p c j", j=D),
                                 in_=io["wo"].ap().rearrange("(c p) j -> p c j", p=128)))
    p.sec("pool", body)
    consts = p.snap()

    def body(g):
        p.w(g, "cckn", p.c["cckn"])
        p.w(g, "ccvt", p.c["ccvt"])
        dy = io["dy"]
        p.dma("dsg", g.dma_start(out=io["stg_k0"][:, :],
                                 in_=agkn[bass.ds(dy["o128l"], 128), :]))
        p.dma("dsg", g.dma_start(out=io["stg_v0"][:, :],
                                 in_=agvt[bass.ds(dy["oTl"], TC), :]))
    p.sec("pool", body)
    stg = p.snap()

    exp2_act = {"prev": p.c["act"]}
    norm_dve0 = {}
    ptcopy_act = {"a": 0, "v": 0}
    last_ctx_dve = 0
    pv_pe = {}

    for h in range(NH):
        par = h % 2

        def body(g, h=h, par=par):
            p.w(g, "pe", pv_pe.get(h - 2, 0))
            p.w(g, "dsg", stg["dsg"])
            p.dma("dk%d" % par, g.dma_start(
                out=sb_kh[par][:, 0:1024],
                in_=io["stg_k0"][:, h * TC:(h + 1) * TC]))
            p.dma("dk%d" % par, g.dma_start(
                out=sb_kh[par][:, 1024:2048],
                in_=io["agkn_in"][:, h * TC:(h + 1) * TC]))
            p.dma("dk%d" % par, g.dma_start(
                out=sb_vh[par][:, 0:1024].rearrange("p (kt j) -> p kt j", j=128),
                in_=io["stg_v0"][:, h * 128:(h + 1) * 128]
                    .rearrange("(kt p) j -> p kt j", p=128)))
            p.dma("dk%d" % par, g.dma_start(
                out=sb_vh[par][:, 1024:2048].rearrange("p (kt j) -> p kt j", j=128),
                in_=io["agvt_in"][:, h * 128:(h + 1) * 128]
                    .rearrange("(kt p) j -> p kt j", p=128)))
        p.sec("pool", body)
        kl = p.c["dk%d" % par]

        for step in range(9):
            if step < 8:
                qt = step
                spar = qt % 2

                def body(t, h=h, qt=qt, par=par, kl=kl, spar=spar,
                         ea=exp2_act.get((h, qt - 2), exp2_act.get("prev", 0))):
                    p.w(t, "dk%d" % par, kl)
                    p.w(t, "act", ea)
                    p.w(t, "dmk", consts["dmk"])
                    for w0, w1 in [(0, 512), (512, 1024), (1024, 1152)]:
                        t.matmul(ps_s[spar][:, w0:w1], sb_idn[:, :],
                                 sb_msk[:, qt * 1152 + w0: qt * 1152 + w1],
                                 start=True, stop=False)
                    lhs = sb_qn[:, h * TC + qt * 128: h * TC + (qt + 1) * 128]
                    for w0, w1 in [(0, 512), (512, 1024), (1024, 1152)]:
                        mmi = t.matmul(ps_s[spar][:, w0:w1], lhs,
                                       sb_kh[par][:, qt * 128 + w0: qt * 128 + w1],
                                       start=False, stop=True)
                    p.inc("pe", mmi)
                p.sec("pe", body)
                sc = p.snap()

                def body(v, qt=qt, spar=spar, sc=sc):
                    p.w(v, "pe", sc["pe"])
                    p.w(v, "dve")
                    p.inc("dve", v.reduce_max(sb_mx[:, :], ps_s[spar][:, :], axis=AXL.X))
                    p.sw(v, "dve")
                    p.inc("dve", v.tensor_scalar(sb_negmx[:, :], sb_mx[:, :], -1.0,
                                                 None, op0=ALU.mult))
                p.sec("dve", body)
                mx_ = p.snap()

                def body(a, spar=spar, mx_=mx_):
                    p.w(a, "dve", mx_["dve"])
                    p.w(a, "act")
                    p.inc("act", a.activation(sb_probs[spar][:, :], ps_s[spar][:, :],
                                              AF.Exp, bias=sb_negmx[:, :],
                                              accum_out=sb_l[:, :]))
                p.sec("act", body)
                exp2_act[(h, qt)] = p.c["act"]
                e1 = p.snap()

                def body(v, spar=spar, e1=e1):
                    p.w(v, "act", e1["act"])
                    p.inc("dve", v.reciprocal(sb_rl[:, :], sb_l[:, :]))
                    p.sw(v, "dve")
                    p.inc("dve", v.tensor_scalar(sb_probs[spar][:, :], sb_probs[spar][:, :],
                                                 sb_rl[:, :], None, op0=ALU.mult))
                p.sec("dve", body)
                norm_dve0[(h, qt)] = p.c["dve"]

            if step >= 1:
                q = step - 1
                spar2 = q % 2
                for half, (t0, t1) in enumerate([(0, 5), (5, 9)]):
                    def body(t, q=q, spar2=spar2, t0=t0, t1=t1,
                             nd=norm_dve0[(h, q)], pc=ptcopy_act):
                        p.w(t, "dve", max(nd, pc["v"]))
                        for j in range(t0, t1):
                            mmi = t.transpose(ps_pt[:, (j - t0) * 128:(j - t0 + 1) * 128],
                                              sb_probs[spar2][:, j * 128:(j + 1) * 128],
                                              sb_idn[:, :])
                        p.inc("pe", mmi)
                    p.sec("pe", body)
                    tp = p.snap()

                    def body(v, t0=t0, t1=t1, tp=tp):
                        p.w(v, "pe", tp["pe"])
                        p.inc("dve", v.tensor_copy(
                            sb_probsT[:, t0 * 128:t1 * 128],
                            ps_pt[:, 0:(t1 - t0) * 128]))
                    p.sec("dve", body)
                    ptcopy_act["v"] = p.c["dve"]
                pt_done = p.snap()

                def body(t, q=q, par=par, kl=kl, pt_done=pt_done, lc=last_ctx_dve):
                    p.w(t, "dve", max(lc, pt_done["dve"]))
                    p.w(t, "dk%d" % par, kl)
                    for j in range(9):
                        mmi = t.matmul(ps_c[:, :],
                                       sb_vh[par][:, (q + j) * 128:(q + j + 1) * 128],
                                       sb_probsT[:, j * 128:(j + 1) * 128],
                                       start=(j == 0), stop=(j == 8))
                    p.inc("pe", mmi)
                p.sec("pe", body)
                pv = p.snap()

                def body(v, h=h, q=q, pv=pv):
                    p.w(v, "pe", pv["pe"])
                    p.inc("dve", v.tensor_copy(
                        sb_ctxT[:, h * TC + q * 128: h * TC + (q + 1) * 128], ps_c[:, :]))
                p.sec("dve", body)
                last_ctx_dve = p.c["dve"]
        pv_pe[h] = p.c["pe"]
        exp2_act["prev"] = exp2_act[(h, 7)]
    attn_done = p.snap()

    for tt in range(8):
        for half in range(2):
            def body(t, tt=tt, half=half, ad=attn_done):
                p.w(t, "dve", ad["dve"])
                p.w(t, "act", ad["act"])
                p.w(t, "dwo", consts["dwo"])
                p.w(t, "dve")
                for js in range(2):
                    for dc in range(DC):
                        lhs = sb_ctxT[:, dc * TC + tt * 128: dc * TC + (tt + 1) * 128]
                        mmi = t.matmul(
                            ps_s[0][:, js * 512:(js + 1) * 512], lhs,
                            sb_wo[:, dc * D + half * 1024 + js * 512: dc * D + half * 1024 + (js + 1) * 512],
                            start=(dc == 0), stop=(dc == DC - 1))
                p.inc("pe", mmi)
            p.sec("pe", body)
            wl = p.snap()

            def body(v, half=half, wl=wl):
                p.w(v, "pe", wl["pe"])
                p.w(v, "act")
                p.inc("dve", v.tensor_copy(sb_at[:, half * 1024:(half + 1) * 1024],
                                           ps_s[0][:, 0:1024]))
            p.sec("dve", body)
        src_ready = p.snap()
        emit_ln_residual(p, scr, sb_at, x, xo, tt, src_ready)
    p.pop()



def emit_ffn(p, io, layer):
    nc = p.nc
    sb_idn = io["sb_idn"]
    sb_onerow = io["sb_onerow"]
    x = io["x1"] if layer == 0 else io["x3"]
    xo = io["x2"] if layer == 0 else io["out"]
    wupT = io["agu0_out"] if layer == 0 else io["agw1_out"]
    wdnT = io["agd0_out"] if layer == 0 else io["agw2_out"]
    ccu = "ccagu0" if layer == 0 else "ccagw1"
    ccd = "ccagd0" if layer == 0 else "ccagw2"

    phase_fence(p)
    p.push()
    pre = "f%d_" % layer
    scr = alloc_ln_scr(p, pre)
    sb_xn = p.sb(pre + "xn", [128, 4 * D], BF16)
    sb_xnT = p.sb(pre + "xnT", [128, DC * 512], BF16)
    sb_hid = p.sb(pre + "hid", [128, HC * 512], BF16)
    sb_wu = [p.sb(pre + "wu%d" % i, [128, D], BF16) for i in range(4)]
    sb_wd = [p.sb(pre + "wd0", [128, 8192], BF16), p.sb(pre + "wd1", [128, 8192], BF16)]
    sb_dn = p.sb(pre + "dn", [128, DC * 512], BF16)
    sb_at = p.sb(pre + "at", [128, D], BF16)
    sb_sc1 = p.sb(pre + "sc1", [128, HC], F32)
    sb_sc2 = p.sb(pre + "sc2", [128, DC], F32)
    sb_osc = p.sb(pre + "osc", [1, 1], F32)
    scr2 = {k: p.sb(pre + "scr_" + k, [128, 1] if k not in ("sqd", "xt") else [128, D],
                    F32 if k != "sqd" else BF16)
            for k in ("s1", "s2", "negm", "m2", "var", "rstd", "sqd", "xt")}
    ps_u = [p.ps(pre + "psu0", [128, 512], F32), p.ps(pre + "psu1", [128, 512], F32)]
    ps_d = [p.ps(pre + "psd0", [128, 512], F32), p.ps(pre + "psd1", [128, 512], F32)]
    ps_t = p.ps(pre + "pst", [128, 512], BF16)
    ps_b = p.ps(pre + "psb", [128, 2], F32)
    for k in ("du0", "du1", "du2", "du3", "dd0", "dd1"):
        p.sem(k)

    def body(sy):
        if layer == 0:
            p.dma("ds", sy.dma_start(out=sb_sc1[:, :], in_=io["upsc"].ap()))
            p.dma("ds", sy.dma_start(out=sb_sc2[:, :], in_=io["dsc"].ap()))
            p.dma("ds", sy.dma_start(out=sb_osc[:, :], in_=io["osc"].ap()))
        else:
            p.dma("ds", sy.dma_start(out=sb_sc1[:, :], in_=io["b1"].ap()))
            p.dma("ds", sy.dma_start(out=sb_sc2[:, :], in_=io["b2"].ap()))
    p.sec("sp", body)
    consts = p.snap()

    if layer == 0:
        def body(t):
            p.w(t, "ds", consts["ds"])
            p.w(t, "pool", 1)
            p.inc("pe", t.matmul(ps_b[:, 0:1], sb_onerow[:, :], sb_osc[:, :],
                                 start=True, stop=True))
        p.sec("pe", body)

        def body(v):
            p.w(v, "pe")
            p.inc("dve", v.scalar_tensor_tensor(sb_sc2[:, :], sb_sc2[:, :], 1.0,
                                                ps_b[:, 0:1].to_broadcast([128, DC]),
                                                op0=ALU.mult, op1=ALU.mult))
        p.sec("dve", body)
    sc_ready = p.snap()

    mm_up = {}
    mm_dn = {}
    for h2 in range(2):
        xnT_done = emit_preln(p, scr2, x[h2 * 512: h2 * 512 + 512, :],
                              sb_xn, sb_xnT, sb_idn, ps_t, 4, 512)

        evac_act = {}
        for hc in range(HC):
            g_ = h2 * HC + hc
            par = hc % 2
            buf = g_ % 4
            rnk, hcl = hc // 8, hc % 8

            def body(g, buf=buf, rnk=rnk, hcl=hcl, g_=g_):
                p.w(g, "pe", mm_up.get(g_ - 4, 0))
                p.w(g, ccu, p.c[ccu])
                p.dma("du%d" % buf, g.dma_start(
                    out=sb_wu[buf][:, :],
                    in_=wupT[rnk * 1024 + hcl * 128: rnk * 1024 + (hcl + 1) * 128, :]))
            p.sec("pool", body)
            wl = p.snap()

            def body(t, buf=buf, par=par, wl=wl, xd=xnT_done,
                     ea=evac_act.get(hc - 2, 0), g_=g_):
                p.w(t, "du%d" % buf, wl["du%d" % buf])
                p.w(t, "dve", xd["dve"])
                p.w(t, "act", max(ea, sc_ready["act"]))
                for dc in range(DC):
                    mmi = t.matmul(ps_u[par][:, :], sb_wu[buf][:, dc * 128:(dc + 1) * 128],
                                   sb_xnT[:, dc * 512:(dc + 1) * 512],
                                   start=(dc == 0), stop=(dc == DC - 1))
                mm_up[g_] = p.inc("pe", mmi)
            p.sec("pe", body)
            ml = p.snap()

            def body(a, hc=hc, par=par, ml=ml):
                p.w(a, "pe", ml["pe"])
                p.w(a, "dve", sc_ready["dve"])
                p.w(a, "ds", consts["ds"])
                if layer == 0:
                    p.inc("act", a.activation(sb_hid[:, hc * 512:(hc + 1) * 512],
                                              ps_u[par][:, :], AF.Relu,
                                              scale=sb_sc1[:, hc:hc + 1]))
                else:
                    p.inc("act", a.activation(sb_hid[:, hc * 512:(hc + 1) * 512],
                                              ps_u[par][:, :], AF.Gelu,
                                              bias=sb_sc1[:, hc:hc + 1]))
            p.sec("act", body)
            evac_act[hc] = p.c["act"]
        hid_done = p.snap()

        evac_dve = {}
        for jc in range(DC):
            g_ = h2 * DC + jc
            par = jc % 2
            rnk, jcl = jc // 2, jc % 2

            def body(g, par=par, rnk=rnk, jcl=jcl, g_=g_):
                p.w(g, "pe", mm_dn.get(g_ - 2, 0))
                p.w(g, ccd, p.c[ccd])
                p.dma("dd%d" % par, g.dma_start(
                    out=sb_wd[par][:, :],
                    in_=wdnT[rnk * 256 + jcl * 128: rnk * 256 + (jcl + 1) * 128, :]))
            p.sec("pool", body)
            wl = p.snap()

            def body(t, par=par, wl=wl, hd=hid_done, ed=evac_dve.get(jc - 2, 0), g_=g_):
                p.w(t, "dd%d" % par, wl["dd%d" % par])
                p.w(t, "act", hd["act"])
                p.w(t, "dve", ed)
                for hc in range(HC):
                    mmi = t.matmul(ps_d[par][:, :], sb_wd[par][:, hc * 128:(hc + 1) * 128],
                                   sb_hid[:, hc * 512:(hc + 1) * 512],
                                   start=(hc == 0), stop=(hc == HC - 1))
                mm_dn[g_] = p.inc("pe", mmi)
            p.sec("pe", body)
            ml = p.snap()

            def body(v, jc=jc, par=par, ml=ml):
                p.w(v, "pe", ml["pe"])
                p.w(v, "ds", consts["ds"])
                op = ALU.mult if layer == 0 else ALU.add
                evac_dve[jc] = p.inc("dve", v.tensor_scalar(
                    sb_dn[:, jc * 512:(jc + 1) * 512], ps_d[par][:, :],
                    sb_sc2[:, jc:jc + 1], None, op0=op))
            p.sec("dve", body)
        dn_done = p.snap()

        for tl in range(4):
            tt = h2 * 4 + tl
            for jc4 in range(0, DC, 4):
                def body(t, tl=tl, jc4=jc4, dd=dn_done):
                    p.w(t, "dve", max(dd["dve"], p.c["dve"]))
                    for jj in range(4):
                        jc = jc4 + jj
                        mmi = t.transpose(ps_t[:, jj * 128:(jj + 1) * 128],
                                          sb_dn[:, jc * 512 + tl * 128: jc * 512 + (tl + 1) * 128],
                                          sb_idn[:, :])
                    p.inc("pe", mmi)
                p.sec("pe", body)
                pl = p.snap()

                def body(v, jc4=jc4, pl=pl):
                    p.w(v, "pe", pl["pe"])
                    p.w(v, "act")
                    p.inc("dve", v.tensor_copy(sb_at[:, jc4 * 128:(jc4 + 4) * 128], ps_t[:, :]))
                p.sec("dve", body)
            src_ready = p.snap()
            emit_ln_residual(p, scr, sb_at, x, xo, tt, src_ready)
    p.pop()



def emit_l1a(p, io):
    nc = p.nc
    sb_idn = io["sb_idn"]
    sb_oneb = io["sb_oneb"]
    sb_qf = io["sb_qf"]
    sb_qt = io["sb_qt"]
    x = io["x2"]
    wqkv = io["wqkv1"]

    phase_fence(p)
    p.push()
    sb_xn = p.sb("a1_xn", [128, 8 * D], BF16)
    sb_xnT = p.sb("a1_xnT", [128, DC * TC], BF16)
    sb_kn = p.sb("a1_kn", [128, NH * TC], BF16)
    sb_vf = sb_xn
    sb_w = [p.sb("a1_w0", [128, D], BF16), p.sb("a1_w1", [128, D], BF16)]
    sb_bias = p.sb("a1_bias", [128, 48], F32)
    sb_qsq = [p.sb("a1_qsq0", [128, TC], BF16), p.sb("a1_qsq1", [128, TC], BF16)]
    sb_nrow = p.sb("a1_nrow", [1, TC], F32)
    sb_vstage = p.sb("a1_vst", [128, D], BF16)
    scr2 = {k: p.sb("a1scr_" + k, [128, 1] if k not in ("sqd", "xt") else [128, D],
                    F32 if k != "sqd" else BF16)
            for k in ("s1", "s2", "negm", "m2", "var", "rstd", "sqd", "xt")}
    ps_a = [p.ps("a1_psa0", [128, 1024], F32), p.ps("a1_psa1", [128, 1024], F32)]
    ps_t = p.ps("a1_pst", [128, 512], BF16)
    ps_r = p.ps("a1_psr", [1, TC], F32)
    p.sem("dn")

    def body(sy):
        p.dma("ds", sy.dma_start(out=sb_bias[:, :], in_=io["bias1"].ap()))
    p.sec("sp", body)
    consts = p.snap()

    xnT_done = emit_preln(p, scr2, x, sb_xn, sb_xnT, sb_idn, ps_t, 8, TC)

    p.sem("dn"); p.sem("ccnrm"); p.sem("cckf1"); p.sem("ccvt1")
    p.sem("dkf"); p.sem("dvt1")
    evac_dve = {}
    mm_qkv = {}
    eidx = [0]

    def emit_qkv_group(jcs):
        for jc in jcs:
            idx = eidx[0]
            eidx[0] += 1
            par = idx % 2

            def body(g, jc=jc, par=par, idx=idx):
                p.w(g, "pe", mm_qkv.get(idx - 2, 0))
                p.dma("dw%d" % par, g.dma_start(
                    out=sb_w[par][:, :],
                    in_=wqkv[jc * 128:(jc + 1) * 128, :]))
            p.sec("pool", body)
            wl = p.snap()

            def body(t, par=par, wl=wl, idx=idx):
                p.w(t, "dw%d" % par, wl["dw%d" % par])
                p.w(t, "dve", max(xnT_done["dve"], evac_dve.get(idx - 2, 0)))
                for half in range(2):
                    for dc in range(DC):
                        lhs = sb_w[par][:, dc * 128:(dc + 1) * 128]
                        mmi = t.matmul(ps_a[par][:, half * 512:(half + 1) * 512], lhs,
                                       sb_xnT[:, dc * TC + half * 512: dc * TC + (half + 1) * 512],
                                       start=(dc == 0), stop=(dc == DC - 1))
                mm_qkv[idx] = p.inc("pe", mmi)
            p.sec("pe", body)
            ml = p.snap()

            def body(v, jc=jc, par=par, ml=ml, idx=idx):
                p.w(v, "pe", ml["pe"])
                p.w(v, "ds", consts["ds"])
                if jc < DC:
                    evac_dve[idx] = p.inc("dve", v.tensor_scalar(
                        sb_qf[:, jc * TC:(jc + 1) * TC], ps_a[par][:, :],
                        sb_bias[:, jc:jc + 1], SM_SCALE, op0=ALU.add, op1=ALU.mult))
                elif jc < 2 * DC:
                    evac_dve[idx] = p.inc("dve", v.tensor_scalar(
                        sb_kn[:, (jc - DC) * TC:(jc - DC + 1) * TC], ps_a[par][:, :],
                        sb_bias[:, jc:jc + 1], None, op0=ALU.add))
                else:
                    evac_dve[idx] = p.inc("dve", v.tensor_scalar(
                        sb_vf[:, (jc - 2 * DC) * TC:(jc - 2 * DC + 1) * TC], ps_a[par][:, :],
                        sb_bias[:, jc:jc + 1], None, op0=ALU.add))
            p.sec("dve", body)

    def emit_transposes(src, is_v):
        for tt in range(8):
            for dc4 in range(0, DC, 4):
                def body(t, tt=tt, dc4=dc4, src=src, dnow=p.c["dve"]):
                    p.w(t, "dve", max(dnow, p.c["dve"]))
                    for jj in range(4):
                        dc = dc4 + jj
                        mmi = t.transpose(ps_t[:, jj * 128:(jj + 1) * 128],
                                          src[:, dc * TC + tt * 128: dc * TC + (tt + 1) * 128],
                                          sb_idn[:, :])
                    p.inc("pe", mmi)
                p.sec("pe", body)
                pl = p.snap()

                if is_v:
                    def body(v, dc4=dc4, pl=pl, dvt_now=p.c["dvt1"]):
                        p.w(v, "pe", pl["pe"])
                        p.w(v, "dvt1", dvt_now)
                        p.inc("dve", v.tensor_copy(sb_vstage[:, dc4 * 128:(dc4 + 4) * 128], ps_t[:, :]))
                    p.sec("dve", body)
                else:
                    def body(v, tt=tt, dc4=dc4, pl=pl):
                        p.w(v, "pe", pl["pe"])
                        p.inc("dve", v.tensor_copy(
                            sb_qt[:, tt * D + dc4 * 128: tt * D + (dc4 + 4) * 128], ps_t[:, :]))
                    p.sec("dve", body)
            if is_v:
                vl = p.snap()

                def body(sy, tt=tt, vl=vl):
                    p.w(sy, "dve", vl["dve"])
                    p.dma("dvt1", sy.dma_start(out=io["agvt1_in"][tt * 128:(tt + 1) * 128, :],
                                               in_=sb_vstage[:, :]))
                p.sec("sp", body)

    emit_qkv_group(range(DC, 2 * DC))
    k_done = p.snap()

    def body(sy, kd=k_done):
        p.w(sy, "dve", kd["dve"])
        p.dma("dkf", sy.dma_start(out=io["agkf1_in"].ap(), in_=sb_kn[:, :]))
    p.sec("sp", body)
    knst = p.snap()

    def body(g, knst=knst):
        p.w(g, "dkf", knst["dkf"])
        emit_cc(p, g, io["agkf1_in"], io["agkf1_out"], "cckf1", rg=RGB,
                simcc=io.get("simcc", False))
    p.sec("pool", body)

    emit_qkv_group(range(0, DC))
    q_done = p.snap()

    for h in range(NH):
        par = h % 2

        def body(a, h=h, par=par, qd=q_done):
            p.w(a, "dve", qd["dve"])
            if h >= 2:
                p.w(a, "pe")
            p.inc("act", a.activation(sb_qsq[par][:, :], sb_qf[:, h * TC:(h + 1) * TC],
                                      AF.Square))
        p.sec("act", body)
        al = p.snap()

        def body(t, h=h, par=par, al=al, dve_now=p.c["dve"]):
            p.w(t, "act", al["act"])
            if h == 0:
                p.w(t, "dve", dve_now)
            if h > 0:
                p.w(t, "dve")
            t.matmul(ps_r[0:1, 0:512], sb_oneb[:, :], sb_qsq[par][:, 0:512],
                     start=True, stop=True)
            p.inc("pe", t.matmul(ps_r[0:1, 512:1024], sb_oneb[:, :], sb_qsq[par][:, 512:1024],
                                 start=True, stop=True))
        p.sec("pe", body)
        pe_l = p.snap()

        def body(v, h=h, pe_l=pe_l):
            p.w(v, "pe", pe_l["pe"])
            p.w(v, "dn", p.c["dn"])
            p.inc("dve", v.tensor_copy(sb_nrow[:, :], ps_r[0:1, :]))
        p.sec("dve", body)
        ne = p.snap()

        def body(sy, h=h, ne=ne):
            p.w(sy, "dve", ne["dve"])
            p.dma("dn", sy.dma_start(out=io["agnrm_in"].ap()[:, h * TC:(h + 1) * TC],
                                     in_=sb_nrow[:, :]))
        p.sec("sp", body)
    nrml = p.snap()

    def body(g, nrml=nrml):
        p.w(g, "dn", nrml["dn"])
        emit_cc(p, g, io["agnrm_in"], io["agnrm_out"], "ccnrm", rg=RGB,
                simcc=io.get("simcc", False))
    p.sec("pool", body)

    emit_transposes(sb_qf, False)

    emit_qkv_group(range(2 * DC, 3 * DC))
    emit_transposes(sb_vf, True)
    vst = p.snap()

    def body(g, vst=vst):
        p.w(g, "dvt1", vst["dvt1"])
        emit_cc(p, g, io["agvt1_in"], io["agvt1_out"], "ccvt1", rg=RGB,
                simcc=io.get("simcc", False))
    p.sec("pool", body)
    p.pop()



def emit_l1mid(p, io):
    nc = p.nc
    simcc = io.get("simcc", False)
    phase_fence(p)
    p.push()
    sb_n = p.sb("m_n", [16, S], F32)
    sb_z = p.sb("m_z", [16, S], F32)
    sb_mx = p.sb("m_mx", [16, 8], F32)
    sb_msk = p.sb("m_msk", [16, S], F32)
    sb_zero = p.sb("m_zero", [16, S], F32)
    sb_slot = p.sb("m_slot", [16, S], F32)
    sb_slown = io["sb_slotm"]
    p.sem("dnm"); p.sem("dsl")

    def body(g):
        p.w(g, "ccnrm", p.c["ccnrm"])
        p.dma("dnm", g.dma_start(
            out=sb_n[:, :].rearrange("h (r t) -> h r t", t=TC),
            in_=io["agnrm_out"].ap().rearrange("r (h t) -> h r t", h=16)))
        p.inc("pool", g.memset(sb_zero[:, :], 0.0))
    p.sec("pool", body)
    cl = p.snap()

    def body(v, cl=cl):
        p.w(v, "dnm", cl["dnm"])
        p.w(v, "pool", cl["pool"])
        src = sb_n
        for r in range(8):
            p.inc("dve", v.max(out=sb_mx[:, :], in_=src[:, :]))
            p.sw(v, "dve")
            if r == 7:
                p.inc("dve", v.memset(sb_mx[:, 6:8], -1e30))
                p.sw(v, "dve")
            p.inc("dve", v.match_replace(out=sb_z[:, :], in_to_replace=sb_mx[:, :],
                                         in_values=src[:, :], imm_value=-1e30))
            p.sw(v, "dve")
            src = sb_z
        p.inc("dve", v.tensor_scalar(sb_msk[:, :], sb_z[:, :], -1e29, None, op0=ALU.is_lt))
        p.sw(v, "dve")
        p.inc("dve", v.memset(sb_msk[:, 0:1], 1.0))
        p.inc("dve", v.memset(sb_msk[:, S - 1:S], 1.0))
        p.sw(v, "dve")
        p.inc("dve", v.tensor_tensor_scan(sb_slot[:, :], sb_zero[:, :], sb_msk[:, :],
                                          0.0, op0=ALU.add, op1=ALU.add))
        p.sw(v, "dve")
        p.inc("dve", v.tensor_mul(sb_slot[:, :], sb_slot[:, :], sb_msk[:, :]))
        p.sw(v, "dve")
        p.inc("dve", v.tensor_scalar(sb_slot[:, :], sb_slot[:, :], -1.0, None, op0=ALU.add))
    p.sec("dve", body)
    done = p.snap()

    def body(g, done=done):
        p.w(g, "dve", done["dve"])
        p.dma("dsl", g.dma_start(out=io["slotscr"].ap(), in_=sb_slot[:, :]))
        p.w(g, "dsl")
        p.dma("dsl", g.dma_start(out=sb_slown[:, :],
                                 in_=io["slotscr"].ap()[:, bass.ds(io["dy"]["oTo"], TC)]))
    p.sec("pool", body)
    p.pop()



def emit_l1b(p, io):
    nc = p.nc
    sb_idn = io["sb_idn"]
    sb_idnf = io["sb_idnf"]
    sb_iota = io["sb_iota"]
    sb_onerow = io["sb_onerow"]
    sb_qf = io["sb_qf"]
    sb_qt = io["sb_qt"]
    sb_slotm = io["sb_slotm"]
    x = io["x2"]
    xo = io["x3"]
    agkf = io["agkf1_out"]
    agvt = io["agvt1_out"]
    RG = [[0, 1, 2, 3], [4, 5, 6, 7]]
    simcc = io.get("simcc", False)

    phase_fence(p)
    p.push()
    scr = alloc_ln_scr(p, "b1")
    sb_ctx = p.sb("b1_ctx", [128, 8 * D], BF16)
    sb_kh = [p.sb("b1_kh0", [128, 1152], BF16), p.sb("b1_kh1", [128, 1152], BF16)]
    sb_vh = [p.sb("b1_vh0", [128, 1152], BF16), p.sb("b1_vh1", [128, 1152], BF16)]
    sb_kg = [p.sb("b1_kg0", [128, S], BF16), p.sb("b1_kg1", [128, S], BF16)]
    sb_vg = [p.sb("b1_vg0", [128, S], BF16), p.sb("b1_vg1", [128, S], BF16)]
    sb_msk = p.sb("b1_msk", [128, 8 * 256], BF16)
    sb_probs = [p.sb("b1_probs0", [128, 256], BF16), p.sb("b1_probs1", [128, 256], BF16)]
    sb_probsT = p.sb("b1_probsT", [128, 256], BF16)
    sb_mx = p.sb("b1_mx", [128, 1], F32)
    sb_negmx = p.sb("b1_negmx", [128, 1], F32)
    sb_l = p.sb("b1_l", [128, 1], F32)
    sb_rl = p.sb("b1_rl", [128, 1], F32)
    sb_slotm_tm = p.sb("b1_sltm", [128, 8 * 16], F32)
    sb_sct = p.sb("b1_sct", [128, NH * 8 * 64], BF16)
    sb_qg = p.sb("b1_qg", [64, 2048], F32)
    sb_qga = p.sb("b1_qga", [64, 2048], F32)
    sb_qgb = p.sb("b1_qgb", [64, 128], BF16)
    sb_qgT = p.sb("b1_qgT", [128, 64], BF16)
    sb_mh = p.sb("b1_mh", [64, 2], F32)
    sb_mrow = p.sb("b1_mrow", [64, 1], F32)
    sb_m64 = p.sb("b1_m64", [1, 64], F32)
    sb_mp = p.sb("b1_mp", [1, 1], F32)
    sb_negmb = p.sb("b1_negmb", [128, 1], F32)
    sb_lacc = p.sb("b1_lacc", [64, 8], F32)
    sb_lsum = p.sb("b1_lsum", [64, 1], F32)
    sb_grl = p.sb("b1_grl", [64, 1], F32)
    sb_dump = p.sb("b1_dump", [64, 2048], BF16)
    sb_pt = [p.sb("b1_pt0", [128, 256], BF16), p.sb("b1_pt1", [128, 256], BF16)]
    sb_ga = p.sb("b1_ga", [64, 129], BF16)
    sb_scc = p.sb("b1_scc", [64, 128], BF16)
    sb_om = p.sb("b1_om", [128, 1], F32)
    p.sem("dmk1"); p.sem("dq0"); p.sem("ccar")
    p.sem("dk0"); p.sem("dk1"); p.sem("dsl")

    def body(g):
        p.dma("dmk1", g.dma_start(out=sb_msk[:, :], in_=io["lmask"].ap()))
        p.inc("pool", g.memset(sb_ga[:, :], 1.0))
    p.sec("pool", body)
    consts = p.snap()

    scA = ExitStack()
    ps_trf = scA.enter_context(nc.psum_tensor("b1_pstrf", [128, 16], F32))
    ps_qg = scA.enter_context(nc.psum_tensor("b1_psqg", [64, 128], F32))
    ps_s = [scA.enter_context(nc.psum_tensor("b1_pss0", [128, 256], F32)),
            scA.enter_context(nc.psum_tensor("b1_pss1", [128, 256], F32))]
    ps_pt = scA.enter_context(nc.psum_tensor("b1_pspt", [128, 256], BF16))
    ps_c = scA.enter_context(nc.psum_tensor("b1_psc", [128, 128], F32))

    for tt in range(8):
        def body(t, tt=tt):
            p.w(t, "dsl", p.c["dsl"])
            if tt > 0:
                p.w(t, "dve")
            p.inc("pe", t.transpose(ps_trf[:, :], sb_slotm[:, tt * 128:(tt + 1) * 128],
                                    sb_idnf[0:16, 0:16]))
        p.sec("pe", body)
        pl = p.snap()

        def body(v, tt=tt, pl=pl):
            p.w(v, "pe", pl["pe"])
            p.inc("dve", v.tensor_copy(sb_slotm_tm[:, tt * 16:(tt + 1) * 16], ps_trf[:, :]))
        p.sec("dve", body)
    sl_done = p.snap()

    for h in range(NH):
        def body(v, h=h):
            p.w(v, "dve", sl_done["dve"])
            for tt in range(8):
                p.inc("dve", v.tensor_scalar(
                    sb_sct[:, (h * 8 + tt) * 64:(h * 8 + tt + 1) * 64],
                    sb_iota[:, :], sb_slotm_tm[:, tt * 16 + h: tt * 16 + h + 1],
                    None, op0=ALU.is_equal))
        p.sec("dve", body)
    sct_done = p.snap()

    for h in range(NH):
        def body(t, h=h, sd=sct_done):
            p.w(t, "dve", sd["dve"])
            if h > 0:
                p.w(t, "dve")
            for tt in range(8):
                mmi = t.matmul(ps_qg[:, :],
                               sb_sct[:, (h * 8 + tt) * 64:(h * 8 + tt + 1) * 64],
                               sb_qt[:, tt * D + h * 128: tt * D + (h + 1) * 128],
                               start=(tt == 0), stop=(tt == 7))
            p.inc("pe", mmi)
        p.sec("pe", body)
        ql = p.snap()

        def body(v, h=h, ql=ql):
            p.w(v, "pe", ql["pe"])
            p.inc("dve", v.tensor_copy(sb_qg[:, h * 128:(h + 1) * 128], ps_qg[:, :]))
        p.sec("dve", body)
    qg_done = p.snap()

    def body(g, qd=qg_done):
        p.w(g, "dve", qd["dve"])
        p.dma("dq0", g.dma_start(out=io["arqg_in"].ap(), in_=sb_qg[:, :]))
        p.w(g, "dq0", p.c["dq0"])
        emit_cc(p, g, io["arqg_in"], io["arqg_out"], "ccar",
                kind="AllReduce", alu=ALU.add, rg=RG, simcc=simcc)
    p.sec("pool", body)

    def body(g):
        p.w(g, "cckf1", p.c["cckf1"])
        p.w(g, "ccvt1", p.c["ccvt1"])
        dy = io["dy"]
        p.dma("dsg", g.dma_start(out=io["stg_k1l"][:, :],
                                 in_=agkf[bass.ds(dy["o128l"], 128), :]))
        p.dma("dsg", g.dma_start(out=io["stg_k1r"][:, :],
                                 in_=agkf[bass.ds(dy["o128r"], 128), :]))
        p.dma("dsg", g.dma_start(out=io["stg_v1l"][:, :],
                                 in_=agvt[bass.ds(dy["oTl960"], 64), :]))
        p.dma("dsg", g.dma_start(out=io["stg_v1r"][:, :],
                                 in_=agvt[bass.ds(dy["oTr"], 64), :]))
    p.sec("pool", body)
    stg1 = p.snap()

    exp_act1 = {"prev": p.c["act"]}
    ptc_dve1 = {0: 0}
    norm_dve1 = {}
    last_ctx_dve = 0
    pv_pe1 = {}
    for h in range(NH):
        par = h % 2

        def body(g, h=h, par=par):
            p.w(g, "pe", pv_pe1.get(h - 2, 0))
            p.w(g, "dsg", stg1["dsg"])
            p.dma("dk%d" % par, g.dma_start(
                out=sb_kh[par][:, 0:64],
                in_=io["stg_k1l"][:, h * TC + 960: h * TC + 1024]))
            p.dma("dk%d" % par, g.dma_start(
                out=sb_kh[par][:, 64:1088],
                in_=io["agkf1_in"][:, h * TC:(h + 1) * TC]))
            p.dma("dk%d" % par, g.dma_start(
                out=sb_kh[par][:, 1088:1152],
                in_=io["stg_k1r"][:, h * TC: h * TC + 64]))
            p.dma("dk%d" % par, g.dma_start(
                out=sb_vh[par][0:64, 0:128],
                in_=io["stg_v1l"][:, h * 128:(h + 1) * 128]))
            p.dma("dk%d" % par, g.dma_start(
                out=sb_vh[par][64:128, 0:128],
                in_=io["agvt1_in"][0:64, h * 128:(h + 1) * 128]))
            p.dma("dk%d" % par, g.dma_start(
                out=sb_vh[par][:, 128:1024].rearrange("p (kt j) -> p kt j", j=128),
                in_=io["agvt1_in"][64:960, h * 128:(h + 1) * 128]
                    .rearrange("(kt p) j -> p kt j", p=128)))
            p.dma("dk%d" % par, g.dma_start(
                out=sb_vh[par][0:64, 1024:1152],
                in_=io["agvt1_in"][960:1024, h * 128:(h + 1) * 128]))
            p.dma("dk%d" % par, g.dma_start(
                out=sb_vh[par][64:128, 1024:1152],
                in_=io["stg_v1r"][:, h * 128:(h + 1) * 128]))
        p.sec("pool", body)
        kl = p.c["dk%d" % par]

        for step in range(9):
            if step < 8:
                qt = step
                spar = qt % 2

                def body(t, h=h, qt=qt, par=par, kl=kl, spar=spar,
                         ea=exp_act1.get((h, qt - 2), exp_act1.get("prev", 0))):
                    p.w(t, "dk%d" % par, kl)
                    p.w(t, "act", ea)
                    p.w(t, "dmk1", consts["dmk1"])
                    t.matmul(ps_s[spar][:, 0:256], sb_idn[:, :],
                             sb_msk[:, qt * 256:(qt + 1) * 256],
                             start=True, stop=False)
                    p.inc("pe", t.matmul(ps_s[spar][:, 0:256],
                                         sb_qf[:, h * TC + qt * 128: h * TC + (qt + 1) * 128],
                                         sb_kh[par][:, qt * 128: qt * 128 + 256],
                                         start=False, stop=True))
                p.sec("pe", body)
                sc = p.snap()

                def body(v, qt=qt, spar=spar, sc=sc):
                    p.w(v, "pe", sc["pe"])
                    p.w(v, "dve")
                    p.inc("dve", v.reduce_max(sb_mx[:, :], ps_s[spar][:, 0:256], axis=AXL.X))
                    p.sw(v, "dve")
                    p.inc("dve", v.tensor_scalar(sb_negmx[:, :], sb_mx[:, :], -1.0,
                                                 None, op0=ALU.mult))
                p.sec("dve", body)
                mm_ = p.snap()

                def body(a, spar=spar, mm_=mm_):
                    p.w(a, "dve", mm_["dve"])
                    p.inc("act", a.activation(sb_probs[spar][:, :], ps_s[spar][:, 0:256],
                                              AF.Exp, bias=sb_negmx[:, :],
                                              accum_out=sb_l[:, :]))
                p.sec("act", body)
                ee = p.snap()
                exp_act1[(h, qt)] = ee["act"]

                def body(v, spar=spar, ee=ee):
                    p.w(v, "act", ee["act"])
                    p.inc("dve", v.reciprocal(sb_rl[:, :], sb_l[:, :]))
                    p.sw(v, "dve")
                    p.inc("dve", v.tensor_scalar(sb_probs[spar][:, :], sb_probs[spar][:, :],
                                                 sb_rl[:, :], None, op0=ALU.mult))
                p.sec("dve", body)
                norm_dve1[(h, qt)] = p.c["dve"]

            if step >= 1:
                q = step - 1
                spar2 = q % 2

                def body(t, q=q, spar2=spar2, nd=norm_dve1[(h, q)], pt=ptc_dve1[0]):
                    p.w(t, "dve", max(nd, pt))
                    for j in range(2):
                        mmi = t.transpose(ps_pt[:, j * 128:(j + 1) * 128],
                                          sb_probs[spar2][:, j * 128:(j + 1) * 128],
                                          sb_idn[:, :])
                    p.inc("pe", mmi)
                p.sec("pe", body)
                tp = p.snap()

                def body(v, tp=tp):
                    p.w(v, "pe", tp["pe"])
                    p.inc("dve", v.tensor_copy(sb_probsT[:, :], ps_pt[:, :]))
                p.sec("dve", body)
                ptc_dve1[0] = p.c["dve"]
                pte = p.snap()

                def body(t, q=q, par=par, kl=kl, pte=pte, lc=last_ctx_dve):
                    p.w(t, "dve", max(pte["dve"], lc))
                    p.w(t, "dk%d" % par, kl)
                    for j in range(2):
                        mmi = t.matmul(ps_c[:, :],
                                       sb_probsT[:, j * 128:(j + 1) * 128],
                                       sb_vh[par][:, (q + j) * 128:(q + j + 1) * 128],
                                       start=(j == 0), stop=(j == 1))
                    p.inc("pe", mmi)
                p.sec("pe", body)
                pv = p.snap()

                def body(v, h=h, q=q, pv=pv):
                    p.w(v, "pe", pv["pe"])
                    p.inc("dve", v.tensor_copy(
                        sb_ctx[:, q * D + h * 128: q * D + (h + 1) * 128], ps_c[:, :]))
                p.sec("dve", body)
                last_ctx_dve = p.c["dve"]
        pv_pe1[h] = p.c["pe"]
        exp_act1["prev"] = exp_act1[(h, 7)]
    local_done = p.snap()

    def body(g):
        p.w(g, "ccar")
        p.dma("dq0", g.dma_start(out=sb_qga[:, :], in_=io["arqg_out"].ap()))
    p.sec("pool", body)
    ar_l = p.snap()

    scA.close()
    ps_g = p.ps("b1_psg", [64, 2048], F32)
    ps_st = [p.ps("b1_psst0", [128, 256], F32), p.ps("b1_psst1", [128, 256], F32)]
    ps_gcbl = p.ps("b1_psgc", [128, 132], F32)
    ps_tr2 = p.ps("b1_pstr2", [128, 128], BF16)

    gpv_pe = {}
    p.sem("dkg0"); p.sem("dkg1")
    for h in range(NH):
        par = h % 2

        def body(g, h=h, par=par):
            p.w(g, "pe", gpv_pe.get(h - 2, 0))
            p.dma("dkg%d" % par, g.dma_start(
                out=sb_kg[par][:, :].rearrange("p (r t) -> p r t", t=TC),
                in_=agkf[:, h * TC:(h + 1) * TC].rearrange("(r p) t -> p r t", p=128)))
            p.dma("dkg%d" % par, g.dma_start(
                out=sb_vg[par][:, :].rearrange("p (kt j2) -> p kt j2", j2=128),
                in_=agvt[:, h * 128:(h + 1) * 128]
                    .rearrange("(kt p) j2 -> p kt j2", p=128)))
        p.sec("pool", body)
        kl = p.c["dkg%d" % par]

        def body(v, h=h, al=ar_l):
            p.w(v, "dq0", al["dq0"])
            p.w(v, "pe")
            p.inc("dve", v.tensor_copy(sb_qgb[:, :], sb_qga[:, h * 128:(h + 1) * 128]))
        p.sec("dve", body)
        cq = p.snap()

        def body(t, cq=cq):
            p.w(t, "dve", cq["dve"])
            p.inc("pe", t.transpose(ps_tr2[0:128, 0:64], sb_qgb[:, :], sb_idn[0:64, 0:64]))
        p.sec("pe", body)
        tq = p.snap()

        def body(v, tq=tq):
            p.w(v, "pe", tq["pe"])
            p.inc("dve", v.tensor_copy(sb_qgT[:, :], ps_tr2[0:128, 0:64]))
        p.sec("dve", body)
        qgT_done = p.snap()

        def emit_scores_half(half, dw, aw, kl=kl, par=par):
            def body(t, half=half, dw=dw, aw=aw, kl=kl, par=par):
                p.w(t, "dkg%d" % par, kl)
                p.w(t, "dve", dw)
                p.w(t, "act", aw)
                for i4 in range(4):
                    mmi = t.matmul(ps_g[:, i4 * 512:(i4 + 1) * 512], sb_qgT[:, :],
                                   sb_kg[par][:, half * 2048 + i4 * 512: half * 2048 + (i4 + 1) * 512],
                                   start=True, stop=True)
                p.inc("pe", mmi)
            p.sec("pe", body)
            return p.snap()

        s0 = emit_scores_half(0, qgT_done["dve"], p.c["act"])

        def body(v, s0=s0):
            p.w(v, "pe", s0["pe"])
            p.inc("dve", v.reduce_max(sb_mh[:, 0:1], ps_g[:, :], axis=AXL.X))
        p.sec("dve", body)
        m0 = p.snap()

        s1 = emit_scores_half(1, m0["dve"], p.c["act"])

        def body(v, s1=s1):
            p.w(v, "pe", s1["pe"])
            p.inc("dve", v.reduce_max(sb_mh[:, 1:2], ps_g[:, :], axis=AXL.X))
            p.sw(v, "dve")
            p.inc("dve", v.tensor_max(sb_mrow[:, :], sb_mh[:, 0:1], sb_mh[:, 1:2]))
        p.sec("dve", body)
        m1 = p.snap()

        def body(t, m1=m1):
            p.w(t, "dve", m1["dve"])
            p.inc("pe", t.transpose(ps_gcbl[0:1, 0:64], sb_mrow[:, :], sb_idnf[0:64, 0:64]))
        p.sec("pe", body)
        tm = p.snap()

        def body(v, tm=tm):
            p.w(v, "pe", tm["pe"])
            p.inc("dve", v.tensor_copy(sb_m64[:, :], ps_gcbl[0:1, 0:64]))
            p.sw(v, "dve")
            p.inc("dve", v.reduce_max(sb_mp[:, :], sb_m64[:, :], axis=AXL.X))
            p.sw(v, "dve")
            p.inc("dve", v.tensor_scalar(sb_mp[:, :], sb_mp[:, :], -1.0, None, op0=ALU.mult))
        p.sec("dve", body)
        mp = p.snap()

        def body(t, mp=mp):
            p.w(t, "dve", mp["dve"])
            p.inc("pe", t.matmul(ps_gcbl[0:128, 64:65], sb_onerow[:, :], sb_mp[:, :],
                                 start=True, stop=True))
        p.sec("pe", body)
        bb = p.snap()

        def body(v, bb=bb):
            p.w(v, "pe", bb["pe"])
            p.inc("dve", v.tensor_copy(sb_negmb[:, :], ps_gcbl[0:128, 64:65]))
        p.sec("dve", body)
        nb = p.snap()

        def body(a, nb=nb):
            p.w(a, "dve", nb["dve"])
            for i4 in range(4):
                p.inc("act", a.activation(sb_dump[:, i4 * 512:(i4 + 1) * 512],
                                          ps_g[:, i4 * 512:(i4 + 1) * 512],
                                          AF.Exp, bias=sb_negmb[0:64, :],
                                          accum_out=sb_lacc[:, 4 + i4:5 + i4]))
        p.sec("act", body)
        e1 = p.snap()

        s0b = emit_scores_half(0, p.c["dve"], e1["act"])

        def body(a, s0b=s0b):
            p.w(a, "pe", s0b["pe"])
            for i4 in range(4):
                p.inc("act", a.activation(sb_dump[:, i4 * 512:(i4 + 1) * 512],
                                          ps_g[:, i4 * 512:(i4 + 1) * 512],
                                          AF.Exp, bias=sb_negmb[0:64, :],
                                          accum_out=sb_lacc[:, i4:1 + i4]))
        p.sec("act", body)
        e0 = p.snap()

        def body(v, e0=e0):
            p.w(v, "act", e0["act"])
            p.inc("dve", v.reduce_sum(sb_lsum[:, :], sb_lacc[:, :], axis=AXL.X))
        p.sec("dve", body)
        lv = p.snap()

        def body(v, lv=lv):
            p.inc("dve", v.reciprocal(sb_grl[:, :], sb_lsum[:, :]))
        p.sec("dve", body)
        rl_done = p.snap()

        exp_acts = {}
        for kgp in range(4):
            kt0 = kgp * 8

            def body(t, kt0=kt0, par=par, rd=rl_done, ea=exp_acts.get(kgp - 1, 0)):
                p.w(t, "dve", rd["dve"])
                p.w(t, "act", ea)
                for par2 in range(2):
                    for i in range(4):
                        kt = kt0 + par2 * 4 + i
                        mmi = t.matmul(ps_st[par2][:, i * 64:(i + 1) * 64],
                                       sb_kg[par][:, kt * 128:(kt + 1) * 128],
                                       sb_qgT[:, :], start=True, stop=True)
                p.inc("pe", mmi)
            p.sec("pe", body)
            stl = p.snap()

            def body(a, stl=stl):
                p.w(a, "pe", stl["pe"])
                p.inc("act", a.activation(sb_pt[0][:, :], ps_st[0][:, :], AF.Exp,
                                          bias=sb_negmb[:, :]))
                p.inc("act", a.activation(sb_pt[1][:, :], ps_st[1][:, :], AF.Exp,
                                          bias=sb_negmb[:, :]))
            p.sec("act", body)
            exp_acts[kgp] = p.c["act"]
            el = p.snap()

            def body(t, kt0=kt0, kgp=kgp, par=par, el=el):
                p.w(t, "act", el["act"])
                for par2 in range(2):
                    for i in range(4):
                        kt = kt0 + par2 * 4 + i
                        mmi = t.matmul(ps_gcbl[0:64, 0:128],
                                       sb_pt[par2][:, i * 64:(i + 1) * 64],
                                       sb_vg[par][:, kt * 128:(kt + 1) * 128],
                                       start=(kgp == 0 and par2 == 0 and i == 0),
                                       stop=(kgp == 3 and par2 == 1 and i == 3))
                p.inc("pe", mmi)
            p.sec("pe", body)
        pv_done = p.snap()
        gpv_pe[h] = pv_done["pe"]

        def body(v, pv_done=pv_done):
            p.w(v, "pe", pv_done["pe"])
            p.w(v, "act")
            p.w(v, "pool", consts["pool"])
            p.inc("dve", v.tensor_scalar(sb_ga[0:64, 0:128], ps_gcbl[0:64, 0:128],
                                         sb_grl[:, :], None, op0=ALU.mult))
        p.sec("dve", body)
        ga_done = p.snap()

        for tt in range(8):
            def body(t, h=h, tt=tt, ga=ga_done):
                p.w(t, "dve", max(ga["dve"], p.c["dve"]))
                p.inc("pe", t.transpose(ps_tr2[0:64, 0:128],
                                        sb_sct[:, (h * 8 + tt) * 64:(h * 8 + tt + 1) * 64],
                                        sb_idn[:, :]))
            p.sec("pe", body)
            t1 = p.snap()

            def body(v, t1=t1):
                p.w(v, "pe", t1["pe"])
                p.inc("dve", v.tensor_copy(sb_scc[:, :], ps_tr2[0:64, 0:128]))
            p.sec("dve", body)
            t2 = p.snap()

            def body(t, t2=t2):
                p.w(t, "dve", t2["dve"])
                p.inc("pe", t.matmul(ps_gcbl[0:128, 0:129], sb_scc[:, :], sb_ga[:, :],
                                     start=True, stop=True))
            p.sec("pe", body)
            t3 = p.snap()

            def body(v, h=h, tt=tt, t3=t3):
                p.w(v, "pe", t3["pe"])
                p.inc("dve", v.tensor_scalar(sb_om[:, :], ps_gcbl[0:128, 128:129],
                                             -1.0, 1.0, op0=ALU.mult, op1=ALU.add))
                p.sw(v, "dve")
                p.inc("dve", v.scalar_tensor_tensor(
                    sb_ctx[:, tt * D + h * 128: tt * D + (h + 1) * 128],
                    sb_ctx[:, tt * D + h * 128: tt * D + (h + 1) * 128],
                    sb_om[:, :], ps_gcbl[0:128, 0:128],
                    op0=ALU.mult, op1=ALU.add))
            p.sec("dve", body)

    for tt in range(8):
        src_ready = p.snap()
        emit_ln_residual(p, scr, sb_ctx[:, tt * D:(tt + 1) * D], x, xo, tt, src_ready)
    p.pop()



def build_fused(sim_no_cc=False, upto=8):
    p = P("mega")
    nc = p.nc
    io = {"simcc": sim_no_cc}

    def ei(n, s, d=F32):
        return nc.dram_tensor(n, s, d, kind="ExternalInput")

    io["x"] = ei("x", [TC, D])
    io["wqkv0"] = ei("wqkv0", [3 * D, D])
    io["qw"] = ei("qw", [128, DC])
    io["kw"] = ei("kw", [128, DC])
    io["cosr"] = ei("cosr", [128, TC])
    io["sinr"] = ei("sinr", [128, TC])
    io["amask"] = ei("amask", [128, 8 * 1152], BF16)
    io["wo"] = ei("wo", [D, D])
    io["up0s"] = ei("up0s", [1024, 2048])
    io["dn0s"] = ei("dn0s", [256, 8192])
    io["w1s"] = ei("w1s", [1024, 2048])
    io["w2s"] = ei("w2s", [256, 8192])
    io["upsc"] = ei("upsc", [128, 64])
    io["dsc"] = ei("dsc", [128, 16])
    io["osc"] = ei("osc", [1, 1])
    io["wqkv1"] = ei("wqkv1", [3 * D, D])
    io["bias1"] = ei("bias1", [128, 48])
    io["lmask"] = ei("lmask", [128, 8 * 256], BF16)
    io["b1"] = ei("b1", [128, 64])
    io["b2"] = ei("b2", [128, 16])
    io["out"] = nc.dram_tensor("out", [TC, D], F32, kind="ExternalOutput")

    io["x1"] = p.dram("x1s", [TC, D])
    io["x2"] = p.dram("x2s", [TC, D])
    io["x3"] = p.dram("x3s", [TC, D])
    io["arth_in"] = p.dram("arth_in", [1, 2])
    io["arth_out"] = p.dram("arth_out", [1, 2])
    for nm, shp in [("agu0", ([1024, 2048], [8 * 1024, 2048])),
                    ("agd0", ([256, 8192], [8 * 256, 8192])),
                    ("agw1", ([1024, 2048], [8 * 1024, 2048])),
                    ("agw2", ([256, 8192], [8 * 256, 8192]))]:
        io[nm + "_in"] = p.dram(nm + "_in", shp[0], BF16)
        io[nm + "_out"] = p.dram(nm + "_out", shp[1], BF16)
    io["agkn_in"] = p.dram("agkn_in", [128, NH * TC], BF16)
    io["agkn_out"] = p.dram("agkn_out", [4 * 128, NH * TC], BF16)
    io["agvt_in"] = p.dram("agvt_in", [TC, D], BF16)
    io["agvt_out"] = p.dram("agvt_out", [4 * TC, D], BF16)
    io["agkf1_in"] = p.dram("agkf1_in", [128, NH * TC], BF16)
    io["agkf1_out"] = p.dram("agkf1_out", [4 * 128, NH * TC], BF16)
    io["agvt1_in"] = p.dram("agvt1_in", [TC, D], BF16)
    io["agvt1_out"] = p.dram("agvt1_out", [4 * TC, D], BF16)
    io["agnrm_in"] = p.dram("agnrm_in", [1, NH * TC])
    io["agnrm_out"] = p.dram("agnrm_out", [4, NH * TC])
    io["arqg_in"] = p.dram("arqg_in", [64, 2048])
    io["arqg_out"] = p.dram("arqg_out", [64, 2048])
    io["slotscr"] = p.dram("slotscr", [16, S])
    io["stg_k0"] = p.dram("stg_k0", [128, NH * TC], BF16)
    io["stg_v0"] = p.dram("stg_v0", [TC, D], BF16)
    io["stg_k1l"] = p.dram("stg_k1l", [128, NH * TC], BF16)
    io["stg_k1r"] = p.dram("stg_k1r", [128, NH * TC], BF16)
    io["stg_v1l"] = p.dram("stg_v1l", [64, D], BF16)
    io["stg_v1r"] = p.dram("stg_v1r", [64, D], BF16)

    idn = p.const(IDN_BF, "c_idn")
    idnf = p.const(IDN_F32, "c_idnf")
    iota = p.const(IOTA64, "c_iota")

    p.start()

    io["sb_idn"] = p.sb("g_idn", [128, 128], BF16)
    io["sb_idnf"] = p.sb("g_idnf", [128, 128], F32)
    io["sb_iota"] = p.sb("g_iota", [128, 64], BF16)
    io["sb_onerow"] = p.sb("g_onerow", [1, 128], F32)
    io["sb_oneb"] = p.sb("g_oneb", [128, 1], BF16)
    io["sb_onef"] = p.sb("g_onef", [128, 1], F32)

    p.sem("dxa0"); p.sem("dxa1"); p.sem("dst")

    def body(sy):
        p.dma("ds", sy.dma_start(out=io["sb_idn"][:, :], in_=idn.ap()))
        p.dma("ds", sy.dma_start(out=io["sb_idnf"][:, :], in_=idnf.ap()))
        p.dma("ds", sy.dma_start(out=io["sb_iota"][:, :], in_=iota.ap()))
    p.sec("sp", body)
    io["boot_ds"] = p.c["ds"]

    def body(g):
        io["dy"] = p.dyn(static=sim_no_cc)
        g.memset(io["sb_onerow"][:, :], 1.0)
        g.memset(io["sb_oneb"][:, :], 1.0)
        p.inc("pool", g.memset(io["sb_onef"][:, :], 1.0))
    p.sec("pool", body)

    ag_emitters = []
    if upto >= 1:
        ag_emitters = emit_prep(p, io)

    if upto >= 2:
        p.push()
        io["sb_qn"] = p.sb("g_qn", [128, NH * TC], BF16)
        emit_l0a(p, io)
        for em in ag_emitters[:2]:
            em()
        if upto >= 3:
            emit_l0b(p, io)
        p.pop()

    if upto >= 4:
        emit_ffn(p, io, 0)

    if upto >= 5:
        p.push()
        io["sb_qf"] = p.sb("g_qf", [128, NH * TC], BF16)
        io["sb_qt"] = p.sb("g_qt", [128, 8 * D], BF16)
        io["sb_slotm"] = p.sb("g_slotm", [16, TC], F32)
        emit_l1a(p, io)
        for em in ag_emitters[2:]:
            em()
        if upto >= 6:
            emit_l1mid(p, io)
        if upto >= 7:
            emit_l1b(p, io)
        p.pop()

    if upto >= 8:
        emit_ffn(p, io, 1)

    def body(sy):
        for k in list(p.S.keys()):
            if k in ("pe", "dve", "act", "pool"):
                continue
            p.w(sy, k)
    p.sec("sp", body)
    return p



def _rope_tables():
    inv_freq = 1.0 / (10000.0 ** (np.arange(0, 128, 2, dtype=np.float32) / 128.0))
    pos = np.arange(S, dtype=np.float32)
    fr = pos[:, None] * inv_freq[None, :]
    return np.cos(fr).astype(np.float32), np.sin(fr).astype(np.float32)


def _l0_mask(c):
    s0 = (c % 4) * TC
    m = np.zeros((8, 128, 1152), np.float32)
    for qt in range(8):
        qpos = s0 + qt * 128 + np.arange(128)
        kpos = s0 - 1024 + qt * 128 + np.arange(1152)
        valid = (kpos[None, :] <= qpos[:, None]) & (qpos[:, None] - kpos[None, :] <= 1023) \
                & (kpos[None, :] >= 0)
        m[qt] = np.where(valid, 0.0, NEG)
    return np.ascontiguousarray(m.transpose(1, 0, 2).reshape(128, 8 * 1152).astype(BF))


def _l1_mask(c):
    s0 = (c % 4) * TC
    lo = s0 - 64
    m = np.zeros((8, 128, 256), np.float32)
    for qt in range(8):
        kpos = lo + qt * 128 + np.arange(256)
        bad = (kpos < 0) | (kpos >= S)
        m[qt, :, :] = np.where(bad[None, :], NEG, 0.0)
    return np.ascontiguousarray(m.transpose(1, 0, 2).reshape(128, 8 * 256).astype(BF))


def _qkv_arr(w):
    return np.ascontiguousarray(
        w.reshape(16, 128, 48, 128).transpose(2, 1, 0, 3).reshape(6144, 2048))


def kernel(**inputs):
    x_full = np.asarray(inputs["L0_attn_hidden_states"], dtype=np.float32)
    wqkv0 = np.ascontiguousarray(inputs["L0_attn_Wqkv"], dtype=np.float32)
    qw = np.asarray(inputs["L0_attn_q_norm_w"], dtype=np.float32)
    kw = np.asarray(inputs["L0_attn_k_norm_w"], dtype=np.float32)
    wo = np.ascontiguousarray(inputs["L0_attn_Wo"], dtype=np.float32)
    up0 = np.asarray(inputs["L0_ffn_up_weight"], dtype=np.float32)
    dn0 = np.asarray(inputs["L0_ffn_down_weight"], dtype=np.float32)
    upsc = np.asarray(inputs["L0_ffn_up_scale"], dtype=np.float32)
    dsc = np.asarray(inputs["L0_ffn_down_scale"], dtype=np.float32)
    osc = np.asarray(inputs["L0_ffn_output_scale"], dtype=np.float32)
    wq1 = np.asarray(inputs["L1_attn_Wq"], dtype=np.float32)
    wk1 = np.asarray(inputs["L1_attn_Wk"], dtype=np.float32)
    wv1 = np.asarray(inputs["L1_attn_Wv"], dtype=np.float32)
    bq1 = np.asarray(inputs["L1_attn_bq"], dtype=np.float32)
    bk1 = np.asarray(inputs["L1_attn_bk"], dtype=np.float32)
    bv1 = np.asarray(inputs["L1_attn_bv"], dtype=np.float32)
    w1 = np.asarray(inputs["L1_ffn_w1"], dtype=np.float32)
    b1 = np.asarray(inputs["L1_ffn_b1"], dtype=np.float32)
    w2 = np.asarray(inputs["L1_ffn_w2"], dtype=np.float32)
    b2 = np.asarray(inputs["L1_ffn_b2"], dtype=np.float32)

    cosf, sinf = _rope_tables()
    wqkv0a = _qkv_arr(wqkv0)
    wqkv1a = _qkv_arr(np.concatenate([wq1, wk1, wv1], axis=1))
    bias1 = np.ascontiguousarray(np.concatenate([bq1, bk1, bv1]).reshape(48, 128).T)

    p = build_fused()
    nc = p.finish()
    in_maps = []
    for c in range(NCORES):
        b, s0 = c // 4, (c % 4) * TC
        in_maps.append({
            "x": np.ascontiguousarray(x_full[b, s0:s0 + TC, :]),
            "wqkv0": wqkv0a,
            "qw": np.ascontiguousarray(qw.reshape(16, 128).T),
            "kw": np.ascontiguousarray(kw.reshape(16, 128).T),
            "cosr": np.ascontiguousarray(np.concatenate([cosf[s0:s0 + TC, :].T] * 2, 0)),
            "sinr": np.ascontiguousarray(np.concatenate([sinf[s0:s0 + TC, :].T] * 2, 0)),
            "amask": _l0_mask(c),
            "wo": wo,
            "up0s": np.ascontiguousarray(up0[c * 1024:(c + 1) * 1024, :]),
            "dn0s": np.ascontiguousarray(dn0[c * 256:(c + 1) * 256, :]),
            "w1s": np.ascontiguousarray(w1[c * 1024:(c + 1) * 1024, :]),
            "w2s": np.ascontiguousarray(w2[c * 256:(c + 1) * 256, :]),
            "upsc": np.ascontiguousarray(upsc.reshape(64, 128).T),
            "dsc": np.ascontiguousarray(dsc.reshape(16, 128).T),
            "osc": np.ascontiguousarray(osc.reshape(1, 1)),
            "wqkv1": wqkv1a,
            "bias1": bias1,
            "lmask": _l1_mask(c),
            "b1": np.ascontiguousarray(b1.reshape(64, 128).T),
            "b2": np.ascontiguousarray(b2.reshape(16, 128).T),
        })
    res = run_bass_kernel_spmd(nc, in_maps, core_ids=list(range(NCORES))).results
    out = np.concatenate([np.asarray(r["out"]) for r in res], axis=0)
    return out.reshape(B, S, D)


# version 14 (final)
# speedup vs baseline: 1.0127x; 1.0098x over previous
import numpy as np
import ml_dtypes
import concourse.bass as bass
import concourse.mybir as mybir
from concourse.bass_utils import run_bass_kernel_spmd
from concourse.expressions import smax, smin
from contextlib import ExitStack

F32 = mybir.dt.float32
BF16 = mybir.dt.bfloat16
AF = mybir.ActivationFunctionType
ALU = mybir.AluOpType
AXL = mybir.AxisListType
BF = ml_dtypes.bfloat16

NCORES = 8
B, S, D, NH, HD = 2, 4096, 2048, 16, 128
TC = 1024
DC = D // 128
HC = 64
SM_SCALE = 1.0 / float(np.sqrt(HD))
NEG = -1e9

RGB = [[0, 1, 2, 3], [4, 5, 6, 7]]

IDN_BF = np.eye(128, dtype=BF)
IDN_F32 = np.eye(128, dtype=np.float32)
IOTA64 = np.broadcast_to(np.arange(64, dtype=np.float32), (128, 64)).astype(BF).copy()

_ENG = {"pe": "tensor", "dve": "vector", "act": "scalar", "pool": "gpsimd",
        "sp": "sync"}


class P:
    def __init__(self, name="prog"):
        self.nc = bass.Bass("TRN2", target_bir_lowering=False)
        self.ctx = ExitStack()
        self.scopes = []
        self.S = {}
        self.c = {}
        self.block = None

    def start(self):
        e = self.ctx.enter_context
        for s in ("pe", "dve", "act", "pool", "ds", "dg"):
            self.S[s] = e(self.nc.semaphore(s))
            self.c[s] = 0
        self.block = e(self.nc.Block())

    def sem(self, name):
        if name not in self.S:
            self.S[name] = self.ctx.enter_context(self.nc.semaphore(name))
            self.c[name] = 0
        return self.S[name]

    def finish(self):
        while self.scopes:
            self.pop()
        self.ctx.close()
        return self.nc

    def push(self):
        self.scopes.append(ExitStack())

    def pop(self):
        self.scopes.pop().close()

    def _enter(self, cm):
        tgt = self.scopes[-1] if self.scopes else self.ctx
        return tgt.enter_context(cm)

    def sb(self, name, shape, dtype):
        return self._enter(self.nc.sbuf_tensor(name, shape, dtype))

    def ps(self, name, shape, dtype=F32):
        return self._enter(self.nc.psum_tensor(name, shape, dtype))

    def dram(self, name, shape, dtype=F32, kind="Internal"):
        return self.nc.dram_tensor(name, shape, dtype, kind=kind)

    def const(self, arr, name):
        return self.nc.inline_tensor(np.ascontiguousarray(arr), name)

    def sec(self, eng, fn):
        getattr(self.block, _ENG[eng])(fn)

    def inc(self, key, inst, n=1):
        inst.then_inc(self.S[key], n)
        self.c[key] += n
        return self.c[key]

    def dma(self, key, inst):
        return self.inc(key, inst, 16)

    def w(self, eng_obj, key, count=None):
        if count is None:
            count = self.c[key]
        if count > 0:
            eng_obj.wait_ge(self.S[key], count)

    def sw(self, eng_obj, key):
        self.w(eng_obj, key)

    def snap(self, *keys):
        if not keys:
            return dict(self.c)
        return {k: self.c[k] for k in keys}

    def dyn(self, static=False):
        if static:
            return {"o128l": 0, "o128r": 128, "oTl": 0, "oTl960": 960,
                    "oTr": TC, "oTo": 0}
        pid = self.nc.partition_id(engines=[mybir.EngineType.Pool])
        q4 = pid % 4
        lft = smax(q4 - 1, 0)
        rgt = smin(q4 + 1, 3)
        return {
            "o128l": lft * 128, "o128r": rgt * 128,
            "oTl": lft * TC, "oTl960": lft * TC + 960, "oTr": rgt * TC,
            "oTo": q4 * TC,
        }


def emit_cc(p, g, ain, aout, semname, kind="AllGather", alu=ALU.bypass,
            rg=None, simcc=False):
    p.sem(semname)
    if simcc:
        rr = ain.shape[0]
        for r in range(aout.shape[0] // rr):
            p.dma(semname, g.dma_start(out=aout[r * rr:(r + 1) * rr, :], in_=ain[:, :]))
    else:
        if rg is None:
            rg = [list(range(NCORES))]
        g.collective_compute(
            kind, alu, replica_groups=rg,
            ins=[ain.ap().opt()], outs=[aout.ap().opt()]).then_inc(p.S[semname], 1)
        p.c[semname] += 1


def phase_fence(p):
    sn = p.snap()
    dma_sems = [k for k in p.S
                if k not in ("pe", "dve", "act", "pool") and not k.startswith("cc")]

    def body(sy, sn=sn, dma_sems=dma_sems):
        for k in ("pe", "dve", "act", "pool"):
            p.w(sy, k, sn[k])
        for k in dma_sems:
            p.w(sy, k, sn[k])
    p.sec("sp", body)

    def body(g, sn=sn, dma_sems=dma_sems):
        for k in ("pe", "dve", "act"):
            p.w(g, k, sn[k])
        for k in dma_sems:
            p.w(g, k, sn[k])
    p.sec("pool", body)

    def body(v, sn=sn, dma_sems=dma_sems):
        for k in dma_sems:
            p.w(v, k, sn[k])
    p.sec("dve", body)

    def body(a, sn=sn, dma_sems=dma_sems):
        for k in dma_sems:
            p.w(a, k, sn[k])
    p.sec("act", body)



def alloc_ln_scr(p, pre):
    return {
        "s1": p.sb(pre + "lr_s1", [128, 1], F32),
        "s2": p.sb(pre + "lr_s2", [128, 1], F32),
        "negm": p.sb(pre + "lr_negm", [128, 1], F32),
        "m2": p.sb(pre + "lr_m2", [128, 1], F32),
        "var": p.sb(pre + "lr_var", [128, 1], F32),
        "rstd": p.sb(pre + "lr_rstd", [128, 1], F32),
        "sqd": p.sb(pre + "lr_sqd", [128, D], BF16),
        "xt": p.sb(pre + "lr_xt", [128, D], F32),
        "xo": p.sb(pre + "lr_xo", [128, D], F32),
    }


def emit_ln_residual(p, scr, src_sb, x, xo, tt, src_ready):
    lpar = "dxa%d" % (tt % 2)

    def body(sy, tt=tt, lpar=lpar):
        p.w(sy, "dve")
        p.w(sy, "dst")
        p.w(sy, lpar)
        p.dma(lpar, sy.dma_start(out=scr["xt"][:, :], in_=x[tt * 128:(tt + 1) * 128, :]))
    p.sec("sp", body)
    dl = p.snap()

    def body(v, dl=dl, src_ready=src_ready):
        p.w(v, "dve", src_ready["dve"])
        p.inc("dve", v.reduce_sum(scr["s1"][:, :], src_sb[:, :], axis=AXL.X))
    p.sec("dve", body)

    def body(a, src_ready=src_ready):
        p.w(a, "dve", src_ready["dve"])
        p.w(a, "act", src_ready["act"])
        p.inc("act", a.activation(scr["sqd"][:, :], src_sb[:, :], AF.Square,
                                  accum_out=scr["s2"][:, :]))
    p.sec("act", body)
    stat = p.snap()

    def body(v, stat=stat, dl=dl, lpar=lpar):
        p.w(v, "act", stat["act"])
        p.w(v, lpar, dl[lpar])
        p.w(v, "dst", dl["dst"])
        p.w(v, "dve")
        p.inc("dve", v.tensor_scalar(scr["negm"][:, :], scr["s1"][:, :], -1.0 / D, None, op0=ALU.mult))
        p.sw(v, "dve")
        p.inc("dve", v.tensor_mul(scr["m2"][:, :], scr["negm"][:, :], scr["negm"][:, :]))
        p.inc("dve", v.tensor_scalar(scr["var"][:, :], scr["s2"][:, :], 1.0 / D, 1e-5,
                                     op0=ALU.mult, op1=ALU.add))
        p.sw(v, "dve")
        p.inc("dve", v.tensor_tensor(scr["var"][:, :], scr["var"][:, :], scr["m2"][:, :], ALU.subtract))
    p.sec("dve", body)

    def body(a):
        p.w(a, "dve")
        p.inc("act", a.activation(scr["var"][:, :], scr["var"][:, :], AF.Sqrt))
    p.sec("act", body)

    def body(v):
        p.w(v, "act")
        p.inc("dve", v.reciprocal(scr["rstd"][:, :], scr["var"][:, :]))
        p.sw(v, "dve")
        p.inc("dve", v.tensor_scalar(scr["xo"][:, :], src_sb[:, :], scr["negm"][:, :],
                                     scr["rstd"][:, :], op0=ALU.add, op1=ALU.mult))
        p.sw(v, "dve")
        p.inc("dve", v.tensor_tensor(scr["xo"][:, :], scr["xo"][:, :], scr["xt"][:, :], ALU.add))
    p.sec("dve", body)
    done = p.snap()

    def body(sy, tt=tt, done=done):
        p.w(sy, "dve", done["dve"])
        p.dma("dst", sy.dma_start(out=xo[tt * 128:(tt + 1) * 128, :], in_=scr["xo"][:, :]))
    p.sec("sp", body)



def emit_preln(p, scr2, x, sb_xn, sb_xnT, sb_idn, ps_t, ntile, per_t):
    for tt in range(ntile):
        lpar = "dxa%d" % (tt % 2)

        def body(sy, tt=tt, lpar=lpar):
            p.w(sy, "dve")
            p.w(sy, "dst")
            p.w(sy, lpar)
            p.dma(lpar, sy.dma_start(out=scr2["xt"][:, :], in_=x[tt * 128:(tt + 1) * 128, :]))
        p.sec("sp", body)
        dl = p.snap()

        def body(v, dl=dl, lpar=lpar):
            p.w(v, lpar, dl[lpar])
            p.inc("dve", v.reduce_sum(scr2["s1"][:, :], scr2["xt"][:, :], axis=AXL.X))
        p.sec("dve", body)

        def body(a, dl=dl, lpar=lpar):
            p.w(a, lpar, dl[lpar])
            p.w(a, "act")
            p.inc("act", a.activation(scr2["sqd"][:, :], scr2["xt"][:, :], AF.Square,
                                      accum_out=scr2["s2"][:, :]))
        p.sec("act", body)
        stat = p.snap()

        def body(v, stat=stat):
            p.w(v, "act", stat["act"])
            p.w(v, "dve")
            p.inc("dve", v.tensor_scalar(scr2["negm"][:, :], scr2["s1"][:, :], -1.0 / D, None, op0=ALU.mult))
            p.sw(v, "dve")
            p.inc("dve", v.tensor_mul(scr2["m2"][:, :], scr2["negm"][:, :], scr2["negm"][:, :]))
            p.inc("dve", v.tensor_scalar(scr2["var"][:, :], scr2["s2"][:, :], 1.0 / D, 1e-5,
                                         op0=ALU.mult, op1=ALU.add))
            p.sw(v, "dve")
            p.inc("dve", v.tensor_tensor(scr2["var"][:, :], scr2["var"][:, :], scr2["m2"][:, :], ALU.subtract))
        p.sec("dve", body)

        def body(a):
            p.w(a, "dve")
            p.inc("act", a.activation(scr2["var"][:, :], scr2["var"][:, :], AF.Sqrt))
        p.sec("act", body)

        def body(v, tt=tt):
            p.w(v, "act")
            p.w(v, "pe")
            p.inc("dve", v.reciprocal(scr2["rstd"][:, :], scr2["var"][:, :]))
            p.sw(v, "dve")
            p.inc("dve", v.tensor_scalar(sb_xn[:, tt * D:(tt + 1) * D], scr2["xt"][:, :],
                                         scr2["negm"][:, :], scr2["rstd"][:, :],
                                         op0=ALU.add, op1=ALU.mult))
        p.sec("dve", body)

    for tt in range(ntile):
        for dc4 in range(0, DC, 4):
            def body(t, tt=tt, dc4=dc4):
                p.w(t, "dve")
                for jj in range(4):
                    dc = dc4 + jj
                    mmi = t.transpose(ps_t[:, jj * 128:(jj + 1) * 128],
                                      sb_xn[:, tt * D + dc * 128: tt * D + (dc + 1) * 128],
                                      sb_idn[:, :])
                p.inc("pe", mmi)
            p.sec("pe", body)
            pl = p.snap()

            def body(v, tt=tt, dc4=dc4, pl=pl):
                p.w(v, "pe", pl["pe"])
                dst = sb_xnT[:, :].rearrange("p (c t) -> p c t", t=per_t)[
                    :, dc4:dc4 + 4, tt * 128:(tt + 1) * 128]
                p.inc("dve", v.tensor_copy(dst, ps_t[:, :].rearrange(
                    "p (c t) -> p c t", t=128)[:, 0:4, :]))
            p.sec("dve", body)
    return p.snap()



def emit_prep(p, io):
    nc = p.nc
    sb_idn = io["sb_idn"]
    sb_onerow = io["sb_onerow"]
    sb_onef = io["sb_onef"]
    p.push()
    sb_in = [p.sb("pp_in0", [128, 8192], F32), p.sb("pp_in1", [128, 8192], F32)]
    sb_q = p.sb("pp_q", [128, 8192], BF16)
    sb_s2 = p.sb("pp_s2", [128, 8192], BF16)
    sb_abs = p.sb("pp_abs", [128, 1], F32)
    sb_acc = p.sb("pp_acc", [128, 2], F32)
    sb_tot = p.sb("pp_tot", [1, 2], F32)
    sb_th = p.sb("pp_th", [128, 2], F32)
    sb_thn = p.sb("pp_thn", [128, 2], F32)
    sb_stage = p.sb("pp_stage", [128, 16384], BF16)
    ps_t = p.ps("pp_pst", [128, 512], BF16)
    ps_f = p.ps("pp_psf", [128, 2], F32)

    def body(g):
        p.inc("pool", g.memset(sb_acc[:, :], 0.0))
    p.sec("pool", body)

    ldi = [0]
    in_act = {}
    for mi, tname in enumerate(["w1s", "w2s"]):
        t = io[tname]
        rows, width = (1024, 2048) if mi == 0 else (256, 8192)
        for i in range(rows // 128):
            idx = ldi[0]
            buf = idx % 2
            lpar = "dxa%d" % buf
            ldi[0] += 1

            def body(sy, t=t, i=i, width=width, lpar=lpar, g_=in_act.get(idx - 2, 0)):
                p.w(sy, "act", g_)
                p.w(sy, lpar)
                p.dma(lpar, sy.dma_start(out=sb_in[buf][:, :width],
                                         in_=t[i * 128:(i + 1) * 128, :]))
            p.sec("sp", body)
            dl = p.snap()

            def body(a, width=width, dl=dl, lpar=lpar, buf=buf):
                p.w(a, lpar, dl[lpar])
                p.w(a, "dve")
                p.inc("act", a.activation(sb_q[:, :width], sb_in[buf][:, :width],
                                          AF.Abs, accum_out=sb_abs[:, :]))
            p.sec("act", body)
            in_act[idx] = p.c["act"]

            def body(v, mi=mi):
                p.w(v, "act")
                p.w(v, "pool")
                p.inc("dve", v.tensor_tensor(sb_acc[:, mi:mi + 1], sb_acc[:, mi:mi + 1],
                                             sb_abs[:, :], ALU.add))
            p.sec("dve", body)

    def body(t):
        p.w(t, "dve")
        p.w(t, "pool")
        p.inc("pe", t.matmul(ps_f[0:1, :], sb_onef[:, :], sb_acc[:, :],
                             start=True, stop=True))
    p.sec("pe", body)

    def body(v):
        p.w(v, "pe")
        p.inc("dve", v.tensor_copy(sb_tot[:, :], ps_f[0:1, :]))
    p.sec("dve", body)

    p.sem("ccth")

    def body(g):
        p.w(g, "dve")
        p.dma("dg", g.dma_start(out=io["arth_in"].ap(), in_=sb_tot[:, :]))
        p.w(g, "dg")
        emit_cc(p, g, io["arth_in"], io["arth_out"], "ccth",
                kind="AllReduce", alu=ALU.add, simcc=io.get("simcc", False))
        p.w(g, "ccth")
        p.dma("dg", g.dma_start(out=sb_tot[:, :], in_=io["arth_out"].ap()))
    p.sec("pool", body)
    dl_ar = p.snap()

    jobs = [
        ("up0s", "agu0", 1024, 2048, None),
        ("dn0s", "agd0", 256, 8192, None),
        ("w1s", "agw1", 1024, 2048, 0),
        ("w2s", "agw2", 256, 8192, 1),
    ]
    ag_emitters = []
    prev_ev_box = [None]

    def emit_job(jobi, thw):
        tname, agname, rows, width, thi = jobs[jobi]
        prev_ev = prev_ev_box[0]
        t = io[tname]
        agin = io[agname + "_in"]
        agout = io[agname + "_out"]
        ntile = rows // 128
        nj = width // 128
        evsem = "dga%d" % jobi
        p.sem(evsem)
        for i in range(ntile):
            idx = ldi[0]
            buf = idx % 2
            lpar = "dxa%d" % buf
            ldi[0] += 1

            def body(sy, t=t, i=i, width=width, lpar=lpar, buf=buf,
                     g_=in_act.get(idx - 2, 0)):
                p.w(sy, "act", g_)
                p.w(sy, lpar)
                p.dma(lpar, sy.dma_start(out=sb_in[buf][:, :width],
                                         in_=t[i * 128:(i + 1) * 128, :]))
            p.sec("sp", body)
            dl = p.snap()

            if thi is None:
                def body(a, width=width, dl=dl, lpar=lpar, buf=buf, tg=p.c["pe"]):
                    p.w(a, lpar, dl[lpar])
                    p.w(a, "pe", tg)
                    p.w(a, "dve", thw)
                    p.inc("act", a.activation(sb_q[:, :width], sb_in[buf][:, :width], AF.Sign))
                p.sec("act", body)
                qready, qsem = p.c["act"], "act"
            else:
                def body(a, width=width, dl=dl, thi=thi, lpar=lpar, buf=buf, tg=p.c["pe"]):
                    p.w(a, lpar, dl[lpar])
                    p.w(a, "pe", tg)
                    p.w(a, "dve", thw)
                    p.inc("act", a.activation(sb_q[:, :width], sb_in[buf][:, :width], AF.Sign,
                                              bias=sb_thn[:, thi:thi + 1]))
                    p.inc("act", a.activation(sb_s2[:, :width], sb_in[buf][:, :width], AF.Sign,
                                              bias=sb_th[:, thi:thi + 1]))
                p.sec("act", body)
                al = p.snap()

                def body(v, width=width, al=al):
                    p.w(v, "act", al["act"])
                    p.inc("dve", v.tensor_tensor(sb_q[:, :width], sb_q[:, :width],
                                                 sb_s2[:, :width], ALU.add))
                    p.sw(v, "dve")
                    p.inc("dve", v.tensor_scalar(sb_q[:, :width], sb_q[:, :width],
                                                 0.5, None, op0=ALU.mult))
                p.sec("dve", body)
                qready, qsem = p.c["dve"], "dve"
            in_act[idx] = p.c["act"]

            for j4 in range(0, nj, 4):
                def body(t_, j4=j4, qready=qready, qsem=qsem):
                    p.w(t_, "ds", io["boot_ds"])
                    p.w(t_, qsem, qready)
                    p.w(t_, "dve")
                    for jj in range(4):
                        j = j4 + jj
                        mmi = t_.transpose(ps_t[:, jj * 128:(jj + 1) * 128],
                                           sb_q[:, j * 128:(j + 1) * 128],
                                           sb_idn[:, :])
                    p.inc("pe", mmi)
                p.sec("pe", body)
                pl = p.snap()

                def body(v, j4=j4, i=i, rows=rows, pl=pl, prev_ev=prev_ev):
                    p.w(v, "pe", pl["pe"])
                    if prev_ev is not None:
                        p.w(v, prev_ev[0], prev_ev[1])
                    dst = sb_stage[:, :].rearrange("p (j r) -> p j r", r=rows)[
                        :, j4:j4 + 4, i * 128:(i + 1) * 128]
                    p.inc("dve", v.tensor_copy(dst, ps_t[:, :].rearrange(
                        "p (j r) -> p j r", r=128)[:, 0:4, :]))
                p.sec("dve", body)

        st = p.snap()
        ccname = "cc" + agname
        p.sem(ccname)

        def body(sy, agin=agin, ntile=ntile, rows=rows, st=st, evsem=evsem):
            p.w(sy, "dve", st["dve"])
            for i in range(ntile):
                p.dma(evsem, sy.dma_start(
                    out=agin[i * 128:(i + 1) * 128, :].rearrange("p (j r) -> p j r", r=128),
                    in_=sb_stage[:, :].rearrange("p (j r) -> p j r", r=rows)[:, :, i * 128:(i + 1) * 128]))
        p.sec("sp", body)
        stg = p.snap()

        def emit_ag(agin=agin, agout=agout, ccname=ccname, stg=stg, evsem=evsem):
            def body(g):
                p.w(g, evsem, stg[evsem])
                emit_cc(p, g, agin, agout, ccname, simcc=io.get("simcc", False))
            p.sec("pool", body)
        ag_emitters.append(emit_ag)
        prev_ev_box[0] = (evsem, p.c[evsem])

    emit_job(0, 0)
    emit_job(1, 0)

    def body(a, dl_ar=dl_ar):
        p.w(a, "dg", dl_ar["dg"])
        p.inc("act", a.mul(sb_tot[:, :], sb_tot[:, :], 0.7 / (8192.0 * 2048.0)))
    p.sec("act", body)
    mull = p.snap()

    def body(t, mull=mull):
        p.w(t, "act", mull["act"])
        p.inc("pe", t.matmul(ps_f[:, :], sb_onerow[:, :], sb_tot[0:1, :],
                             start=True, stop=True))
    p.sec("pe", body)

    def body(v):
        p.w(v, "pe")
        p.inc("dve", v.tensor_copy(sb_th[:, :], ps_f[:, :]))
        p.sw(v, "dve")
        p.inc("dve", v.tensor_scalar(sb_thn[:, :], sb_th[:, :], -1.0, None, op0=ALU.mult))
    p.sec("dve", body)
    th_ready = p.snap()

    emit_job(2, th_ready["dve"])
    emit_job(3, th_ready["dve"])
    p.pop()
    return ag_emitters



def emit_l0a(p, io):
    nc = p.nc
    sb_idn = io["sb_idn"]
    sb_onerow = io["sb_onerow"]
    sb_oneb = io["sb_oneb"]
    sb_qn = io["sb_qn"]
    x = io["x"]
    wqkv = io["wqkv0"]

    phase_fence(p)
    p.push()
    sb_xn = p.sb("a0_xn", [128, 8 * D], BF16)
    sb_xnT = p.sb("a0_xnT", [128, DC * TC], BF16)
    sb_kn = p.sb("a0_kn", [128, NH * TC], BF16)
    sb_vf = sb_xn
    sb_w = [p.sb("a0_w0", [128, D], BF16), p.sb("a0_w1", [128, D], BF16)]
    sb_qwsb = p.sb("a0_qw", [128, DC], F32)
    sb_kwsb = p.sb("a0_kw", [128, DC], F32)
    sb_cos = p.sb("a0_cos", [128, TC], BF16)
    sb_sin = p.sb("a0_sin", [128, TC], BF16)
    sb_rp = p.sb("a0_rp", [128, TC], BF16)
    sb_qsq = [p.sb("a0_qsq0", [128, TC], BF16), p.sb("a0_qsq1", [128, TC], BF16)]
    sb_sqrow = p.sb("a0_sqrow", [1, TC], F32)
    sb_vstage = p.sb("a0_vst", [128, D], BF16)
    scr2 = {k: p.sb("a0scr_" + k, [128, 1] if k not in ("sqd", "xt") else [128, D],
                    F32 if k != "sqd" else BF16)
            for k in ("s1", "s2", "negm", "m2", "var", "rstd", "sqd", "xt")}
    ps_a = [p.ps("a0_psa0", [128, 1024], F32), p.ps("a0_psa1", [128, 1024], F32)]
    ps_t = p.ps("a0_pst", [128, 512], BF16)
    ps_rb = p.ps("a0_psrb", [128, TC], F32)

    p.sem("dc"); p.sem("dw0"); p.sem("dw1")

    def body(sy):
        p.dma("ds", sy.dma_start(out=sb_qwsb[:, :], in_=io["qw"].ap()))
        p.dma("ds", sy.dma_start(out=sb_kwsb[:, :], in_=io["kw"].ap()))
    p.sec("sp", body)

    def body(g):
        p.dma("dc", g.dma_start(out=sb_cos[:, :], in_=io["cosr"].ap()))
        p.dma("dc", g.dma_start(out=sb_sin[:, :], in_=io["sinr"].ap()))
    p.sec("pool", body)
    consts = p.snap()

    xnT_done = emit_preln(p, scr2, x, sb_xn, sb_xnT, sb_idn, ps_t, 8, TC)

    p.sem("cckn"); p.sem("ccvt"); p.sem("dkn"); p.sem("dvt")
    evac_dve = {}
    mm_qkv = {}
    eidx = [0]

    def emit_qkv_group(jcs):
        for jc in jcs:
            idx = eidx[0]
            eidx[0] += 1
            par = idx % 2

            def body(g, jc=jc, par=par, idx=idx):
                p.w(g, "pe", mm_qkv.get(idx - 2, 0))
                p.dma("dw%d" % par, g.dma_start(
                    out=sb_w[par][:, :],
                    in_=wqkv[jc * 128:(jc + 1) * 128, :]))
            p.sec("pool", body)
            wl = p.snap()

            def body(t, par=par, wl=wl, idx=idx):
                p.w(t, "dw%d" % par, wl["dw%d" % par])
                p.w(t, "dve", max(xnT_done["dve"], evac_dve.get(idx - 2, 0)))
                for half in range(2):
                    for dc in range(DC):
                        lhs = sb_w[par][:, dc * 128:(dc + 1) * 128]
                        mmi = t.matmul(ps_a[par][:, half * 512:(half + 1) * 512], lhs,
                                       sb_xnT[:, dc * TC + half * 512: dc * TC + (half + 1) * 512],
                                       start=(dc == 0), stop=(dc == DC - 1))
                mm_qkv[idx] = p.inc("pe", mmi)
            p.sec("pe", body)
            ml = p.snap()

            def body(v, jc=jc, par=par, ml=ml, idx=idx):
                p.w(v, "pe", ml["pe"])
                if jc < DC:
                    dst = sb_qn[:, jc * TC:(jc + 1) * TC]
                elif jc < 2 * DC:
                    dst = sb_kn[:, (jc - DC) * TC:(jc - DC + 1) * TC]
                else:
                    dst = sb_vf[:, (jc - 2 * DC) * TC:(jc - 2 * DC + 1) * TC]
                evac_dve[idx] = p.inc("dve", v.tensor_copy(dst, ps_a[par][:, :]))
            p.sec("dve", body)

    def emit_rmsnorm(src, wsb, is_q):
        for h in range(NH):
            par = h % 2

            def body(a, h=h, par=par, src=src):
                p.w(a, "dve", p.c["dve"])
                if h >= 2:
                    p.w(a, "pe")
                p.inc("act", a.activation(sb_qsq[par][:, :], src[:, h * TC:(h + 1) * TC],
                                          AF.Square))
            p.sec("act", body)
            al = p.snap()

            def body(t, h=h, par=par, al=al, dve_now=p.c["dve"]):
                p.w(t, "act", al["act"])
                if h == 0:
                    p.w(t, "dve", dve_now)
                t.matmul(ps_rb[0:1, 0:512], sb_oneb[:, :], sb_qsq[par][:, 0:512],
                         start=(h == 0), stop=(h == NH - 1))
                p.inc("pe", t.matmul(ps_rb[0:1, 512:1024], sb_oneb[:, :],
                                     sb_qsq[par][:, 512:1024],
                                     start=(h == 0), stop=(h == NH - 1)))
            p.sec("pe", body)
        sq = p.snap()

        def body(v, sq=sq):
            p.w(v, "pe", sq["pe"])
            p.inc("dve", v.tensor_scalar(sb_sqrow[:, :], ps_rb[0:1, :], 1.0 / D, 1e-6,
                                         op0=ALU.mult, op1=ALU.add))
        p.sec("dve", body)

        def body(a):
            p.w(a, "dve")
            p.inc("act", a.activation(sb_sqrow[:, :], sb_sqrow[:, :], AF.Sqrt))
        p.sec("act", body)

        def body(v, is_q=is_q):
            p.w(v, "act")
            p.inc("dve", v.reciprocal(sb_sqrow[:, :], sb_sqrow[:, :]))
            if is_q:
                p.sw(v, "dve")
                p.inc("dve", v.tensor_scalar(sb_sqrow[:, :], sb_sqrow[:, :], SM_SCALE,
                                             None, op0=ALU.mult))
        p.sec("dve", body)
        rr = p.snap()

        def body(t, rr=rr):
            p.w(t, "dve", rr["dve"])
            t.matmul(ps_rb[:, 0:512], sb_onerow[:, :], sb_sqrow[0:1, 0:512],
                     start=True, stop=True)
            p.inc("pe", t.matmul(ps_rb[:, 512:1024], sb_onerow[:, :],
                                 sb_sqrow[0:1, 512:1024], start=True, stop=True))
        p.sec("pe", body)
        bb = p.snap()

        for h in range(NH):
            def body(v, h=h, src=src, wsb=wsb, bb=bb):
                p.w(v, "pe", bb["pe"])
                p.w(v, "ds", consts["ds"])
                p.inc("dve", v.scalar_tensor_tensor(
                    src[:, h * TC:(h + 1) * TC], src[:, h * TC:(h + 1) * TC],
                    wsb[:, h:h + 1], ps_rb[:, :],
                    op0=ALU.mult, op1=ALU.mult))
            p.sec("dve", body)

    def emit_rope(src):
        for h in range(NH):
            x1 = src[0:64, h * TC:(h + 1) * TC]
            x2 = src[64:128, h * TC:(h + 1) * TC]

            def body(g, x1=x1, x2=x2):
                p.w(g, "dc", consts["dc"])
                p.w(g, "dve")
                p.inc("pool", g.tensor_tensor(sb_rp[0:64, :], x2, sb_sin[64:128, :], ALU.mult))
                p.inc("pool", g.tensor_tensor(sb_rp[64:128, :], x1, sb_sin[0:64, :], ALU.mult))
            p.sec("pool", body)
            gl = p.snap()

            def body(v, x1=x1, x2=x2, gl=gl):
                p.w(v, "pool", gl["pool"])
                p.inc("dve", v.tensor_tensor(x1, x1, sb_cos[0:64, :], ALU.mult))
                p.inc("dve", v.tensor_tensor(x2, x2, sb_cos[64:128, :], ALU.mult))
                p.sw(v, "dve")
                p.inc("dve", v.tensor_tensor(x1, x1, sb_rp[0:64, :], ALU.subtract))
                p.inc("dve", v.tensor_tensor(x2, x2, sb_rp[64:128, :], ALU.add))
            p.sec("dve", body)

    emit_qkv_group(range(DC, 2 * DC))
    emit_rmsnorm(sb_kn, sb_kwsb, False)
    emit_rope(sb_kn)
    krope_done = p.snap()

    def body(sy, kd=krope_done):
        p.w(sy, "dve", kd["dve"])
        p.dma("dkn", sy.dma_start(out=io["agkn_in"].ap(), in_=sb_kn[:, :]))
    p.sec("sp", body)
    knst = p.snap()

    def body(g, knst=knst):
        p.w(g, "dkn", knst["dkn"])
        emit_cc(p, g, io["agkn_in"], io["agkn_out"], "cckn", rg=RGB,
                simcc=io.get("simcc", False))
    p.sec("pool", body)

    emit_qkv_group(range(2 * DC, 3 * DC))

    for tt in range(8):
        for dc4 in range(0, DC, 4):
            def body(t, tt=tt, dc4=dc4, dnow=p.c["dve"]):
                p.w(t, "dve", max(dnow, p.c["dve"]))
                for jj in range(4):
                    dc = dc4 + jj
                    mmi = t.transpose(ps_t[:, jj * 128:(jj + 1) * 128],
                                      sb_vf[:, dc * TC + tt * 128: dc * TC + (tt + 1) * 128],
                                      sb_idn[:, :])
                p.inc("pe", mmi)
            p.sec("pe", body)
            pl = p.snap()

            def body(v, dc4=dc4, pl=pl, dvt_now=p.c["dvt"]):
                p.w(v, "pe", pl["pe"])
                p.w(v, "dvt", dvt_now)
                p.inc("dve", v.tensor_copy(sb_vstage[:, dc4 * 128:(dc4 + 4) * 128], ps_t[:, :]))
            p.sec("dve", body)
        vl = p.snap()

        def body(sy, tt=tt, vl=vl):
            p.w(sy, "dve", vl["dve"])
            p.dma("dvt", sy.dma_start(out=io["agvt_in"][tt * 128:(tt + 1) * 128, :],
                                      in_=sb_vstage[:, :]))
        p.sec("sp", body)
    vst = p.snap()

    def body(g, vst=vst):
        p.w(g, "dvt", vst["dvt"])
        emit_cc(p, g, io["agvt_in"], io["agvt_out"], "ccvt", rg=RGB,
                simcc=io.get("simcc", False))
    p.sec("pool", body)

    emit_qkv_group(range(0, DC))
    emit_rmsnorm(sb_qn, sb_qwsb, True)
    emit_rope(sb_qn)
    p.pop()



def emit_l0b(p, io):
    nc = p.nc
    sb_idn = io["sb_idn"]
    sb_qn = io["sb_qn"]
    x = io["x"]
    xo = io["x1"]
    agkn = io["agkn_out"]
    agvt = io["agvt_out"]
    simcc = io.get("simcc", False)

    phase_fence(p)
    p.push()
    scr = alloc_ln_scr(p, "b0")
    sb_ctxT = p.sb("b0_ctxT", [128, NH * TC], BF16)
    sb_kh = [p.sb("b0_kh0", [128, 2048], BF16), p.sb("b0_kh1", [128, 2048], BF16)]
    sb_vh = [p.sb("b0_vh0", [128, 2048], BF16), p.sb("b0_vh1", [128, 2048], BF16)]
    sb_msk = p.sb("b0_msk", [128, 8 * 1152], BF16)
    sb_probs = [p.sb("b0_probs0", [128, 1152], BF16), p.sb("b0_probs1", [128, 1152], BF16)]
    sb_probsT = p.sb("b0_probsT", [128, 1152], BF16)
    sb_mx = p.sb("b0_mx", [128, 1], F32)
    sb_negmx = p.sb("b0_negmx", [128, 1], F32)
    sb_l = p.sb("b0_l", [128, 1], F32)
    sb_rl = p.sb("b0_rl", [128, 1], F32)
    sb_wo = p.sb("b0_wo", [128, DC * D], BF16)
    sb_at = p.sb("b0_at", [128, D], F32)
    ps_s = [p.ps("b0_pss0", [128, 1152], F32), p.ps("b0_pss1", [128, 1152], F32)]
    ps_pt = p.ps("b0_pspt", [128, 640], BF16)
    ps_c = p.ps("b0_psc", [128, 128], F32)

    p.sem("dmk"); p.sem("dwo"); p.sem("dk0"); p.sem("dk1"); p.sem("dsg")

    def body(g):
        p.dma("dmk", g.dma_start(out=sb_msk[:, :], in_=io["amask"].ap()))
        p.dma("dwo", g.dma_start(out=sb_wo[:, :].rearrange("p (c j) -> p c j", j=D),
                                 in_=io["wo"].ap().rearrange("(c p) j -> p c j", p=128)))
    p.sec("pool", body)
    consts = p.snap()

    def body(g):
        p.w(g, "cckn", p.c["cckn"])
        p.w(g, "ccvt", p.c["ccvt"])
        dy = io["dy"]
        p.dma("dsg", g.dma_start(out=io["stg_k0"][:, :],
                                 in_=agkn[bass.ds(dy["o128l"], 128), :]))
        p.dma("dsg", g.dma_start(out=io["stg_v0"][:, :],
                                 in_=agvt[bass.ds(dy["oTl"], TC), :]))
    p.sec("pool", body)
    stg = p.snap()

    exp2_act = {"prev": p.c["act"]}
    norm_dve0 = {}
    ptcopy_act = {"a": 0, "v": 0}
    last_ctx_dve = 0
    pv_pe = {}

    for h in range(NH):
        par = h % 2

        def body(g, h=h, par=par):
            p.w(g, "pe", pv_pe.get(h - 2, 0))
            p.w(g, "dsg", stg["dsg"])
            p.dma("dk%d" % par, g.dma_start(
                out=sb_kh[par][:, 0:1024],
                in_=io["stg_k0"][:, h * TC:(h + 1) * TC]))
            p.dma("dk%d" % par, g.dma_start(
                out=sb_kh[par][:, 1024:2048],
                in_=io["agkn_in"][:, h * TC:(h + 1) * TC]))
            p.dma("dk%d" % par, g.dma_start(
                out=sb_vh[par][:, 0:1024].rearrange("p (kt j) -> p kt j", j=128),
                in_=io["stg_v0"][:, h * 128:(h + 1) * 128]
                    .rearrange("(kt p) j -> p kt j", p=128)))
            p.dma("dk%d" % par, g.dma_start(
                out=sb_vh[par][:, 1024:2048].rearrange("p (kt j) -> p kt j", j=128),
                in_=io["agvt_in"][:, h * 128:(h + 1) * 128]
                    .rearrange("(kt p) j -> p kt j", p=128)))
        p.sec("pool", body)
        kl = p.c["dk%d" % par]

        for step in range(9):
            if step < 8:
                qt = step
                spar = qt % 2

                def body(t, h=h, qt=qt, par=par, kl=kl, spar=spar,
                         ea=exp2_act.get((h, qt - 2), exp2_act.get("prev", 0))):
                    p.w(t, "dk%d" % par, kl)
                    p.w(t, "act", ea)
                    p.w(t, "dmk", consts["dmk"])
                    for w0, w1 in [(0, 512), (512, 1024), (1024, 1152)]:
                        t.matmul(ps_s[spar][:, w0:w1], sb_idn[:, :],
                                 sb_msk[:, qt * 1152 + w0: qt * 1152 + w1],
                                 start=True, stop=False)
                    lhs = sb_qn[:, h * TC + qt * 128: h * TC + (qt + 1) * 128]
                    for w0, w1 in [(0, 512), (512, 1024), (1024, 1152)]:
                        mmi = t.matmul(ps_s[spar][:, w0:w1], lhs,
                                       sb_kh[par][:, qt * 128 + w0: qt * 128 + w1],
                                       start=False, stop=True)
                    p.inc("pe", mmi)
                p.sec("pe", body)
                sc = p.snap()

                def body(v, qt=qt, spar=spar, sc=sc):
                    p.w(v, "pe", sc["pe"])
                    p.w(v, "dve")
                    p.inc("dve", v.reduce_max(sb_mx[:, :], ps_s[spar][:, :], axis=AXL.X))
                    p.sw(v, "dve")
                    p.inc("dve", v.tensor_scalar(sb_negmx[:, :], sb_mx[:, :], -1.0,
                                                 None, op0=ALU.mult))
                p.sec("dve", body)
                mx_ = p.snap()

                def body(a, spar=spar, mx_=mx_):
                    p.w(a, "dve", mx_["dve"])
                    p.w(a, "act")
                    p.inc("act", a.activation(sb_probs[spar][:, :], ps_s[spar][:, :],
                                              AF.Exp, bias=sb_negmx[:, :],
                                              accum_out=sb_l[:, :]))
                p.sec("act", body)
                exp2_act[(h, qt)] = p.c["act"]
                e1 = p.snap()

                def body(v, spar=spar, e1=e1):
                    p.w(v, "act", e1["act"])
                    p.inc("dve", v.reciprocal(sb_rl[:, :], sb_l[:, :]))
                    p.sw(v, "dve")
                    p.inc("dve", v.tensor_scalar(sb_probs[spar][:, :], sb_probs[spar][:, :],
                                                 sb_rl[:, :], None, op0=ALU.mult))
                p.sec("dve", body)
                norm_dve0[(h, qt)] = p.c["dve"]

            if step >= 1:
                q = step - 1
                spar2 = q % 2
                for half, (t0, t1) in enumerate([(0, 5), (5, 9)]):
                    def body(t, q=q, spar2=spar2, t0=t0, t1=t1,
                             nd=norm_dve0[(h, q)], pc=ptcopy_act):
                        p.w(t, "dve", max(nd, pc["v"]))
                        for j in range(t0, t1):
                            mmi = t.transpose(ps_pt[:, (j - t0) * 128:(j - t0 + 1) * 128],
                                              sb_probs[spar2][:, j * 128:(j + 1) * 128],
                                              sb_idn[:, :])
                        p.inc("pe", mmi)
                    p.sec("pe", body)
                    tp = p.snap()

                    def body(v, t0=t0, t1=t1, tp=tp):
                        p.w(v, "pe", tp["pe"])
                        p.inc("dve", v.tensor_copy(
                            sb_probsT[:, t0 * 128:t1 * 128],
                            ps_pt[:, 0:(t1 - t0) * 128]))
                    p.sec("dve", body)
                    ptcopy_act["v"] = p.c["dve"]
                pt_done = p.snap()

                def body(t, q=q, par=par, kl=kl, pt_done=pt_done, lc=last_ctx_dve):
                    p.w(t, "dve", max(lc, pt_done["dve"]))
                    p.w(t, "dk%d" % par, kl)
                    for j in range(9):
                        mmi = t.matmul(ps_c[:, :],
                                       sb_vh[par][:, (q + j) * 128:(q + j + 1) * 128],
                                       sb_probsT[:, j * 128:(j + 1) * 128],
                                       start=(j == 0), stop=(j == 8))
                    p.inc("pe", mmi)
                p.sec("pe", body)
                pv = p.snap()

                def body(v, h=h, q=q, pv=pv):
                    p.w(v, "pe", pv["pe"])
                    p.inc("dve", v.tensor_copy(
                        sb_ctxT[:, h * TC + q * 128: h * TC + (q + 1) * 128], ps_c[:, :]))
                p.sec("dve", body)
                last_ctx_dve = p.c["dve"]
        pv_pe[h] = p.c["pe"]
        exp2_act["prev"] = exp2_act[(h, 7)]
    attn_done = p.snap()

    for tt in range(8):
        for half in range(2):
            def body(t, tt=tt, half=half, ad=attn_done):
                p.w(t, "dve", ad["dve"])
                p.w(t, "act", ad["act"])
                p.w(t, "dwo", consts["dwo"])
                p.w(t, "dve")
                for js in range(2):
                    for dc in range(DC):
                        lhs = sb_ctxT[:, dc * TC + tt * 128: dc * TC + (tt + 1) * 128]
                        mmi = t.matmul(
                            ps_s[0][:, js * 512:(js + 1) * 512], lhs,
                            sb_wo[:, dc * D + half * 1024 + js * 512: dc * D + half * 1024 + (js + 1) * 512],
                            start=(dc == 0), stop=(dc == DC - 1))
                p.inc("pe", mmi)
            p.sec("pe", body)
            wl = p.snap()

            def body(v, half=half, wl=wl):
                p.w(v, "pe", wl["pe"])
                p.w(v, "act")
                p.inc("dve", v.tensor_copy(sb_at[:, half * 1024:(half + 1) * 1024],
                                           ps_s[0][:, 0:1024]))
            p.sec("dve", body)
        src_ready = p.snap()
        emit_ln_residual(p, scr, sb_at, x, xo, tt, src_ready)
    p.pop()



def emit_ffn(p, io, layer):
    nc = p.nc
    sb_idn = io["sb_idn"]
    sb_onerow = io["sb_onerow"]
    x = io["x1"] if layer == 0 else io["x3"]
    xo = io["x2"] if layer == 0 else io["out"]
    wupT = io["agu0_out"] if layer == 0 else io["agw1_out"]
    wdnT = io["agd0_out"] if layer == 0 else io["agw2_out"]
    ccu = "ccagu0" if layer == 0 else "ccagw1"
    ccd = "ccagd0" if layer == 0 else "ccagw2"

    phase_fence(p)
    p.push()
    pre = "f%d_" % layer
    scr = alloc_ln_scr(p, pre)
    sb_xn = p.sb(pre + "xn", [128, 4 * D], BF16)
    sb_xnT = p.sb(pre + "xnT", [128, DC * 512], BF16)
    sb_hid = p.sb(pre + "hid", [128, HC * 512], BF16)
    sb_wu = [p.sb(pre + "wu%d" % i, [128, D], BF16) for i in range(4)]
    sb_wd = [p.sb(pre + "wd0", [128, 8192], BF16), p.sb(pre + "wd1", [128, 8192], BF16)]
    sb_dn = p.sb(pre + "dn", [128, DC * 512], BF16)
    sb_at = p.sb(pre + "at", [128, D], BF16)
    sb_sc1 = p.sb(pre + "sc1", [128, HC], F32)
    sb_sc2 = p.sb(pre + "sc2", [128, DC], F32)
    sb_osc = p.sb(pre + "osc", [1, 1], F32)
    scr2 = {k: p.sb(pre + "scr_" + k, [128, 1] if k not in ("sqd", "xt") else [128, D],
                    F32 if k != "sqd" else BF16)
            for k in ("s1", "s2", "negm", "m2", "var", "rstd", "sqd", "xt")}
    ps_u = [p.ps(pre + "psu0", [128, 512], F32), p.ps(pre + "psu1", [128, 512], F32)]
    ps_d = [p.ps(pre + "psd0", [128, 512], F32), p.ps(pre + "psd1", [128, 512], F32)]
    ps_t = p.ps(pre + "pst", [128, 512], BF16)
    ps_b = p.ps(pre + "psb", [128, 2], F32)
    for k in ("du0", "du1", "du2", "du3", "dd0", "dd1"):
        p.sem(k)

    def body(sy):
        if layer == 0:
            p.dma("ds", sy.dma_start(out=sb_sc1[:, :], in_=io["upsc"].ap()))
            p.dma("ds", sy.dma_start(out=sb_sc2[:, :], in_=io["dsc"].ap()))
            p.dma("ds", sy.dma_start(out=sb_osc[:, :], in_=io["osc"].ap()))
        else:
            p.dma("ds", sy.dma_start(out=sb_sc1[:, :], in_=io["b1"].ap()))
            p.dma("ds", sy.dma_start(out=sb_sc2[:, :], in_=io["b2"].ap()))
    p.sec("sp", body)
    consts = p.snap()

    if layer == 0:
        def body(t):
            p.w(t, "ds", consts["ds"])
            p.w(t, "pool", 1)
            p.inc("pe", t.matmul(ps_b[:, 0:1], sb_onerow[:, :], sb_osc[:, :],
                                 start=True, stop=True))
        p.sec("pe", body)

        def body(v):
            p.w(v, "pe")
            p.inc("dve", v.scalar_tensor_tensor(sb_sc2[:, :], sb_sc2[:, :], 1.0,
                                                ps_b[:, 0:1].to_broadcast([128, DC]),
                                                op0=ALU.mult, op1=ALU.mult))
        p.sec("dve", body)
    sc_ready = p.snap()

    mm_up = {}
    mm_dn = {}
    for h2 in range(2):
        xnT_done = emit_preln(p, scr2, x[h2 * 512: h2 * 512 + 512, :],
                              sb_xn, sb_xnT, sb_idn, ps_t, 4, 512)

        evac_act = {}
        for hc in range(HC):
            g_ = h2 * HC + hc
            par = hc % 2
            buf = g_ % 4
            rnk, hcl = hc // 8, hc % 8

            def body(g, buf=buf, rnk=rnk, hcl=hcl, g_=g_):
                p.w(g, "pe", mm_up.get(g_ - 4, 0))
                p.w(g, ccu, p.c[ccu])
                p.dma("du%d" % buf, g.dma_start(
                    out=sb_wu[buf][:, :],
                    in_=wupT[rnk * 1024 + hcl * 128: rnk * 1024 + (hcl + 1) * 128, :]))
            p.sec("pool", body)
            wl = p.snap()

            def body(t, buf=buf, par=par, wl=wl, xd=xnT_done,
                     ea=evac_act.get(hc - 2, 0), g_=g_):
                p.w(t, "du%d" % buf, wl["du%d" % buf])
                p.w(t, "dve", xd["dve"])
                p.w(t, "act", max(ea, sc_ready["act"]))
                for dc in range(DC):
                    mmi = t.matmul(ps_u[par][:, :], sb_wu[buf][:, dc * 128:(dc + 1) * 128],
                                   sb_xnT[:, dc * 512:(dc + 1) * 512],
                                   start=(dc == 0), stop=(dc == DC - 1))
                mm_up[g_] = p.inc("pe", mmi)
            p.sec("pe", body)
            ml = p.snap()

            def body(a, hc=hc, par=par, ml=ml):
                p.w(a, "pe", ml["pe"])
                p.w(a, "dve", sc_ready["dve"])
                p.w(a, "ds", consts["ds"])
                if layer == 0:
                    p.inc("act", a.activation(sb_hid[:, hc * 512:(hc + 1) * 512],
                                              ps_u[par][:, :], AF.Relu,
                                              scale=sb_sc1[:, hc:hc + 1]))
                else:
                    p.inc("act", a.activation(sb_hid[:, hc * 512:(hc + 1) * 512],
                                              ps_u[par][:, :], AF.Gelu,
                                              bias=sb_sc1[:, hc:hc + 1]))
            p.sec("act", body)
            evac_act[hc] = p.c["act"]
        hid_done = p.snap()

        evac_dve = {}
        for jc in range(DC):
            g_ = h2 * DC + jc
            par = jc % 2
            rnk, jcl = jc // 2, jc % 2

            def body(g, par=par, rnk=rnk, jcl=jcl, g_=g_):
                p.w(g, "pe", mm_dn.get(g_ - 2, 0))
                p.w(g, ccd, p.c[ccd])
                p.dma("dd%d" % par, g.dma_start(
                    out=sb_wd[par][:, :],
                    in_=wdnT[rnk * 256 + jcl * 128: rnk * 256 + (jcl + 1) * 128, :]))
            p.sec("pool", body)
            wl = p.snap()

            def body(t, par=par, wl=wl, hd=hid_done, ed=evac_dve.get(jc - 2, 0), g_=g_):
                p.w(t, "dd%d" % par, wl["dd%d" % par])
                p.w(t, "act", hd["act"])
                p.w(t, "dve", ed)
                for hc in range(HC):
                    mmi = t.matmul(ps_d[par][:, :], sb_wd[par][:, hc * 128:(hc + 1) * 128],
                                   sb_hid[:, hc * 512:(hc + 1) * 512],
                                   start=(hc == 0), stop=(hc == HC - 1))
                mm_dn[g_] = p.inc("pe", mmi)
            p.sec("pe", body)
            ml = p.snap()

            def body(v, jc=jc, par=par, ml=ml):
                p.w(v, "pe", ml["pe"])
                p.w(v, "ds", consts["ds"])
                op = ALU.mult if layer == 0 else ALU.add
                evac_dve[jc] = p.inc("dve", v.tensor_scalar(
                    sb_dn[:, jc * 512:(jc + 1) * 512], ps_d[par][:, :],
                    sb_sc2[:, jc:jc + 1], None, op0=op))
            p.sec("dve", body)
        dn_done = p.snap()

        for tl in range(4):
            tt = h2 * 4 + tl
            for jc4 in range(0, DC, 4):
                def body(t, tl=tl, jc4=jc4, dd=dn_done):
                    p.w(t, "dve", max(dd["dve"], p.c["dve"]))
                    for jj in range(4):
                        jc = jc4 + jj
                        mmi = t.transpose(ps_t[:, jj * 128:(jj + 1) * 128],
                                          sb_dn[:, jc * 512 + tl * 128: jc * 512 + (tl + 1) * 128],
                                          sb_idn[:, :])
                    p.inc("pe", mmi)
                p.sec("pe", body)
                pl = p.snap()

                def body(v, jc4=jc4, pl=pl):
                    p.w(v, "pe", pl["pe"])
                    p.w(v, "act")
                    p.inc("dve", v.tensor_copy(sb_at[:, jc4 * 128:(jc4 + 4) * 128], ps_t[:, :]))
                p.sec("dve", body)
            src_ready = p.snap()
            emit_ln_residual(p, scr, sb_at, x, xo, tt, src_ready)
    p.pop()



def emit_l1a(p, io):
    nc = p.nc
    sb_idn = io["sb_idn"]
    sb_oneb = io["sb_oneb"]
    sb_qf = io["sb_qf"]
    sb_qt = io["sb_qt"]
    x = io["x2"]
    wqkv = io["wqkv1"]

    phase_fence(p)
    p.push()
    sb_xn = p.sb("a1_xn", [128, 8 * D], BF16)
    sb_xnT = p.sb("a1_xnT", [128, DC * TC], BF16)
    sb_kn = p.sb("a1_kn", [128, NH * TC], BF16)
    sb_vf = sb_xn
    sb_w = [p.sb("a1_w0", [128, D], BF16), p.sb("a1_w1", [128, D], BF16)]
    sb_bias = p.sb("a1_bias", [128, 48], F32)
    sb_qsq = [p.sb("a1_qsq0", [128, TC], BF16), p.sb("a1_qsq1", [128, TC], BF16)]
    sb_nrow = p.sb("a1_nrow", [1, TC], F32)
    sb_vstage = p.sb("a1_vst", [128, D], BF16)
    scr2 = {k: p.sb("a1scr_" + k, [128, 1] if k not in ("sqd", "xt") else [128, D],
                    F32 if k != "sqd" else BF16)
            for k in ("s1", "s2", "negm", "m2", "var", "rstd", "sqd", "xt")}
    ps_a = [p.ps("a1_psa0", [128, 1024], F32), p.ps("a1_psa1", [128, 1024], F32)]
    ps_t = p.ps("a1_pst", [128, 512], BF16)
    ps_r = p.ps("a1_psr", [1, TC], F32)
    p.sem("dn")

    def body(sy):
        p.dma("ds", sy.dma_start(out=sb_bias[:, :], in_=io["bias1"].ap()))
    p.sec("sp", body)
    consts = p.snap()

    xnT_done = emit_preln(p, scr2, x, sb_xn, sb_xnT, sb_idn, ps_t, 8, TC)

    p.sem("dn"); p.sem("ccnrm"); p.sem("cckf1"); p.sem("ccvt1")
    p.sem("dkf"); p.sem("dvt1")
    evac_dve = {}
    mm_qkv = {}
    eidx = [0]

    def emit_qkv_group(jcs):
        for jc in jcs:
            idx = eidx[0]
            eidx[0] += 1
            par = idx % 2

            def body(g, jc=jc, par=par, idx=idx):
                p.w(g, "pe", mm_qkv.get(idx - 2, 0))
                p.dma("dw%d" % par, g.dma_start(
                    out=sb_w[par][:, :],
                    in_=wqkv[jc * 128:(jc + 1) * 128, :]))
            p.sec("pool", body)
            wl = p.snap()

            def body(t, par=par, wl=wl, idx=idx):
                p.w(t, "dw%d" % par, wl["dw%d" % par])
                p.w(t, "dve", max(xnT_done["dve"], evac_dve.get(idx - 2, 0)))
                for half in range(2):
                    for dc in range(DC):
                        lhs = sb_w[par][:, dc * 128:(dc + 1) * 128]
                        mmi = t.matmul(ps_a[par][:, half * 512:(half + 1) * 512], lhs,
                                       sb_xnT[:, dc * TC + half * 512: dc * TC + (half + 1) * 512],
                                       start=(dc == 0), stop=(dc == DC - 1))
                mm_qkv[idx] = p.inc("pe", mmi)
            p.sec("pe", body)
            ml = p.snap()

            def body(v, jc=jc, par=par, ml=ml, idx=idx):
                p.w(v, "pe", ml["pe"])
                p.w(v, "ds", consts["ds"])
                if jc < DC:
                    evac_dve[idx] = p.inc("dve", v.tensor_scalar(
                        sb_qf[:, jc * TC:(jc + 1) * TC], ps_a[par][:, :],
                        sb_bias[:, jc:jc + 1], SM_SCALE, op0=ALU.add, op1=ALU.mult))
                elif jc < 2 * DC:
                    evac_dve[idx] = p.inc("dve", v.tensor_scalar(
                        sb_kn[:, (jc - DC) * TC:(jc - DC + 1) * TC], ps_a[par][:, :],
                        sb_bias[:, jc:jc + 1], None, op0=ALU.add))
                else:
                    evac_dve[idx] = p.inc("dve", v.tensor_scalar(
                        sb_vf[:, (jc - 2 * DC) * TC:(jc - 2 * DC + 1) * TC], ps_a[par][:, :],
                        sb_bias[:, jc:jc + 1], None, op0=ALU.add))
            p.sec("dve", body)

    def emit_transposes(src, is_v):
        for tt in range(8):
            for dc4 in range(0, DC, 4):
                def body(t, tt=tt, dc4=dc4, src=src, dnow=p.c["dve"]):
                    p.w(t, "dve", max(dnow, p.c["dve"]))
                    for jj in range(4):
                        dc = dc4 + jj
                        mmi = t.transpose(ps_t[:, jj * 128:(jj + 1) * 128],
                                          src[:, dc * TC + tt * 128: dc * TC + (tt + 1) * 128],
                                          sb_idn[:, :])
                    p.inc("pe", mmi)
                p.sec("pe", body)
                pl = p.snap()

                if is_v:
                    def body(v, dc4=dc4, pl=pl, dvt_now=p.c["dvt1"]):
                        p.w(v, "pe", pl["pe"])
                        p.w(v, "dvt1", dvt_now)
                        p.inc("dve", v.tensor_copy(sb_vstage[:, dc4 * 128:(dc4 + 4) * 128], ps_t[:, :]))
                    p.sec("dve", body)
                else:
                    def body(v, tt=tt, dc4=dc4, pl=pl):
                        p.w(v, "pe", pl["pe"])
                        p.inc("dve", v.tensor_copy(
                            sb_qt[:, tt * D + dc4 * 128: tt * D + (dc4 + 4) * 128], ps_t[:, :]))
                    p.sec("dve", body)
            if is_v:
                vl = p.snap()

                def body(sy, tt=tt, vl=vl):
                    p.w(sy, "dve", vl["dve"])
                    p.dma("dvt1", sy.dma_start(out=io["agvt1_in"][tt * 128:(tt + 1) * 128, :],
                                               in_=sb_vstage[:, :]))
                p.sec("sp", body)

    emit_qkv_group(range(DC, 2 * DC))
    k_done = p.snap()

    def body(sy, kd=k_done):
        p.w(sy, "dve", kd["dve"])
        p.dma("dkf", sy.dma_start(out=io["agkf1_in"].ap(), in_=sb_kn[:, :]))
    p.sec("sp", body)
    knst = p.snap()

    def body(g, knst=knst):
        p.w(g, "dkf", knst["dkf"])
        emit_cc(p, g, io["agkf1_in"], io["agkf1_out"], "cckf1", rg=RGB,
                simcc=io.get("simcc", False))
    p.sec("pool", body)

    emit_qkv_group(range(0, DC))
    q_done = p.snap()

    for h in range(NH):
        par = h % 2

        def body(a, h=h, par=par, qd=q_done):
            p.w(a, "dve", qd["dve"])
            if h >= 2:
                p.w(a, "pe")
            p.inc("act", a.activation(sb_qsq[par][:, :], sb_qf[:, h * TC:(h + 1) * TC],
                                      AF.Square))
        p.sec("act", body)
        al = p.snap()

        def body(t, h=h, par=par, al=al, dve_now=p.c["dve"]):
            p.w(t, "act", al["act"])
            if h == 0:
                p.w(t, "dve", dve_now)
            if h > 0:
                p.w(t, "dve")
            t.matmul(ps_r[0:1, 0:512], sb_oneb[:, :], sb_qsq[par][:, 0:512],
                     start=True, stop=True)
            p.inc("pe", t.matmul(ps_r[0:1, 512:1024], sb_oneb[:, :], sb_qsq[par][:, 512:1024],
                                 start=True, stop=True))
        p.sec("pe", body)
        pe_l = p.snap()

        def body(v, h=h, pe_l=pe_l):
            p.w(v, "pe", pe_l["pe"])
            p.w(v, "dn", p.c["dn"])
            p.inc("dve", v.tensor_copy(sb_nrow[:, :], ps_r[0:1, :]))
        p.sec("dve", body)
        ne = p.snap()

        def body(sy, h=h, ne=ne):
            p.w(sy, "dve", ne["dve"])
            p.dma("dn", sy.dma_start(out=io["agnrm_in"].ap()[:, h * TC:(h + 1) * TC],
                                     in_=sb_nrow[:, :]))
        p.sec("sp", body)
    nrml = p.snap()

    def body(g, nrml=nrml):
        p.w(g, "dn", nrml["dn"])
        emit_cc(p, g, io["agnrm_in"], io["agnrm_out"], "ccnrm", rg=RGB,
                simcc=io.get("simcc", False))
    p.sec("pool", body)

    emit_transposes(sb_qf, False)

    emit_qkv_group(range(2 * DC, 3 * DC))
    emit_transposes(sb_vf, True)
    vst = p.snap()

    def body(g, vst=vst):
        p.w(g, "dvt1", vst["dvt1"])
        emit_cc(p, g, io["agvt1_in"], io["agvt1_out"], "ccvt1", rg=RGB,
                simcc=io.get("simcc", False))
    p.sec("pool", body)
    p.pop()



def emit_l1mid(p, io):
    nc = p.nc
    simcc = io.get("simcc", False)
    phase_fence(p)
    p.push()
    sb_n = p.sb("m_n", [16, S], F32)
    sb_z = p.sb("m_z", [16, S], F32)
    sb_mx = p.sb("m_mx", [16, 8], F32)
    sb_msk = p.sb("m_msk", [16, S], F32)
    sb_zero = p.sb("m_zero", [16, S], F32)
    sb_slot = p.sb("m_slot", [16, S], F32)
    sb_slown = io["sb_slotm"]
    p.sem("dnm"); p.sem("dsl")

    def body(g):
        p.w(g, "ccnrm", p.c["ccnrm"])
        p.dma("dnm", g.dma_start(
            out=sb_n[:, :].rearrange("h (r t) -> h r t", t=TC),
            in_=io["agnrm_out"].ap().rearrange("r (h t) -> h r t", h=16)))
        p.inc("pool", g.memset(sb_zero[:, :], 0.0))
    p.sec("pool", body)
    cl = p.snap()

    def body(v, cl=cl):
        p.w(v, "dnm", cl["dnm"])
        p.w(v, "pool", cl["pool"])
        src = sb_n
        for r in range(8):
            p.inc("dve", v.max(out=sb_mx[:, :], in_=src[:, :]))
            p.sw(v, "dve")
            if r == 7:
                p.inc("dve", v.memset(sb_mx[:, 6:8], -1e30))
                p.sw(v, "dve")
            p.inc("dve", v.match_replace(out=sb_z[:, :], in_to_replace=sb_mx[:, :],
                                         in_values=src[:, :], imm_value=-1e30))
            p.sw(v, "dve")
            src = sb_z
        p.inc("dve", v.tensor_scalar(sb_msk[:, :], sb_z[:, :], -1e29, None, op0=ALU.is_lt))
        p.sw(v, "dve")
        p.inc("dve", v.memset(sb_msk[:, 0:1], 1.0))
        p.inc("dve", v.memset(sb_msk[:, S - 1:S], 1.0))
        p.sw(v, "dve")
        p.inc("dve", v.tensor_tensor_scan(sb_slot[:, :], sb_zero[:, :], sb_msk[:, :],
                                          0.0, op0=ALU.add, op1=ALU.add))
        p.sw(v, "dve")
        p.inc("dve", v.tensor_mul(sb_slot[:, :], sb_slot[:, :], sb_msk[:, :]))
        p.sw(v, "dve")
        p.inc("dve", v.tensor_scalar(sb_slot[:, :], sb_slot[:, :], -1.0, None, op0=ALU.add))
    p.sec("dve", body)
    done = p.snap()

    def body(g, done=done):
        p.w(g, "dve", done["dve"])
        p.dma("dsl", g.dma_start(out=io["slotscr"].ap(), in_=sb_slot[:, :]))
        p.w(g, "dsl")
        p.dma("dsl", g.dma_start(out=sb_slown[:, :],
                                 in_=io["slotscr"].ap()[:, bass.ds(io["dy"]["oTo"], TC)]))
    p.sec("pool", body)
    p.pop()



def emit_l1b(p, io):
    nc = p.nc
    sb_idn = io["sb_idn"]
    sb_idnf = io["sb_idnf"]
    sb_iota = io["sb_iota"]
    sb_onerow = io["sb_onerow"]
    sb_qf = io["sb_qf"]
    sb_qt = io["sb_qt"]
    sb_slotm = io["sb_slotm"]
    x = io["x2"]
    xo = io["x3"]
    agkf = io["agkf1_out"]
    agvt = io["agvt1_out"]
    RG = [[0, 1, 2, 3], [4, 5, 6, 7]]
    simcc = io.get("simcc", False)

    phase_fence(p)
    p.push()
    scr = alloc_ln_scr(p, "b1")
    sb_ctx = p.sb("b1_ctx", [128, 8 * D], BF16)
    sb_kh = [p.sb("b1_kh0", [128, 1152], BF16), p.sb("b1_kh1", [128, 1152], BF16)]
    sb_vh = [p.sb("b1_vh0", [128, 1152], BF16), p.sb("b1_vh1", [128, 1152], BF16)]
    sb_kg = [p.sb("b1_kg0", [128, S], BF16), p.sb("b1_kg1", [128, S], BF16)]
    sb_vg = [p.sb("b1_vg0", [128, S], BF16), p.sb("b1_vg1", [128, S], BF16)]
    sb_msk = p.sb("b1_msk", [128, 8 * 256], BF16)
    sb_probs = [p.sb("b1_probs0", [128, 256], BF16), p.sb("b1_probs1", [128, 256], BF16)]
    sb_probsT = p.sb("b1_probsT", [128, 256], BF16)
    sb_mx = p.sb("b1_mx", [128, 1], F32)
    sb_negmx = p.sb("b1_negmx", [128, 1], F32)
    sb_l = p.sb("b1_l", [128, 1], F32)
    sb_rl = p.sb("b1_rl", [128, 1], F32)
    sb_slotm_tm = p.sb("b1_sltm", [128, 8 * 16], F32)
    sb_sct = p.sb("b1_sct", [128, NH * 8 * 64], BF16)
    sb_qg = p.sb("b1_qg", [64, 2048], F32)
    sb_qga = p.sb("b1_qga", [64, 2048], F32)
    sb_qgb = p.sb("b1_qgb", [64, 128], BF16)
    sb_qgT = p.sb("b1_qgT", [128, 64], BF16)
    sb_mh = p.sb("b1_mh", [64, 2], F32)
    sb_mrow = p.sb("b1_mrow", [64, 1], F32)
    sb_m64 = p.sb("b1_m64", [1, 64], F32)
    sb_mp = p.sb("b1_mp", [1, 1], F32)
    sb_negmb = p.sb("b1_negmb", [128, 1], F32)
    sb_lacc = p.sb("b1_lacc", [64, 8], F32)
    sb_lsum = p.sb("b1_lsum", [64, 1], F32)
    sb_grl = p.sb("b1_grl", [64, 1], F32)
    sb_dump = p.sb("b1_dump", [64, 2048], BF16)
    sb_pt = [p.sb("b1_pt0", [128, 256], BF16), p.sb("b1_pt1", [128, 256], BF16)]
    sb_ga = p.sb("b1_ga", [64, 129], BF16)
    sb_scc = p.sb("b1_scc", [64, 128], BF16)
    sb_om = p.sb("b1_om", [128, 1], F32)
    p.sem("dmk1"); p.sem("dq0"); p.sem("ccar")
    p.sem("dk0"); p.sem("dk1"); p.sem("dsl")

    def body(g):
        p.dma("dmk1", g.dma_start(out=sb_msk[:, :], in_=io["lmask"].ap()))
        p.inc("pool", g.memset(sb_ga[:, :], 1.0))
    p.sec("pool", body)
    consts = p.snap()

    scA = ExitStack()
    ps_trf = scA.enter_context(nc.psum_tensor("b1_pstrf", [128, 16], F32))
    ps_qg = scA.enter_context(nc.psum_tensor("b1_psqg", [64, 128], F32))
    ps_s = [scA.enter_context(nc.psum_tensor("b1_pss0", [128, 256], F32)),
            scA.enter_context(nc.psum_tensor("b1_pss1", [128, 256], F32))]
    ps_pt = scA.enter_context(nc.psum_tensor("b1_pspt", [128, 256], BF16))
    ps_c = scA.enter_context(nc.psum_tensor("b1_psc", [128, 128], F32))

    for tt in range(8):
        def body(t, tt=tt):
            p.w(t, "dsl", p.c["dsl"])
            if tt > 0:
                p.w(t, "dve")
            p.inc("pe", t.transpose(ps_trf[:, :], sb_slotm[:, tt * 128:(tt + 1) * 128],
                                    sb_idnf[0:16, 0:16]))
        p.sec("pe", body)
        pl = p.snap()

        def body(v, tt=tt, pl=pl):
            p.w(v, "pe", pl["pe"])
            p.inc("dve", v.tensor_copy(sb_slotm_tm[:, tt * 16:(tt + 1) * 16], ps_trf[:, :]))
        p.sec("dve", body)
    sl_done = p.snap()

    for h in range(NH):
        def body(v, h=h):
            p.w(v, "dve", sl_done["dve"])
            for tt in range(8):
                p.inc("dve", v.tensor_scalar(
                    sb_sct[:, (h * 8 + tt) * 64:(h * 8 + tt + 1) * 64],
                    sb_iota[:, :], sb_slotm_tm[:, tt * 16 + h: tt * 16 + h + 1],
                    None, op0=ALU.is_equal))
        p.sec("dve", body)
    sct_done = p.snap()

    for h in range(NH):
        def body(t, h=h, sd=sct_done):
            p.w(t, "dve", sd["dve"])
            if h > 0:
                p.w(t, "dve")
            for tt in range(8):
                mmi = t.matmul(ps_qg[:, :],
                               sb_sct[:, (h * 8 + tt) * 64:(h * 8 + tt + 1) * 64],
                               sb_qt[:, tt * D + h * 128: tt * D + (h + 1) * 128],
                               start=(tt == 0), stop=(tt == 7))
            p.inc("pe", mmi)
        p.sec("pe", body)
        ql = p.snap()

        def body(v, h=h, ql=ql):
            p.w(v, "pe", ql["pe"])
            p.inc("dve", v.tensor_copy(sb_qg[:, h * 128:(h + 1) * 128], ps_qg[:, :]))
        p.sec("dve", body)
    qg_done = p.snap()

    def body(g, qd=qg_done):
        p.w(g, "dve", qd["dve"])
        p.dma("dq0", g.dma_start(out=io["arqg_in"].ap(), in_=sb_qg[:, :]))
        p.w(g, "dq0", p.c["dq0"])
        emit_cc(p, g, io["arqg_in"], io["arqg_out"], "ccar",
                kind="AllReduce", alu=ALU.add, rg=RG, simcc=simcc)
    p.sec("pool", body)

    def body(g):
        p.w(g, "cckf1", p.c["cckf1"])
        p.w(g, "ccvt1", p.c["ccvt1"])
        dy = io["dy"]
        p.dma("dsg", g.dma_start(out=io["stg_k1l"][:, :],
                                 in_=agkf[bass.ds(dy["o128l"], 128), :]))
        p.dma("dsg", g.dma_start(out=io["stg_k1r"][:, :],
                                 in_=agkf[bass.ds(dy["o128r"], 128), :]))
        p.dma("dsg", g.dma_start(out=io["stg_v1l"][:, :],
                                 in_=agvt[bass.ds(dy["oTl960"], 64), :]))
        p.dma("dsg", g.dma_start(out=io["stg_v1r"][:, :],
                                 in_=agvt[bass.ds(dy["oTr"], 64), :]))
    p.sec("pool", body)
    stg1 = p.snap()

    exp_act1 = {"prev": p.c["act"]}
    ptc_dve1 = {0: 0}
    norm_dve1 = {}
    last_ctx_dve = 0
    pv_pe1 = {}
    for h in range(NH):
        par = h % 2

        def body(g, h=h, par=par):
            p.w(g, "pe", pv_pe1.get(h - 2, 0))
            p.w(g, "dsg", stg1["dsg"])
            p.dma("dk%d" % par, g.dma_start(
                out=sb_kh[par][:, 0:64],
                in_=io["stg_k1l"][:, h * TC + 960: h * TC + 1024]))
            p.dma("dk%d" % par, g.dma_start(
                out=sb_kh[par][:, 64:1088],
                in_=io["agkf1_in"][:, h * TC:(h + 1) * TC]))
            p.dma("dk%d" % par, g.dma_start(
                out=sb_kh[par][:, 1088:1152],
                in_=io["stg_k1r"][:, h * TC: h * TC + 64]))
            p.dma("dk%d" % par, g.dma_start(
                out=sb_vh[par][0:64, 0:128],
                in_=io["stg_v1l"][:, h * 128:(h + 1) * 128]))
            p.dma("dk%d" % par, g.dma_start(
                out=sb_vh[par][64:128, 0:128],
                in_=io["agvt1_in"][0:64, h * 128:(h + 1) * 128]))
            p.dma("dk%d" % par, g.dma_start(
                out=sb_vh[par][:, 128:1024].rearrange("p (kt j) -> p kt j", j=128),
                in_=io["agvt1_in"][64:960, h * 128:(h + 1) * 128]
                    .rearrange("(kt p) j -> p kt j", p=128)))
            p.dma("dk%d" % par, g.dma_start(
                out=sb_vh[par][0:64, 1024:1152],
                in_=io["agvt1_in"][960:1024, h * 128:(h + 1) * 128]))
            p.dma("dk%d" % par, g.dma_start(
                out=sb_vh[par][64:128, 1024:1152],
                in_=io["stg_v1r"][:, h * 128:(h + 1) * 128]))
        p.sec("pool", body)
        kl = p.c["dk%d" % par]

        for step in range(9):
            if step < 8:
                qt = step
                spar = qt % 2

                def body(t, h=h, qt=qt, par=par, kl=kl, spar=spar,
                         ea=exp_act1.get((h, qt - 2), exp_act1.get("prev", 0))):
                    p.w(t, "dk%d" % par, kl)
                    p.w(t, "act", ea)
                    p.w(t, "dmk1", consts["dmk1"])
                    t.matmul(ps_s[spar][:, 0:256], sb_idn[:, :],
                             sb_msk[:, qt * 256:(qt + 1) * 256],
                             start=True, stop=False)
                    p.inc("pe", t.matmul(ps_s[spar][:, 0:256],
                                         sb_qf[:, h * TC + qt * 128: h * TC + (qt + 1) * 128],
                                         sb_kh[par][:, qt * 128: qt * 128 + 256],
                                         start=False, stop=True))
                p.sec("pe", body)
                sc = p.snap()

                def body(v, qt=qt, spar=spar, sc=sc):
                    p.w(v, "pe", sc["pe"])
                    p.w(v, "dve")
                    p.inc("dve", v.reduce_max(sb_mx[:, :], ps_s[spar][:, 0:256], axis=AXL.X))
                    p.sw(v, "dve")
                    p.inc("dve", v.tensor_scalar(sb_negmx[:, :], sb_mx[:, :], -1.0,
                                                 None, op0=ALU.mult))
                p.sec("dve", body)
                mm_ = p.snap()

                def body(a, spar=spar, mm_=mm_):
                    p.w(a, "dve", mm_["dve"])
                    p.inc("act", a.activation(sb_probs[spar][:, :], ps_s[spar][:, 0:256],
                                              AF.Exp, bias=sb_negmx[:, :],
                                              accum_out=sb_l[:, :]))
                p.sec("act", body)
                ee = p.snap()
                exp_act1[(h, qt)] = ee["act"]

                def body(v, spar=spar, ee=ee):
                    p.w(v, "act", ee["act"])
                    p.inc("dve", v.reciprocal(sb_rl[:, :], sb_l[:, :]))
                    p.sw(v, "dve")
                    p.inc("dve", v.tensor_scalar(sb_probs[spar][:, :], sb_probs[spar][:, :],
                                                 sb_rl[:, :], None, op0=ALU.mult))
                p.sec("dve", body)
                norm_dve1[(h, qt)] = p.c["dve"]

            if step >= 1:
                q = step - 1
                spar2 = q % 2

                def body(t, q=q, spar2=spar2, nd=norm_dve1[(h, q)], pt=ptc_dve1[0]):
                    p.w(t, "dve", max(nd, pt))
                    for j in range(2):
                        mmi = t.transpose(ps_pt[:, j * 128:(j + 1) * 128],
                                          sb_probs[spar2][:, j * 128:(j + 1) * 128],
                                          sb_idn[:, :])
                    p.inc("pe", mmi)
                p.sec("pe", body)
                tp = p.snap()

                def body(v, tp=tp):
                    p.w(v, "pe", tp["pe"])
                    p.inc("dve", v.tensor_copy(sb_probsT[:, :], ps_pt[:, :]))
                p.sec("dve", body)
                ptc_dve1[0] = p.c["dve"]
                pte = p.snap()

                def body(t, q=q, par=par, kl=kl, pte=pte, lc=last_ctx_dve):
                    p.w(t, "dve", max(pte["dve"], lc))
                    p.w(t, "dk%d" % par, kl)
                    for j in range(2):
                        mmi = t.matmul(ps_c[:, :],
                                       sb_probsT[:, j * 128:(j + 1) * 128],
                                       sb_vh[par][:, (q + j) * 128:(q + j + 1) * 128],
                                       start=(j == 0), stop=(j == 1))
                    p.inc("pe", mmi)
                p.sec("pe", body)
                pv = p.snap()

                def body(v, h=h, q=q, pv=pv):
                    p.w(v, "pe", pv["pe"])
                    p.inc("dve", v.tensor_copy(
                        sb_ctx[:, q * D + h * 128: q * D + (h + 1) * 128], ps_c[:, :]))
                p.sec("dve", body)
                last_ctx_dve = p.c["dve"]
        pv_pe1[h] = p.c["pe"]
        exp_act1["prev"] = exp_act1[(h, 7)]
    local_done = p.snap()

    def body(g):
        p.w(g, "ccar")
        p.dma("dq0", g.dma_start(out=sb_qga[:, :], in_=io["arqg_out"].ap()))
    p.sec("pool", body)
    ar_l = p.snap()

    scA.close()
    ps_g = p.ps("b1_psg", [64, 2048], F32)
    ps_st = [p.ps("b1_psst0", [128, 256], F32), p.ps("b1_psst1", [128, 256], F32)]
    ps_gcbl = p.ps("b1_psgc", [128, 132], F32)
    ps_tr2 = p.ps("b1_pstr2", [128, 128], BF16)

    gpv_pe = {}
    p.sem("dkg0"); p.sem("dkg1")
    for h in range(NH):
        par = h % 2

        def body(g, h=h, par=par):
            p.w(g, "pe", gpv_pe.get(h - 2, 0))
            p.dma("dkg%d" % par, g.dma_start(
                out=sb_kg[par][:, :].rearrange("p (r t) -> p r t", t=TC),
                in_=agkf[:, h * TC:(h + 1) * TC].rearrange("(r p) t -> p r t", p=128)))
            p.dma("dkg%d" % par, g.dma_start(
                out=sb_vg[par][:, :].rearrange("p (kt j2) -> p kt j2", j2=128),
                in_=agvt[:, h * 128:(h + 1) * 128]
                    .rearrange("(kt p) j2 -> p kt j2", p=128)))
        p.sec("pool", body)
        kl = p.c["dkg%d" % par]

        def body(v, h=h, al=ar_l):
            p.w(v, "dq0", al["dq0"])
            p.w(v, "pe")
            p.inc("dve", v.tensor_copy(sb_qgb[:, :], sb_qga[:, h * 128:(h + 1) * 128]))
        p.sec("dve", body)
        cq = p.snap()

        def body(t, cq=cq):
            p.w(t, "dve", cq["dve"])
            p.inc("pe", t.transpose(ps_tr2[0:128, 0:64], sb_qgb[:, :], sb_idn[0:64, 0:64]))
        p.sec("pe", body)
        tq = p.snap()

        def body(v, tq=tq):
            p.w(v, "pe", tq["pe"])
            p.inc("dve", v.tensor_copy(sb_qgT[:, :], ps_tr2[0:128, 0:64]))
        p.sec("dve", body)
        qgT_done = p.snap()

        def emit_scores_half(half, dw, aw, kl=kl, par=par):
            def body(t, half=half, dw=dw, aw=aw, kl=kl, par=par):
                p.w(t, "dkg%d" % par, kl)
                p.w(t, "dve", dw)
                p.w(t, "act", aw)
                for i4 in range(4):
                    mmi = t.matmul(ps_g[:, i4 * 512:(i4 + 1) * 512], sb_qgT[:, :],
                                   sb_kg[par][:, half * 2048 + i4 * 512: half * 2048 + (i4 + 1) * 512],
                                   start=True, stop=True)
                p.inc("pe", mmi)
            p.sec("pe", body)
            return p.snap()

        s0 = emit_scores_half(0, qgT_done["dve"], p.c["act"])

        def body(v, s0=s0):
            p.w(v, "pe", s0["pe"])
            p.inc("dve", v.reduce_max(sb_mh[:, 0:1], ps_g[:, :], axis=AXL.X))
        p.sec("dve", body)
        m0 = p.snap()

        s1 = emit_scores_half(1, m0["dve"], p.c["act"])

        def body(v, s1=s1):
            p.w(v, "pe", s1["pe"])
            p.inc("dve", v.reduce_max(sb_mh[:, 1:2], ps_g[:, :], axis=AXL.X))
            p.sw(v, "dve")
            p.inc("dve", v.tensor_max(sb_mrow[:, :], sb_mh[:, 0:1], sb_mh[:, 1:2]))
        p.sec("dve", body)
        m1 = p.snap()

        def body(t, m1=m1):
            p.w(t, "dve", m1["dve"])
            p.inc("pe", t.transpose(ps_gcbl[0:1, 0:64], sb_mrow[:, :], sb_idnf[0:64, 0:64]))
        p.sec("pe", body)
        tm = p.snap()

        def body(v, tm=tm):
            p.w(v, "pe", tm["pe"])
            p.inc("dve", v.tensor_copy(sb_m64[:, :], ps_gcbl[0:1, 0:64]))
            p.sw(v, "dve")
            p.inc("dve", v.reduce_max(sb_mp[:, :], sb_m64[:, :], axis=AXL.X))
            p.sw(v, "dve")
            p.inc("dve", v.tensor_scalar(sb_mp[:, :], sb_mp[:, :], -1.0, None, op0=ALU.mult))
        p.sec("dve", body)
        mp = p.snap()

        def body(t, mp=mp):
            p.w(t, "dve", mp["dve"])
            p.inc("pe", t.matmul(ps_gcbl[0:128, 64:65], sb_onerow[:, :], sb_mp[:, :],
                                 start=True, stop=True))
        p.sec("pe", body)
        bb = p.snap()

        def body(v, bb=bb):
            p.w(v, "pe", bb["pe"])
            p.inc("dve", v.tensor_copy(sb_negmb[:, :], ps_gcbl[0:128, 64:65]))
        p.sec("dve", body)
        nb = p.snap()

        def body(a, nb=nb):
            p.w(a, "dve", nb["dve"])
            for i4 in range(4):
                p.inc("act", a.activation(sb_dump[:, i4 * 512:(i4 + 1) * 512],
                                          ps_g[:, i4 * 512:(i4 + 1) * 512],
                                          AF.Exp, bias=sb_negmb[0:64, :],
                                          accum_out=sb_lacc[:, 4 + i4:5 + i4]))
        p.sec("act", body)
        e1 = p.snap()

        s0b = emit_scores_half(0, p.c["dve"], e1["act"])

        def body(a, s0b=s0b):
            p.w(a, "pe", s0b["pe"])
            for i4 in range(4):
                p.inc("act", a.activation(sb_dump[:, i4 * 512:(i4 + 1) * 512],
                                          ps_g[:, i4 * 512:(i4 + 1) * 512],
                                          AF.Exp, bias=sb_negmb[0:64, :],
                                          accum_out=sb_lacc[:, i4:1 + i4]))
        p.sec("act", body)
        e0 = p.snap()

        def body(v, e0=e0):
            p.w(v, "act", e0["act"])
            p.inc("dve", v.reduce_sum(sb_lsum[:, :], sb_lacc[:, :], axis=AXL.X))
        p.sec("dve", body)
        lv = p.snap()

        def body(v, lv=lv):
            p.w(v, "dve", lv["dve"])
            p.inc("dve", v.reciprocal(sb_grl[:, :], sb_lsum[:, :]))
        p.sec("dve", body)
        rl_done = p.snap()

        exp_acts = {}
        for kgp in range(4):
            kt0 = kgp * 8

            def body(t, kt0=kt0, par=par, rd=rl_done, ea=exp_acts.get(kgp - 1, 0)):
                p.w(t, "dve", rd["dve"])
                p.w(t, "act", ea)
                for par2 in range(2):
                    for i in range(4):
                        kt = kt0 + par2 * 4 + i
                        mmi = t.matmul(ps_st[par2][:, i * 64:(i + 1) * 64],
                                       sb_kg[par][:, kt * 128:(kt + 1) * 128],
                                       sb_qgT[:, :], start=True, stop=True)
                p.inc("pe", mmi)
            p.sec("pe", body)
            stl = p.snap()

            def body(a, stl=stl):
                p.w(a, "pe", stl["pe"])
                p.inc("act", a.activation(sb_pt[0][:, :], ps_st[0][:, :], AF.Exp,
                                          bias=sb_negmb[:, :]))
                p.inc("act", a.activation(sb_pt[1][:, :], ps_st[1][:, :], AF.Exp,
                                          bias=sb_negmb[:, :]))
            p.sec("act", body)
            exp_acts[kgp] = p.c["act"]
            el = p.snap()

            def body(t, kt0=kt0, kgp=kgp, par=par, el=el):
                p.w(t, "act", el["act"])
                for par2 in range(2):
                    for i in range(4):
                        kt = kt0 + par2 * 4 + i
                        mmi = t.matmul(ps_gcbl[0:64, 0:128],
                                       sb_pt[par2][:, i * 64:(i + 1) * 64],
                                       sb_vg[par][:, kt * 128:(kt + 1) * 128],
                                       start=(kgp == 0 and par2 == 0 and i == 0),
                                       stop=(kgp == 3 and par2 == 1 and i == 3))
                p.inc("pe", mmi)
            p.sec("pe", body)
        pv_done = p.snap()
        gpv_pe[h] = pv_done["pe"]

        def body(v, pv_done=pv_done):
            p.w(v, "pe", pv_done["pe"])
            p.w(v, "act")
            p.w(v, "pool", consts["pool"])
            p.inc("dve", v.tensor_scalar(sb_ga[0:64, 0:128], ps_gcbl[0:64, 0:128],
                                         sb_grl[:, :], None, op0=ALU.mult))
        p.sec("dve", body)
        ga_done = p.snap()

        for tt in range(8):
            def body(t, h=h, tt=tt, ga=ga_done):
                p.w(t, "dve", max(ga["dve"], p.c["dve"]))
                p.inc("pe", t.transpose(ps_tr2[0:64, 0:128],
                                        sb_sct[:, (h * 8 + tt) * 64:(h * 8 + tt + 1) * 64],
                                        sb_idn[:, :]))
            p.sec("pe", body)
            t1 = p.snap()

            def body(v, t1=t1):
                p.w(v, "pe", t1["pe"])
                p.inc("dve", v.tensor_copy(sb_scc[:, :], ps_tr2[0:64, 0:128]))
            p.sec("dve", body)
            t2 = p.snap()

            def body(t, t2=t2):
                p.w(t, "dve", t2["dve"])
                p.inc("pe", t.matmul(ps_gcbl[0:128, 0:129], sb_scc[:, :], sb_ga[:, :],
                                     start=True, stop=True))
            p.sec("pe", body)
            t3 = p.snap()

            def body(v, h=h, tt=tt, t3=t3):
                p.w(v, "pe", t3["pe"])
                p.inc("dve", v.tensor_scalar(sb_om[:, :], ps_gcbl[0:128, 128:129],
                                             -1.0, 1.0, op0=ALU.mult, op1=ALU.add))
                p.sw(v, "dve")
                p.inc("dve", v.scalar_tensor_tensor(
                    sb_ctx[:, tt * D + h * 128: tt * D + (h + 1) * 128],
                    sb_ctx[:, tt * D + h * 128: tt * D + (h + 1) * 128],
                    sb_om[:, :], ps_gcbl[0:128, 0:128],
                    op0=ALU.mult, op1=ALU.add))
            p.sec("dve", body)

    for tt in range(8):
        src_ready = p.snap()
        emit_ln_residual(p, scr, sb_ctx[:, tt * D:(tt + 1) * D], x, xo, tt, src_ready)
    p.pop()



def build_fused(sim_no_cc=False, upto=8):
    p = P("mega")
    nc = p.nc
    io = {"simcc": sim_no_cc}

    def ei(n, s, d=F32):
        return nc.dram_tensor(n, s, d, kind="ExternalInput")

    io["x"] = ei("x", [TC, D])
    io["wqkv0"] = ei("wqkv0", [3 * D, D])
    io["qw"] = ei("qw", [128, DC])
    io["kw"] = ei("kw", [128, DC])
    io["cosr"] = ei("cosr", [128, TC])
    io["sinr"] = ei("sinr", [128, TC])
    io["amask"] = ei("amask", [128, 8 * 1152], BF16)
    io["wo"] = ei("wo", [D, D])
    io["up0s"] = ei("up0s", [1024, 2048])
    io["dn0s"] = ei("dn0s", [256, 8192])
    io["w1s"] = ei("w1s", [1024, 2048])
    io["w2s"] = ei("w2s", [256, 8192])
    io["upsc"] = ei("upsc", [128, 64])
    io["dsc"] = ei("dsc", [128, 16])
    io["osc"] = ei("osc", [1, 1])
    io["wqkv1"] = ei("wqkv1", [3 * D, D])
    io["bias1"] = ei("bias1", [128, 48])
    io["lmask"] = ei("lmask", [128, 8 * 256], BF16)
    io["b1"] = ei("b1", [128, 64])
    io["b2"] = ei("b2", [128, 16])
    io["out"] = nc.dram_tensor("out", [TC, D], F32, kind="ExternalOutput")

    io["x1"] = p.dram("x1s", [TC, D])
    io["x2"] = p.dram("x2s", [TC, D])
    io["x3"] = p.dram("x3s", [TC, D])
    io["arth_in"] = p.dram("arth_in", [1, 2])
    io["arth_out"] = p.dram("arth_out", [1, 2])
    for nm, shp in [("agu0", ([1024, 2048], [8 * 1024, 2048])),
                    ("agd0", ([256, 8192], [8 * 256, 8192])),
                    ("agw1", ([1024, 2048], [8 * 1024, 2048])),
                    ("agw2", ([256, 8192], [8 * 256, 8192]))]:
        io[nm + "_in"] = p.dram(nm + "_in", shp[0], BF16)
        io[nm + "_out"] = p.dram(nm + "_out", shp[1], BF16)
    io["agkn_in"] = p.dram("agkn_in", [128, NH * TC], BF16)
    io["agkn_out"] = p.dram("agkn_out", [4 * 128, NH * TC], BF16)
    io["agvt_in"] = p.dram("agvt_in", [TC, D], BF16)
    io["agvt_out"] = p.dram("agvt_out", [4 * TC, D], BF16)
    io["agkf1_in"] = p.dram("agkf1_in", [128, NH * TC], BF16)
    io["agkf1_out"] = p.dram("agkf1_out", [4 * 128, NH * TC], BF16)
    io["agvt1_in"] = p.dram("agvt1_in", [TC, D], BF16)
    io["agvt1_out"] = p.dram("agvt1_out", [4 * TC, D], BF16)
    io["agnrm_in"] = p.dram("agnrm_in", [1, NH * TC])
    io["agnrm_out"] = p.dram("agnrm_out", [4, NH * TC])
    io["arqg_in"] = p.dram("arqg_in", [64, 2048])
    io["arqg_out"] = p.dram("arqg_out", [64, 2048])
    io["slotscr"] = p.dram("slotscr", [16, S])
    io["stg_k0"] = p.dram("stg_k0", [128, NH * TC], BF16)
    io["stg_v0"] = p.dram("stg_v0", [TC, D], BF16)
    io["stg_k1l"] = p.dram("stg_k1l", [128, NH * TC], BF16)
    io["stg_k1r"] = p.dram("stg_k1r", [128, NH * TC], BF16)
    io["stg_v1l"] = p.dram("stg_v1l", [64, D], BF16)
    io["stg_v1r"] = p.dram("stg_v1r", [64, D], BF16)

    idn = p.const(IDN_BF, "c_idn")
    idnf = p.const(IDN_F32, "c_idnf")
    iota = p.const(IOTA64, "c_iota")

    p.start()

    io["sb_idn"] = p.sb("g_idn", [128, 128], BF16)
    io["sb_idnf"] = p.sb("g_idnf", [128, 128], F32)
    io["sb_iota"] = p.sb("g_iota", [128, 64], BF16)
    io["sb_onerow"] = p.sb("g_onerow", [1, 128], F32)
    io["sb_oneb"] = p.sb("g_oneb", [128, 1], BF16)
    io["sb_onef"] = p.sb("g_onef", [128, 1], F32)

    p.sem("dxa0"); p.sem("dxa1"); p.sem("dst")

    def body(sy):
        p.dma("ds", sy.dma_start(out=io["sb_idn"][:, :], in_=idn.ap()))
        p.dma("ds", sy.dma_start(out=io["sb_idnf"][:, :], in_=idnf.ap()))
        p.dma("ds", sy.dma_start(out=io["sb_iota"][:, :], in_=iota.ap()))
    p.sec("sp", body)
    io["boot_ds"] = p.c["ds"]

    def body(g):
        io["dy"] = p.dyn(static=sim_no_cc)
        g.memset(io["sb_onerow"][:, :], 1.0)
        g.memset(io["sb_oneb"][:, :], 1.0)
        p.inc("pool", g.memset(io["sb_onef"][:, :], 1.0))
    p.sec("pool", body)

    ag_emitters = []
    if upto >= 1:
        ag_emitters = emit_prep(p, io)

    if upto >= 2:
        p.push()
        io["sb_qn"] = p.sb("g_qn", [128, NH * TC], BF16)
        emit_l0a(p, io)
        for em in ag_emitters[:2]:
            em()
        if upto >= 3:
            emit_l0b(p, io)
        p.pop()

    if upto >= 4:
        emit_ffn(p, io, 0)

    if upto >= 5:
        p.push()
        io["sb_qf"] = p.sb("g_qf", [128, NH * TC], BF16)
        io["sb_qt"] = p.sb("g_qt", [128, 8 * D], BF16)
        io["sb_slotm"] = p.sb("g_slotm", [16, TC], F32)
        emit_l1a(p, io)
        for em in ag_emitters[2:]:
            em()
        if upto >= 6:
            emit_l1mid(p, io)
        if upto >= 7:
            emit_l1b(p, io)
        p.pop()

    if upto >= 8:
        emit_ffn(p, io, 1)

    def body(sy):
        for k in list(p.S.keys()):
            if k in ("pe", "dve", "act", "pool"):
                continue
            p.w(sy, k)
    p.sec("sp", body)
    return p



def _rope_tables():
    inv_freq = 1.0 / (10000.0 ** (np.arange(0, 128, 2, dtype=np.float32) / 128.0))
    pos = np.arange(S, dtype=np.float32)
    fr = pos[:, None] * inv_freq[None, :]
    return np.cos(fr).astype(np.float32), np.sin(fr).astype(np.float32)


def _l0_mask(c):
    s0 = (c % 4) * TC
    m = np.zeros((8, 128, 1152), np.float32)
    for qt in range(8):
        qpos = s0 + qt * 128 + np.arange(128)
        kpos = s0 - 1024 + qt * 128 + np.arange(1152)
        valid = (kpos[None, :] <= qpos[:, None]) & (qpos[:, None] - kpos[None, :] <= 1023) \
                & (kpos[None, :] >= 0)
        m[qt] = np.where(valid, 0.0, NEG)
    return np.ascontiguousarray(m.transpose(1, 0, 2).reshape(128, 8 * 1152).astype(BF))


def _l1_mask(c):
    s0 = (c % 4) * TC
    lo = s0 - 64
    m = np.zeros((8, 128, 256), np.float32)
    for qt in range(8):
        kpos = lo + qt * 128 + np.arange(256)
        bad = (kpos < 0) | (kpos >= S)
        m[qt, :, :] = np.where(bad[None, :], NEG, 0.0)
    return np.ascontiguousarray(m.transpose(1, 0, 2).reshape(128, 8 * 256).astype(BF))


def _qkv_arr(w):
    return np.ascontiguousarray(
        w.reshape(16, 128, 48, 128).transpose(2, 1, 0, 3).reshape(6144, 2048))


def kernel(**inputs):
    x_full = np.asarray(inputs["L0_attn_hidden_states"], dtype=np.float32)
    wqkv0 = np.ascontiguousarray(inputs["L0_attn_Wqkv"], dtype=np.float32)
    qw = np.asarray(inputs["L0_attn_q_norm_w"], dtype=np.float32)
    kw = np.asarray(inputs["L0_attn_k_norm_w"], dtype=np.float32)
    wo = np.ascontiguousarray(inputs["L0_attn_Wo"], dtype=np.float32)
    up0 = np.asarray(inputs["L0_ffn_up_weight"], dtype=np.float32)
    dn0 = np.asarray(inputs["L0_ffn_down_weight"], dtype=np.float32)
    upsc = np.asarray(inputs["L0_ffn_up_scale"], dtype=np.float32)
    dsc = np.asarray(inputs["L0_ffn_down_scale"], dtype=np.float32)
    osc = np.asarray(inputs["L0_ffn_output_scale"], dtype=np.float32)
    wq1 = np.asarray(inputs["L1_attn_Wq"], dtype=np.float32)
    wk1 = np.asarray(inputs["L1_attn_Wk"], dtype=np.float32)
    wv1 = np.asarray(inputs["L1_attn_Wv"], dtype=np.float32)
    bq1 = np.asarray(inputs["L1_attn_bq"], dtype=np.float32)
    bk1 = np.asarray(inputs["L1_attn_bk"], dtype=np.float32)
    bv1 = np.asarray(inputs["L1_attn_bv"], dtype=np.float32)
    w1 = np.asarray(inputs["L1_ffn_w1"], dtype=np.float32)
    b1 = np.asarray(inputs["L1_ffn_b1"], dtype=np.float32)
    w2 = np.asarray(inputs["L1_ffn_w2"], dtype=np.float32)
    b2 = np.asarray(inputs["L1_ffn_b2"], dtype=np.float32)

    cosf, sinf = _rope_tables()
    wqkv0a = _qkv_arr(wqkv0)
    wqkv1a = _qkv_arr(np.concatenate([wq1, wk1, wv1], axis=1))
    bias1 = np.ascontiguousarray(np.concatenate([bq1, bk1, bv1]).reshape(48, 128).T)

    p = build_fused()
    nc = p.finish()
    in_maps = []
    for c in range(NCORES):
        b, s0 = c // 4, (c % 4) * TC
        in_maps.append({
            "x": np.ascontiguousarray(x_full[b, s0:s0 + TC, :]),
            "wqkv0": wqkv0a,
            "qw": np.ascontiguousarray(qw.reshape(16, 128).T),
            "kw": np.ascontiguousarray(kw.reshape(16, 128).T),
            "cosr": np.ascontiguousarray(np.concatenate([cosf[s0:s0 + TC, :].T] * 2, 0)),
            "sinr": np.ascontiguousarray(np.concatenate([sinf[s0:s0 + TC, :].T] * 2, 0)),
            "amask": _l0_mask(c),
            "wo": wo,
            "up0s": np.ascontiguousarray(up0[c * 1024:(c + 1) * 1024, :]),
            "dn0s": np.ascontiguousarray(dn0[c * 256:(c + 1) * 256, :]),
            "w1s": np.ascontiguousarray(w1[c * 1024:(c + 1) * 1024, :]),
            "w2s": np.ascontiguousarray(w2[c * 256:(c + 1) * 256, :]),
            "upsc": np.ascontiguousarray(upsc.reshape(64, 128).T),
            "dsc": np.ascontiguousarray(dsc.reshape(16, 128).T),
            "osc": np.ascontiguousarray(osc.reshape(1, 1)),
            "wqkv1": wqkv1a,
            "bias1": bias1,
            "lmask": _l1_mask(c),
            "b1": np.ascontiguousarray(b1.reshape(64, 128).T),
            "b2": np.ascontiguousarray(b2.reshape(16, 128).T),
        })
    res = run_bass_kernel_spmd(nc, in_maps, core_ids=list(range(NCORES))).results
    out = np.concatenate([np.asarray(r["out"]) for r in res], axis=0)
    return out.reshape(B, S, D)
